# version 12
# baseline (speedup 1.0000x reference)
import contextlib
import math
import numpy as np
import ml_dtypes
import concourse.bass as bass
import concourse.mybir as mybir

F32 = mybir.dt.float32
BF16 = mybir.dt.bfloat16
AF = mybir.ActivationFunctionType
ALU = mybir.AluOpType
AX = mybir.AxisListType

NCORES = 8
S = 8192
D = 2048
A = S // 4
HD = 128
FF = 5632
FSH = FF // 8
W3SH = D // 8
TB = 1024
NBLK = TB // 128
NEG = float(np.float32(-1.0e38))
MAGIC = float(np.float32(1.5 * 2 ** 23))
SC_ATTN = float(np.float32(1.0 / math.sqrt(HD)))
EPS_LN = 1e-5
EPS_RMS = 1e-6
EPS_Q = 1e-5
INV_D = float(np.float32(1.0 / D))
INV_FF = float(np.float32(1.0 / FF))
INV_WN = float(np.float32(1.0 / (FF * D)))
INV_SD = float(np.float32(1.0 / (S * D)))
INV_127 = float(np.float32(1.0 / 127.0))
CK = 128
J = A // CK
P3_SKIP = set()


def _masks_natural(i, j):
    di = i - j
    loc = (np.abs(di) <= 64) | (i == 0) | (j == 0)
    st = (di % 4 == 0) | (i == 0) | (j == 0)
    return loc, st


def _a_tile(rho_k, kb):
    b = kb + np.arange(CK)
    loc_t = np.full((CK, 4 * CK), NEG, np.float32)
    str_t = np.full((CK, 4 * CK), NEG, np.float32)
    for r in range(4):
        a = kb + np.arange(CK)
        i = 4 * a[None, :] + r
        jn = 4 * b[:, None] + rho_k
        band = np.abs(i - jn) <= 64
        owned = band if kb % CK == 0 else band & ((a[None, :] // CK) != (b[:, None] // CK))
        owned = owned & (jn != 0)
        loc_m, str_m = _masks_natural(i, jn)
        loc_t[:, r * CK:(r + 1) * CK] = np.where(owned & loc_m, 0.0, NEG)
        str_t[:, r * CK:(r + 1) * CK] = np.where(owned & str_m, 0.0, NEG)
    return loc_t, str_t


def build_attn_plan():
    kbs = [CK * j for j in range(J)] + [CK * j + 64 for j in range(J - 1)]
    pieces, tiles, tids = [], [], {}
    for rho_k in range(4):
        for kb in kbs:
            lt, st = _a_tile(rho_k, kb)
            key = lt.tobytes() + st.tobytes()
            vid = tids.get(key)
            if vid is None:
                vid = len(tiles)
                tids[key] = vid
                tiles.append((lt, st))
            pieces.append((rho_k, kb, vid))
    strips, stiles, stids = [], [], {}
    for rho in range(4):
        for j in range(J):
            a_lo = max(0, CK * j - 16)
            a_hi = min(A, CK * j + CK + 16)
            t = np.zeros((CK, a_hi - a_lo), np.float32)
            bb = CK * j + np.arange(CK)
            aa = a_lo + np.arange(a_hi - a_lo)
            t[np.abs(aa[None, :] - bb[:, None]) <= 16] = NEG
            key = (t.shape[1], t.tobytes())
            svid = stids.get(key)
            if svid is None:
                svid = len(stiles)
                stids[key] = svid
                stiles.append(t)
            strips.append((rho, j, a_lo, a_hi, svid))
    return pieces, tiles, strips, stiles


class Sem:
    def __init__(self, handle):
        self.h = handle
        self.v = 0

    def inc(self, instr, n=1):
        instr.then_inc(self.h, n)
        self.v += n
        return self.v

    def dma(self, instr):
        return self.inc(instr, 16)

    def wait(self, eng, v=None):
        eng.wait_ge(self.h, self.v if v is None else v)


def build_nc(run_to="p1", debug=True):
    LVL = {"p1": 1, "p2": 2, "p3": 3, "p4": 4, "p5": 5, "p6": 6, "p7": 7}[run_to]
    nc = bass.Bass("TRN2", target_bir_lowering=False)
    pieces, atiles, strips, stiles = build_attn_plan()
    NT = len(atiles)
    stile_w = [t.shape[1] for t in stiles]
    SW = max(stile_w)
    NS = len(stiles)

    x_sl = nc.dram_tensor("x_sl", [TB, D], F32, kind="ExternalInput")
    wq_sl = nc.dram_tensor("wq_sl", [256, D], F32, kind="ExternalInput")
    wk_sl = nc.dram_tensor("wk_sl", [256, D], F32, kind="ExternalInput")
    wv_sl = nc.dram_tensor("wv_sl", [256, D], F32, kind="ExternalInput")
    wo_sl = nc.dram_tensor("wo_sl", [256, D], F32, kind="ExternalInput")
    bq_sl = nc.dram_tensor("bq_sl", [256, 1], F32, kind="ExternalInput")
    bk_sl = nc.dram_tensor("bk_sl", [256, 1], F32, kind="ExternalInput")
    bv_sl = nc.dram_tensor("bv_sl", [1, 256], F32, kind="ExternalInput")
    bo_in = nc.dram_tensor("bo_in", [1, D], F32, kind="ExternalInput")
    w1_sh = nc.dram_tensor("w1_sh", [256, FF], F32, kind="ExternalInput")
    w2_sh = nc.dram_tensor("w2_sh", [256, FF], F32, kind="ExternalInput")
    w3_sh = nc.dram_tensor("w3_sh", [FSH, D], F32, kind="ExternalInput")
    atl_loc = nc.dram_tensor("atl_loc", [NT * 128, 512], BF16, kind="ExternalInput")
    atl_str = nc.dram_tensor("atl_str", [NT * 128, 512], BF16, kind="ExternalInput")
    strip_in = nc.dram_tensor("strip_in", [NS * 128, SW], BF16, kind="ExternalInput")
    identb_in = nc.dram_tensor("identb_in", [128, 128], BF16, kind="ExternalInput")
    identf_in = nc.dram_tensor("identf_in", [128, 128], F32, kind="ExternalInput")

    out_d = nc.dram_tensor("out", [TB, D], F32, kind="ExternalOutput")

    xn_ag_in = nc.dram_tensor("xn_ag_in", [D, TB], BF16)
    xn_full = nc.dram_tensor("xn_full", [8 * D, TB], BF16)
    stats_in = nc.dram_tensor("stats_in", [1, 8], F32)
    stats_out = nc.dram_tensor("stats_out", [1, 8], F32)
    wq_bf = nc.dram_tensor("wq_bf", [256, D], BF16)
    wk_bf = nc.dram_tensor("wk_bf", [256, D], BF16)
    wv_bf = nc.dram_tensor("wv_bf", [256, D], BF16)
    wo_bf = nc.dram_tensor("wo_bf", [256, D], BF16)
    w1q_in = nc.dram_tensor("w1q_in", [256, FF], BF16)
    w2q_in = nc.dram_tensor("w2q_in", [256, FF], BF16)
    w3q_in = nc.dram_tensor("w3q_in", [FSH, D], BF16)
    w1q = nc.dram_tensor("w1q", [D, FF], BF16)
    w2q = nc.dram_tensor("w2q", [D, FF], BF16)
    w3q = nc.dram_tensor("w3q", [FF, D], BF16)
    xq1_d = nc.dram_tensor("xq1_d", [D, TB], BF16)
    gu_d = nc.dram_tensor("gu_d", [TB, FF], BF16)
    xq3_d = nc.dram_tensor("xq3_d", [TB, FF], BF16)
    v_raw = nc.dram_tensor("v_raw", [128, 2 * S], BF16)
    aop_d = nc.dram_tensor("aop_d", [S, D], F32)
    aor_d = nc.dram_tensor("aor_d", [TB, D], F32)

    dbg = {}
    if debug:
        dbg["gu"] = nc.dram_tensor("dbg_gu", [TB, FF], BF16, kind="ExternalOutput")
        for name, shape in [("xn", [TB, D]), ("xq1", [TB, D]),
                            ("stats", [1, 8]), ("gsc", [1, 8]),
                            ("qt", [128, 2 * S]), ("kt", [128, 2 * S]),
                            ("vsb", [128, 2 * S]), ("nrm", [1, 8]), ("mhat", [1, 4]),
                            ("ebA", [128, 2]), ("ebB", [128, 2]),
                            ("aoT", [128, 2 * S]), ("l", [1, 2 * S]),
                            ("rcol", [128, 128]), ("q3", [128, 4 * NBLK]),
                            ("aor", [TB, D])]:
            dbg[name] = nc.dram_tensor("dbg_" + name, shape, F32, kind="ExternalOutput")

    st = contextlib.ExitStack()
    with st:
        pb = [st.enter_context(nc.psum_tensor(f"pb{i}", [128, 512], F32)) for i in range(8)]
        ones_bf = st.enter_context(nc.sbuf_tensor("ones_bf", [128, 1], BF16))
        onesw = st.enter_context(nc.sbuf_tensor("onesw", [128, 16], BF16))
        mgb = st.enter_context(nc.sbuf_tensor("mgb", [128, 1], F32))
        mgn = st.enter_context(nc.sbuf_tensor("mgn", [128, 1], F32))
        onesr_f = st.enter_context(nc.sbuf_tensor("onesr_f", [1, 128], F32))
        one_f = st.enter_context(nc.sbuf_tensor("one_f", [1, 1], F32))
        one_bf = st.enter_context(nc.sbuf_tensor("one_bf", [1, 1], BF16))
        identf = st.enter_context(nc.sbuf_tensor("identf", [128, 128], F32))
        identb = st.enter_context(nc.sbuf_tensor("identb", [128, 128], BF16))
        q1v = st.enter_context(nc.sbuf_tensor("q1v", [128, NBLK], F32))
        os1v = st.enter_context(nc.sbuf_tensor("os1v", [128, NBLK], F32))
        stats_sb = st.enter_context(nc.sbuf_tensor("stats_sb", [1, 8], F32))
        cntacc = st.enter_context(nc.sbuf_tensor("cntacc", [128, 4], F32))
        swb = st.enter_context(nc.sbuf_tensor("swb", [128, 5], F32))
        gsc = st.enter_context(nc.sbuf_tensor("gsc", [1, 8], F32))

        sems = {}
        for name in ["ld", "ld2", "ld3", "ld4", "stm", "cc", "sv", "ssc", "ste", "sg",
                     "lx0", "lx1", "lw0", "lw1", "ln0", "ln1", "ln2", "ln3", "ln4", "ln5", "sld", "gld", "sx0", "sx1", "sq0", "sq1", "gq0", "gq1", "gpc", "lid", "lat"]:
            sems[name] = Sem(st.enter_context(nc.semaphore(name)))
        ld, ld2, ld3, stm, cc, sv, ssc, ste, sg = (sems[k] for k in
            ["ld", "ld2", "ld3", "stm", "cc", "sv", "ssc", "ste", "sg"])
        lx = [sems["lx0"], sems["lx1"]]
        lw = [sems["lw0"], sems["lw1"]]
        lnt = [sems["ln0"], sems["ln1"], sems["ln2"]]
        lnt6 = [sems[f"ln{i}"] for i in range(6)]
        sld = sems["sld"]
        gld = sems["gld"]
        stx = [sems["sx0"], sems["sx1"]]
        swq = [sems["sq0"], sems["sq1"]]
        gwq = [sems["gq0"], sems["gq1"]]
        gpc = sems["gpc"]
        lid = sems["lid"]
        lat = sems["lat"]
        ldf = [sems["ld3"], sems["ld4"]]

        block = st.enter_context(nc.Block())

        @block.vector
        def _(v):
            v.memset(ones_bf[:, :], 1.0)
            v.memset(onesw[:, :], 1.0)
            v.memset(mgb[:, :], MAGIC)
            v.memset(mgn[:, :], -MAGIC)
            v.memset(onesr_f[:, :], 1.0)
            v.memset(one_f[:, :], 1.0)
            v.memset(one_bf[:, :], 1.0)
            sv.inc(v.memset(cntacc[:, :], 0.0))
        V_CONST = sv.v

        @block.sync
        def _(s):
            lid.dma(s.dma_start(out=identf[:, :], in_=identf_in[:, :]))
            lid.dma(s.dma_start(out=identb[:, :], in_=identb_in[:, :]))

        @block.gpsimd
        def _(g):
            gld.dma(g.dma_start(out=wq_bf[:, :], in_=wq_sl[:, :]))
            gld.dma(g.dma_start(out=wk_bf[:, :], in_=wk_sl[:, :]))
            gld.dma(g.dma_start(out=wv_bf[:, :], in_=wv_sl[:, :]))
            gld.dma(g.dma_start(out=wo_bf[:, :], in_=wo_sl[:, :]))
        LD_WCAST = gld.v

        stp1 = contextlib.ExitStack()
        xnT_sb = stp1.enter_context(nc.sbuf_tensor("xnT_sb", [128, 16 * TB], BF16))
        xq1T_sb = stp1.enter_context(nc.sbuf_tensor("xq1T_sb", [128, 16 * TB], BF16))
        xblk = [stp1.enter_context(nc.sbuf_tensor(f"xblk{i}", [128, D], F32)) for i in range(2)]
        xnblk = [stp1.enter_context(nc.sbuf_tensor(f"xnblk{i}", [128, D], BF16)) for i in range(2)]
        sqblk = stp1.enter_context(nc.sbuf_tensor("sqblk", [128, D], BF16))
        gtblk = stp1.enter_context(nc.sbuf_tensor("gtblk", [128, D], BF16))
        rndblk = stp1.enter_context(nc.sbuf_tensor("rndblk", [128, D], F32))
        xq1blk = [stp1.enter_context(nc.sbuf_tensor(f"xq1blk{i}", [128, D], BF16)) for i in range(2)]
        s_sum = stp1.enter_context(nc.sbuf_tensor("s_sum", [128, 1], F32))
        s_sq = stp1.enter_context(nc.sbuf_tensor("s_sq", [128, 1], F32))
        s_mean = stp1.enter_context(nc.sbuf_tensor("s_mean", [128, 1], F32))
        s_var = stp1.enter_context(nc.sbuf_tensor("s_var", [128, 1], F32))
        s_rstd = stp1.enter_context(nc.sbuf_tensor("s_rstd", [128, 1], F32))
        s_nmr = stp1.enter_context(nc.sbuf_tensor("s_nmr", [128, 1], F32))
        s_t0 = stp1.enter_context(nc.sbuf_tensor("s_t0", [128, 1], F32))
        s_t1 = stp1.enter_context(nc.sbuf_tensor("s_t1", [128, 1], F32))
        s_r1 = stp1.enter_context(nc.sbuf_tensor("s_r1", [128, 1], F32))
        s_m1 = stp1.enter_context(nc.sbuf_tensor("s_m1", [128, 1], F32))
        s_s1 = stp1.enter_context(nc.sbuf_tensor("s_s1", [128, 1], F32))
        gt2 = stp1.enter_context(nc.sbuf_tensor("gt2", [128, D], BF16))
        sqblk2 = stp1.enter_context(nc.sbuf_tensor("sqblk2", [128, D], BF16))
        cnt2048 = stp1.enter_context(nc.sbuf_tensor("cnt2048", [128, D], F32))

        xload_done = [0] * NBLK
        xn_store_done = [0] * NBLK
        T_XNT_DONE = [0] * NBLK

        def load_xblk(b):
            @block.sync
            def _(s):
                if b >= 2:
                    ssc.wait(s, SC_XN_DONE[b - 2])
                xload_done[b] = lx[b % 2].dma(
                    s.dma_start(out=xblk[b % 2][:, :], in_=x_sl[128 * b:128 * (b + 1), :]))

        SC_XN_DONE = [0] * NBLK
        T_EVS = []
        T_EVV = []
        load_xblk(0)
        load_xblk(1)

        for b in range(NBLK):
            xb = xblk[b % 2]
            xnb = xnblk[b % 2]
            xqb = xq1blk[b % 2]

            @block.scalar
            def _(sc, b=b, xb=xb):
                sc.wait_ge(lx[b % 2].h, xload_done[b])
                sc.activation(out=sqblk[:, :], in_=xb[:, :], func=AF.Square,
                              accum_out=s_sq[:, :])
                ssc.inc(sc.activation(out=sqblk2[:, :], in_=xb[:, :], func=AF.Identity,
                                      bias=0.0, scale=1.0, accum_out=s_sum[:, :]))

            @block.vector
            def _(v, b=b, xb=xb):
                ssc.wait(v)
                v.tensor_scalar(out=s_mean[:, :], in0=s_sum[:, :], scalar1=INV_D,
                                scalar2=None, op0=ALU.mult)
                v.tensor_scalar(out=s_var[:, :], in0=s_sq[:, :], scalar1=INV_D,
                                scalar2=None, op0=ALU.mult)
                v.drain()
                v.tensor_tensor(out=s_t0[:, :], in0=s_mean[:, :], in1=s_mean[:, :], op=ALU.mult)
                v.drain()
                v.tensor_tensor(out=s_var[:, :], in0=s_var[:, :], in1=s_t0[:, :], op=ALU.subtract)
                v.drain()
                v.tensor_scalar(out=s_t0[:, :], in0=s_var[:, :], scalar1=EPS_LN,
                                scalar2=None, op0=ALU.add)
                v.drain()
                sv.inc(v.reciprocal(out=s_t1[:, :], in_=s_t0[:, :]))

            @block.scalar
            def _(sc, b=b):
                sv.wait(sc)
                ssc.inc(sc.sqrt(out=s_rstd[:, :], in_=s_t1[:, :]))

            @block.vector
            def _(v, b=b):
                ssc.wait(v)
                v.tensor_tensor(out=s_nmr[:, :], in0=s_mean[:, :], in1=s_rstd[:, :], op=ALU.mult)
                v.drain()
                sv.inc(v.tensor_scalar(out=s_nmr[:, :], in0=s_nmr[:, :], scalar1=-1.0,
                                       scalar2=None, op0=ALU.mult))

            @block.scalar
            def _(sc, b=b, xb=xb, xnb=xnb):
                sv.wait(sc)
                if b >= 2:
                    stx[b % 2].wait(sc, xn_store_done[b - 2])
                    ste.wait(sc, T_XNT_DONE[b - 2])
                sc.activation(out=xnb[:, :], in_=xb[:, :], func=AF.Identity,
                              bias=s_nmr[:, :], scale=s_rstd[:, :])
                ssc.inc(sc.activation(out=sqblk[:, :], in_=xb[:, :], func=AF.Square,
                                      bias=s_nmr[:, :], scale=s_rstd[:, :],
                                      accum_out=s_sq[:, :]))
                SC_XN_DONE[b] = ssc.v

            @block.gpsimd
            def _(g, b=b, xnb=xnb):
                ssc.wait(g)
                if b == 0:
                    g.memset(cnt2048[:, :], 0.0)
                else:
                    gpc.wait(g)
                g.tensor_tensor(out=gt2[:, :], in0=xnb[:, :], in1=xnb[:, :], op=ALU.mult)
                g.drain()
                g.tensor_scalar(out=gtblk[:, :], in0=gt2[:, :], scalar1=0.01,
                                scalar2=None, op0=ALU.is_gt)
                g.drain()
                gpc.inc(g.tensor_tensor(out=cnt2048[:, :], in0=cnt2048[:, :],
                                        in1=gtblk[:, :], op=ALU.add))

            @block.vector
            def _(v, b=b):
                ssc.wait(v)
                v.tensor_scalar(out=s_t0[:, :], in0=s_sq[:, :], scalar1=INV_D,
                                scalar2=EPS_RMS, op0=ALU.mult, op1=ALU.add)
                v.drain()
                sv.inc(v.reciprocal(out=s_t1[:, :], in_=s_t0[:, :]))

            @block.scalar
            def _(sc, b=b):
                sv.wait(sc)
                ssc.inc(sc.sqrt(out=s_r1[:, :], in_=s_t1[:, :]))

            @block.vector
            def _(v, b=b, xnb=xnb, xqb=xqb):
                ssc.wait(v)
                if b >= 2:
                    stx[b % 2].wait(v, xn_store_done[b - 2])
                    ste.wait(v, T_XNT_DONE[b - 2])
                v.reduce_max(out=s_t0[:, :], in_=xnb[:, :], axis=AX.X,
                             apply_absolute_value=True)
                v.drain()
                v.tensor_tensor(out=s_m1[:, :], in0=s_r1[:, :], in1=s_t0[:, :], op=ALU.mult)
                v.drain()
                v.tensor_scalar(out=s_m1[:, :], in0=s_m1[:, :], scalar1=EPS_Q,
                                scalar2=None, op0=ALU.max)
                v.drain()
                v.reciprocal(out=s_s1[:, :], in_=s_m1[:, :])
                v.drain()
                v.tensor_scalar(out=s_s1[:, :], in0=s_s1[:, :], scalar1=127.0,
                                scalar2=None, op0=ALU.mult)
                v.drain()
                v.tensor_tensor(out=q1v[:, b:b + 1], in0=s_r1[:, :], in1=s_s1[:, :], op=ALU.mult)
                sv.inc(v.tensor_scalar(out=os1v[:, b:b + 1], in0=s_m1[:, :], scalar1=INV_127,
                                       scalar2=None, op0=ALU.mult))
            V_Q1V = sv.v

            @block.scalar
            def _(sc, b=b, xnb=xnb, xqb=xqb, V_Q1V=V_Q1V):
                sv.wait(sc, V_Q1V)
                sc.activation(out=rndblk[:, :], in_=xnb[:, :], func=AF.Identity,
                              bias=mgb[:, :], scale=q1v[:, b:b + 1])
                sc.drain()
                ssc.inc(sc.activation(out=xqb[:, :], in_=rndblk[:, :], func=AF.Identity,
                                      bias=mgn[:, :], scale=1.0))
            SC_XQ1 = ssc.v

            @block.gpsimd
            def _(g, b=b, xnb=xnb, xqb=xqb, SC_XQ1=SC_XQ1):
                ssc.wait(g, SC_XQ1)
                if debug:
                    stx[b % 2].dma(g.dma_start(out=dbg["xn"][128 * b:128 * (b + 1), :], in_=xnb[:, :]))
                    stx[b % 2].dma(g.dma_start(out=dbg["xq1"][128 * b:128 * (b + 1), :], in_=xqb[:, :]))
                xn_store_done[b] = stx[b % 2].v

            for dch in range(16):
                ti6 = 16 * b + dch
                tb_ = pb[4 + 2 * (dch % 2)]
                tb2_ = pb[5 + 2 * (dch % 2)]

                @block.tensor
                def _(te, b=b, dch=dch, ti6=ti6, xnb=xnb, xqb=xqb, tb_=tb_, tb2_=tb2_):
                    if b == 0 and dch == 0:
                        lid.wait(te)
                    ssc.wait(te)
                    if ti6 >= 2:
                        ssc.wait(te, T_EVS[ti6 - 2])
                        sv.wait(te, T_EVV[ti6 - 2])
                    te.matmul(tb_[:, 0:128], xnb[:, 128 * dch:128 * (dch + 1)], identb[:, :],
                              start=True, stop=True)
                    ste.inc(te.matmul(tb2_[:, 128:256], xqb[:, 128 * dch:128 * (dch + 1)],
                                      identb[:, :], start=True, stop=True))
                T_MM = ste.v

                @block.scalar
                def _(sc, b=b, dch=dch, tb_=tb_, T_MM=T_MM):
                    ste.wait(sc, T_MM)
                    co = 1024 * dch + 128 * b
                    ssc.inc(sc.copy(out=xnT_sb[:, co:co + 128], in_=tb_[:, 0:128]))
                T_EVS.append(ssc.v)

                @block.vector
                def _(v, b=b, dch=dch, tb2_=tb2_, T_MM=T_MM):
                    ste.wait(v, T_MM)
                    co = 1024 * dch + 128 * b
                    sv.inc(v.tensor_copy(out=xq1T_sb[:, co:co + 128], in_=tb2_[:, 128:256]))
                T_EVV.append(sv.v)
            T_XNT_DONE[b] = ste.v

            if b + 2 < NBLK:
                load_xblk(b + 2)


        @block.gpsimd
        def _(g):
            sv.wait(g)
            ssc.wait(g)
            stm.dma(g.dma_start(
                out=xn_ag_in.ap().rearrange("(c p) t -> p c t", p=128), in_=xnT_sb[:, :]))
            stm.dma(g.dma_start(
                out=xq1_d.ap().rearrange("(c p) t -> p c t", p=128), in_=xq1T_sb[:, :]))
        STM_XNT = stm.v

        @block.gpsimd
        def _(g):
            g.wait_ge(stm.h, STM_XNT)
            cc.inc(g.collective_compute(
                "AllGather", ALU.bypass, replica_groups=[list(range(NCORES))],
                ins=[xn_ag_in.ap().opt()], outs=[xn_full.ap().opt()]))
        CC_XN = cc.v

        @block.sync
        def _(s2):
            sv.wait(s2)
            ssc.wait(s2)
            ste.wait(s2)
            stm.wait(s2)
            stx[0].wait(s2)
            stx[1].wait(s2)
            sg.inc(s2.nop())
        SG_LN_DONE = sg.v
        stp1.close()

        stp1b = contextlib.ExitStack()
        wch = [stp1b.enter_context(nc.sbuf_tensor(f"wch{i}", [128, 5632], F32)) for i in range(2)]
        s_t0b = stp1b.enter_context(nc.sbuf_tensor("s_t0b", [128, 1], F32))

        wabs_chunks = []
        for rb in range(2):
            wabs_chunks.append((w1_sh, 128 * rb, 128, 0, FF, 1))
        for rb in range(2):
            wabs_chunks.append((w2_sh, 128 * rb, 128, 0, FF, 2))
        for rb in range(6):
            r0 = 128 * rb
            wabs_chunks.append((w3_sh, r0, min(128, FSH - r0), 0, D, 3))
        WCH_LD = [0] * len(wabs_chunks)
        WCH_RD = [0] * len(wabs_chunks)
        for ci, (src, r0, nr, c0, ncol, col) in enumerate(wabs_chunks):
            @block.sync
            def _(s2, ci=ci, src=src, r0=r0, nr=nr, c0=c0, ncol=ncol):
                if ci == 0:
                    sg.wait(s2, SG_LN_DONE)
                if ci >= 2:
                    sv.wait(s2, WCH_RD[ci - 2])
                WCH_LD[ci] = swq[ci % 2].dma(s2.dma_start(
                    out=wch[ci % 2][0:nr, 0:ncol], in_=src[r0:r0 + nr, c0:c0 + ncol]))

            @block.vector
            def _(v, ci=ci, nr=nr, ncol=ncol, col=col):
                v.wait_ge(swq[ci % 2].h, WCH_LD[ci])
                v.drain()
                v.reduce_sum(out=s_t0b[0:nr, :], in_=wch[ci % 2][0:nr, 0:ncol], axis=AX.X,
                             apply_absolute_value=True)
                v.drain()
                WCH_RD[ci] = sv.inc(v.tensor_tensor(out=cntacc[0:nr, col:col + 1],
                                                    in0=cntacc[0:nr, col:col + 1],
                                                    in1=s_t0b[0:nr, :], op=ALU.add))

        @block.vector
        def _(v):
            v.drain()
            sv.inc(v.tensor_copy(out=s_t0b[:, :], in_=cntacc[:, 0:1]))
        V_WABS = sv.v

        @block.vector
        def _(v):
            gpc.wait(v)
            v.drain()
            v.reduce_sum(out=cntacc[:, 0:1], in_=cnt2048[:, :], axis=AX.X)
            sv.inc(v.drain())

        @block.tensor
        def _(te):
            sv.wait(te)
            lid.wait(te)
            for c2 in range(4):
                ste.inc(te.matmul(pb[0][0:1, 128 * c2:128 * (c2 + 1)],
                                  cntacc[:, c2:c2 + 1], identf[:, :], start=True, stop=True))

        @block.vector
        def _(v):
            ste.wait(v)
            for c2 in range(4):
                v.reduce_sum(out=stats_sb[0:1, c2:c2 + 1], in_=pb[0][0:1, 128 * c2:128 * (c2 + 1)],
                             axis=AX.X)
            sv.inc(v.memset(stats_sb[0:1, 4:8], 0.0))
        V_STATS = sv.v

        @block.gpsimd
        def _(g):
            sv.wait(g, V_STATS)
            stm.dma(g.dma_start(out=stats_in[:, :], in_=stats_sb[:, :]))
            stm.wait(g)
            cc.inc(g.collective_compute(
                "AllReduce", ALU.add, replica_groups=[list(range(NCORES))],
                ins=[stats_in.ap().opt()], outs=[stats_out.ap().opt()]))
        CC_STATS = cc.v

        if LVL >= 2:
            build_p2(locals())



        @block.sync
        def _(s2):
            sv.wait(s2)
            ssc.wait(s2)
            stm.wait(s2)
            swq[0].wait(s2)
            swq[1].wait(s2)
            stx[0].wait(s2)
            stx[1].wait(s2)
            sg.inc(s2.nop())
        SG_P1DONE = sg.v
        stp1b.close()

        stp2 = contextlib.ExitStack()
        stp2a = contextlib.ExitStack()
        qt = stp2.enter_context(nc.sbuf_tensor("qt", [128, 2 * S], BF16))
        kt = stp2.enter_context(nc.sbuf_tensor("kt", [128, 2 * S], BF16))
        vsb = stp2.enter_context(nc.sbuf_tensor("vsb", [128, 2 * S], BF16))
        nrm = stp2.enter_context(nc.sbuf_tensor("nrm", [1, 8], F32))
        nrmw = stp2.enter_context(nc.sbuf_tensor("nrmw", [16, 8], F32))
        nrmt = stp2.enter_context(nc.sbuf_tensor("nrmt", [4, 17], F32))
        mhat = stp2.enter_context(nc.sbuf_tensor("mhat", [1, 4], F32))
        ebA = stp2.enter_context(nc.sbuf_tensor("ebA", [128, 2], F32))
        ebB = stp2.enter_context(nc.sbuf_tensor("ebB", [128, 2], F32))
        ebA0 = stp2.enter_context(nc.sbuf_tensor("ebA0", [128, 2], F32))
        ebB0 = stp2.enter_context(nc.sbuf_tensor("ebB0", [128, 2], F32))
        atb = stp2.enter_context(nc.sbuf_tensor("atb", [128, NT * 512], BF16))
        ssb = stp2.enter_context(nc.sbuf_tensor("ssb", [128, NS * SW], BF16))
        qin2 = [stp2.enter_context(nc.sbuf_tensor("qin0", [128, 1024], F32))]
        aoT = stp2.enter_context(nc.sbuf_tensor("aoT", [128, 2 * S], BF16))
        rcol = stp2.enter_context(nc.sbuf_tensor("rcol", [128, 128], F32))
        wqt_all = stp2a.enter_context(nc.sbuf_tensor("wqt_all", [128, 16 * 256], BF16))
        wkt_all = stp2a.enter_context(nc.sbuf_tensor("wkt_all", [128, 16 * 256], BF16))
        wvt_all = stp2a.enter_context(nc.sbuf_tensor("wvt_all", [128, 16 * 256], BF16))
        xnt = [stp2a.enter_context(nc.sbuf_tensor(f"xnt{i}", [128, 512], BF16)) for i in range(6)]
        bqsb = stp2a.enter_context(nc.sbuf_tensor("bqsb", [128, 2], F32))
        bksb = stp2a.enter_context(nc.sbuf_tensor("bksb", [128, 2], F32))
        bvsb = stp2a.enter_context(nc.sbuf_tensor("bvsb", [1, 256], F32))
        bvrep = stp2a.enter_context(nc.sbuf_tensor("bvrep", [128, 256], F32))
        sqsc = [stp2a.enter_context(nc.sbuf_tensor(f"sqsc{i}", [128, 512], BF16)) for i in range(4)]
        atscr = stp2a.enter_context(nc.sbuf_tensor("atscr", [128, 2 * 512], BF16))
        whalf2 = [stp2a.enter_context(nc.sbuf_tensor(f"whalf{i}", [128, 2 * D], BF16)) for i in range(2)]

        WH = {}
        WT_EVAC = {}
        WT_TE = {}
        EVW = []
        for wi, (wsrc, wdst) in enumerate(((wq_bf, wqt_all), (wk_bf, wkt_all), (wv_bf, wvt_all))):
            @block.sync
            def _(s, wi=wi, wsrc=wsrc):
                if wi == 0:
                    sg.wait(s, SG_P1DONE)
                    s.wait_ge(gld.h, LD_WCAST)
                elif wi >= 2:
                    ste.wait(s, WT_TE[wi - 2])
                lw[wi % 2].dma(s.dma_start(out=whalf2[wi % 2][:, 0:D], in_=wsrc[0:128, :]))
                WH[wi] = lw[wi % 2].dma(s.dma_start(out=whalf2[wi % 2][:, D:2 * D],
                                                    in_=wsrc[128:256, :]))

            for h2 in range(2):
                for grp in range(4):
                    it = len(EVW)
                    pbs = 0 if it % 2 == 0 else 4

                    @block.tensor
                    def _(te, wi=wi, h2=h2, grp=grp, it=it, pbs=pbs):
                        te.wait_ge(lw[wi % 2].h, WH[wi])
                        if it < 2:
                            lid.wait(te)
                            sv.wait(te)
                        else:
                            sv.wait(te, EVW[it - 2])
                        last = None
                        for j4 in range(4):
                            dch = 4 * grp + j4
                            last = te.matmul(pb[pbs + j4][:, 0:128],
                                             whalf2[wi % 2][:, D * h2 + 128 * dch:D * h2 + 128 * (dch + 1)],
                                             identb[:, :], start=True, stop=True)
                        ste.inc(last)
                    if h2 == 1 and grp == 3:
                        WT_TE[wi] = ste.v

                    @block.vector
                    def _(v, wi=wi, h2=h2, grp=grp, pbs=pbs, wdst=wdst):
                        ste.wait(v)
                        lastv = None
                        for j4 in range(4):
                            dch = 4 * grp + j4
                            lastv = v.tensor_copy(
                                out=wdst[:, 256 * dch + 128 * h2:256 * dch + 128 * (h2 + 1)],
                                in_=pb[pbs + j4][:, 0:128])
                        sv.inc(lastv)
                    EVW.append(sv.v)
            WT_EVAC[wi] = sv.v

        @block.sync
        def _(s):
            sv.wait(s, WT_EVAC[2])
            ld.dma(s.dma_start(out=bqsb[:, 0:1], in_=bq_sl[0:128, :]))
            ld.dma(s.dma_start(out=bqsb[:, 1:2], in_=bq_sl[128:256, :]))
            ld.dma(s.dma_start(out=bksb[:, 0:1], in_=bk_sl[0:128, :]))
            ld.dma(s.dma_start(out=bksb[:, 1:2], in_=bk_sl[128:256, :]))
            ld.dma(s.dma_start(out=bvsb[:, :], in_=bv_sl[:, :]))
            for i2 in range(NS):
                ld.dma(s.dma_start(out=ssb[:, SW * i2:SW * (i2 + 1)],
                                   in_=strip_in[128 * i2:128 * (i2 + 1), :]))
        LD_WT = ld.v

        @block.tensor
        def _(te):
            ld.wait(te, LD_WT)
            sv.wait(te)
            ste.inc(te.matmul(pb[1][:, 0:256], onesr_f[:, :], bvsb[:, :], start=True, stop=True))
        TE_BV = ste.v

        @block.vector
        def _(v):
            ste.wait(v, TE_BV)
            sv.inc(v.tensor_copy(out=bvrep[:, :], in_=pb[1][:, 0:256]))
        V_BVREP = sv.v


        @block.vector
        def _(v):
            v.memset(mhat[:, :], 0.0)
            v.memset(nrm[:, :], 0.0)
            sv.inc(v.memset(nrmw[:, :], 0.0))
        V_NRM0 = sv.v

        xnt_loaded = [0] * (16 * 16)
        evac_done = [0] * 16
        vevac_done = [0] * 16
        qkevac_done = [0] * 16
        nrm_mm = [0] * 64
        nrm_rd = [0] * 64

        def load_xnt(idx):
            bb, dch = divmod(idx, 16)

            @block.scalar
            def _(sc, idx=idx, bb=bb, dch=dch):
                if idx >= 6:
                    ste.wait(sc, mm_done[idx - 6])
                if idx == 0:
                    sc.wait_ge(cc.h, CC_XN)
                    sg.wait(sc, SG_P1DONE)
                c2 = bb // 2
                toff = 512 * (bb % 2)
                xnt_loaded[idx] = lnt6[idx % 6].dma(
                    sc.dma_start(out=xnt[idx % 6][:, :],
                                 in_=xn_full[D * c2 + 128 * dch:D * c2 + 128 * (dch + 1),
                                             toff:toff + 512]))

        mm_done = [0] * (16 * 16)
        for _pre in range(6):
            load_xnt(_pre)
        for bb in range(16):
            for dch in range(16):
                idx = 16 * bb + dch

                @block.tensor
                def _(te, bb=bb, dch=dch, idx=idx):
                    te.wait_ge(lnt6[idx % 6].h, xnt_loaded[idx])
                    if dch == 0:
                        ld.wait(te, LD_WT)
                        if bb == 0:
                            sv.wait(te, V_BVREP)
                        if bb >= 1:
                            ssc.wait(te, qkevac_done[bb - 1])
                            sv.wait(te, vevac_done[bb - 1])
                    xt = xnt[idx % 6]
                    st_ = dch == 0
                    sp = dch == 15
                    te.matmul(pb[0][:, :], wqt_all[:, 256 * dch:256 * dch + 128], xt[:, :],
                              start=st_, stop=sp)
                    te.matmul(pb[1][:, :], wqt_all[:, 256 * dch + 128:256 * dch + 256], xt[:, :],
                              start=st_, stop=sp)
                    te.matmul(pb[2][:, :], wkt_all[:, 256 * dch:256 * dch + 128], xt[:, :],
                              start=st_, stop=sp)
                    te.matmul(pb[3][:, :], wkt_all[:, 256 * dch + 128:256 * dch + 256], xt[:, :],
                              start=st_, stop=sp)
                    lastmm = None
                    for sb2 in range(4):
                        lastmm = te.matmul(pb[4 + sb2][:, 0:256], xt[:, 128 * sb2:128 * (sb2 + 1)],
                                           wvt_all[:, 256 * dch:256 * (dch + 1)], start=st_, stop=sp)
                    mm_done[idx] = ste.inc(lastmm)
                if idx + 6 < 256:
                    load_xnt(idx + 6)

            @block.scalar
            def _(sc, bb=bb):
                ste.wait(sc, mm_done[16 * bb + 15])
                for hh, pbq in ((0, pb[0]), (1, pb[1])):
                    sc.activation(out=qt[:, S * hh + 512 * bb:S * hh + 512 * (bb + 1)],
                                  in_=pbq[:, :], func=AF.Identity,
                                  bias=bqsb[:, hh:hh + 1], scale=1.0)
                last = None
                for hh, pbk in ((0, pb[2]), (1, pb[3])):
                    last = sc.activation(out=kt[:, S * hh + 512 * bb:S * hh + 512 * (bb + 1)],
                                         in_=pbk[:, :], func=AF.Identity,
                                         bias=bksb[:, hh:hh + 1], scale=1.0)
                ssc.inc(last)
            qkevac_done[bb] = ssc.v

            @block.vector
            def _(v, bb=bb):
                ssc.wait(v)
                sv.wait(v, V_BVREP)
                last = None
                for sb2 in range(4):
                    cj = 4 * bb + sb2
                    for hh in range(2):
                        last = v.tensor_tensor(
                            out=vsb[:, S * hh + 128 * cj:S * hh + 128 * (cj + 1)],
                            in0=pb[4 + sb2][:, 128 * hh:128 * (hh + 1)],
                            in1=bvrep[:, 128 * hh:128 * (hh + 1)], op=ALU.add)
                sv.inc(last)
                vevac_done[bb] = sv.v
            evac_done[bb] = sv.v
        V_PROJ = sv.v

        @block.sync
        def _(s):
            s.wait_ge(cc.h, CC_STATS)
            sld.dma(s.dma_start(out=stats_sb[:, :], in_=stats_out[:, :]))
        LD_STATS = sld.v

        @block.vector
        def _(v):
            v.wait_ge(sld.h, LD_STATS)
            v.tensor_scalar(out=gsc[0:1, 0:1], in0=stats_sb[0:1, 0:1], scalar1=INV_SD,
                            scalar2=0.5, op0=ALU.mult, op1=ALU.is_gt)
            v.drain()
            v.tensor_scalar(out=gsc[0:1, 1:2], in0=gsc[0:1, 0:1], scalar1=NEG,
                            scalar2=None, op0=ALU.mult)
            for c2 in range(3):
                v.tensor_scalar(out=gsc[0:1, 2 + c2:3 + c2], in0=stats_sb[0:1, 1 + c2:2 + c2],
                                scalar1=INV_WN, scalar2=EPS_Q, op0=ALU.mult, op1=ALU.max)
            v.drain()
            sv.inc(v.reciprocal(out=gsc[0:1, 5:8], in_=gsc[0:1, 2:5]))
        V_G = sv.v

        if debug:
            @block.gpsimd
            def _(g):
                sv.wait(g, V_G)
                stm.dma(g.dma_start(out=dbg["stats"][:, :], in_=stats_sb[:, :]))
                stm.dma(g.dma_start(out=dbg["gsc"][:, :], in_=gsc[:, :]))



        for ti, src in enumerate((qt, qt, kt, kt)):
            hh = ti % 2
            for c16 in range(16):
                ni = 16 * ti + c16
                nrm_pb = pb[4 + ni % 4]

                @block.scalar
                def _(sc, ti=ti, hh=hh, c16=c16, ni=ni, src=src):
                    if ni == 0:
                        ssc.wait(sc)
                        sv.wait(sc, V_PROJ)
                    if ni >= 4:
                        ste.wait(sc, nrm_mm[ni - 4])
                    ssc.inc(sc.activation(out=sqsc[ni % 4][:, :],
                                          in_=src[:, S * hh + 512 * c16:S * hh + 512 * (c16 + 1)],
                                          func=AF.Square, bias=0.0, scale=1.0))

                @block.tensor
                def _(te, ni=ni, nrm_pb=nrm_pb):
                    ssc.wait(te)
                    if ni == 0:
                        sv.wait(te, V_PROJ)
                    if ni >= 4:
                        sv.wait(te, nrm_rd[ni - 4])
                    nrm_mm[ni] = ste.inc(te.matmul(nrm_pb[0:16, :], onesw[:, :],
                                                   sqsc[ni % 4][:, :], start=True, stop=True))

                @block.vector
                def _(v, ti=ti, ni=ni, nrm_pb=nrm_pb):
                    ste.wait(v, nrm_mm[ni])
                    if ni == 0:
                        sv.wait(v, V_NRM0)
                    v.drain()
                    v.reduce_max(out=nrmw[:, 4:5], in_=nrm_pb[0:16, :], axis=AX.X)
                    v.drain()
                    nrm_rd[ni] = sv.inc(v.tensor_tensor(out=nrmw[:, ti:ti + 1],
                                                        in0=nrmw[:, ti:ti + 1],
                                                        in1=nrmw[:, 4:5], op=ALU.max))
        V_PROJ = sv.v

        @block.tensor
        def _(te):
            sv.wait(te, V_PROJ)
            ste.inc(te.matmul(pb[0][0:4, 0:16], nrmw[:, 0:4], identf[0:16, 0:16],
                              start=True, stop=True))
        TE_NRMT = ste.v

        @block.vector
        def _(v):
            ste.wait(v, TE_NRMT)
            v.tensor_copy(out=nrmt[:, 0:16], in_=pb[0][0:4, 0:16])
            v.drain()
            sv.inc(v.reduce_max(out=nrmt[:, 16:17], in_=nrmt[:, 0:16], axis=AX.X))
        V_NRMF = sv.v

        @block.tensor
        def _(te):
            sv.wait(te, V_NRMF)
            ste.inc(te.matmul(pb[1][0:1, 0:4], nrmt[:, 16:17], identf[0:4, 0:4],
                              start=True, stop=True))
        TE_NRMR = ste.v

        @block.vector
        def _(v):
            ste.wait(v, TE_NRMR)
            sv.inc(v.tensor_copy(out=nrm[0:1, 0:4], in_=pb[1][0:1, 0:4]))
        V_PROJ = sv.v

        @block.vector
        def _(v):
            sv.inc(v.memset(aoT[:, :], 0.0))
        V_AOT0 = sv.v

        @block.tensor
        def _(te):
            sv.wait(te, V_PROJ)
            for i in range(3):
                ste.inc(te.matmul(pb[1][:, i:i + 1], onesr_f[:, :], gsc[0:1, 5 + i:6 + i],
                                  start=True, stop=True))
            ste.inc(te.matmul(pb[1][:, 3:4], onesr_f[:, :], gsc[0:1, 0:1],
                              start=True, stop=True))
        TE_SWB = ste.v

        @block.vector
        def _(v):
            ste.wait(v, TE_SWB)
            sv.inc(v.tensor_copy(out=swb[:, 0:4], in_=pb[1][:, 0:4]))
        V_SWB = sv.v

        @block.scalar
        def _(sc):
            sv.wait(sc, V_PROJ)
            ssc.inc(sc.sqrt(out=nrm[0:1, 0:4], in_=nrm[0:1, 0:4]))

        @block.vector
        def _(v):
            ssc.wait(v)
            v.tensor_tensor(out=mhat[0:1, 0:2], in0=nrm[0:1, 0:2], in1=nrm[0:1, 2:4], op=ALU.mult)
            v.drain()
            sv.inc(v.tensor_scalar(out=mhat[0:1, 0:2], in0=mhat[0:1, 0:2],
                                   scalar1=-SC_ATTN, scalar2=None, op0=ALU.mult))
        V_MHAT = sv.v

        @block.tensor
        def _(te):
            sv.wait(te, V_MHAT)
            ste.inc(te.matmul(pb[6][:, 0:2], onesr_f[:, :], mhat[0:1, 0:2], start=True, stop=True))
            ste.inc(te.matmul(pb[6][:, 2:3], onesr_f[:, :], gsc[0:1, 1:2], start=True, stop=True))
        TE_EB = ste.v

        @block.vector
        def _(v):
            ste.wait(v, TE_EB)
            v.tensor_copy(out=ebA[:, :], in_=pb[6][:, 0:2])
            v.drain()
            for hh in range(2):
                v.tensor_tensor(out=ebB[:, hh:hh + 1], in0=ebA[:, hh:hh + 1],
                                in1=pb[6][:, 2:3], op=ALU.add)
            v.drain()
            v.tensor_copy(out=ebA0[:, :], in_=ebA[:, :])
            v.tensor_copy(out=ebB0[:, :], in_=ebB[:, :])
            v.drain()
            v.tensor_scalar(out=ebA0[0:1, :], in0=ebA0[0:1, :], scalar1=NEG,
                            scalar2=None, op0=ALU.add)
            sv.inc(v.tensor_scalar(out=ebB0[0:1, :], in0=ebB0[0:1, :], scalar1=NEG,
                                   scalar2=None, op0=ALU.add))
        V_EB = sv.v

        @block.gpsimd
        def _(g):
            sv.wait(g, V_PROJ)
            stm.dma(g.dma_start(out=v_raw[:, :], in_=vsb[:, :]))
        STM_VRAW = stm.v

        @block.vector
        def _(v):
            sv.wait(v, V_SWB)
            sv.inc(v.tensor_scalar(out=swb[:, 4:5], in0=swb[:, 3:4], scalar1=-1.0,
                                   scalar2=1.0, op0=ALU.mult, op1=ALU.add))
        V_GB = sv.v

        atb_done = 0
        for i2 in range(NT):
            @block.sync
            def _(s, i2=i2):
                if i2 >= 1:
                    sv.wait(s, atb_done)
                lat.dma(s.dma_start(out=atscr[:, 0:512], in_=atl_loc[128 * i2:128 * (i2 + 1), :]))
                lat.dma(s.dma_start(out=atscr[:, 512:1024], in_=atl_str[128 * i2:128 * (i2 + 1), :]))
            ld_at = lat.v

            @block.vector
            def _(v, i2=i2, ld_at=ld_at):
                v.wait_ge(lat.h, ld_at)
                sv.wait(v, V_GB)
                v.tensor_scalar(out=atscr[:, 0:512], in0=atscr[:, 0:512],
                                scalar1=swb[:, 3:4], scalar2=None, op0=ALU.mult)
                v.tensor_scalar(out=atb[:, 512 * i2:512 * (i2 + 1)], in0=atscr[:, 512:1024],
                                scalar1=swb[:, 4:5], scalar2=None, op0=ALU.mult)
                v.drain()
                sv.inc(v.tensor_tensor(out=atb[:, 512 * i2:512 * (i2 + 1)],
                                       in0=atb[:, 512 * i2:512 * (i2 + 1)],
                                       in1=atscr[:, 0:512], op=ALU.add))
            atb_done = sv.v
        V_ATB = sv.v

        q_chunks = []
        for widx, (wsrc, wdst, nrows, ncols) in enumerate(
                ((w1_sh, w1q_in, 256, FF), (w2_sh, w2q_in, 256, FF), (w3_sh, w3q_in, FSH, D))):
            for r0 in range(0, nrows, 128):
                nr = min(128, nrows - r0)
                for c0 in range(0, ncols, 512):
                    wd = min(512, ncols - c0)
                    q_chunks.append((widx, wsrc, wdst, r0, nr, c0, wd))
        QLD = [0] * len(q_chunks)
        QCP = [0] * len(q_chunks)
        QST = [0] * len(q_chunks)
        for ci, (widx, wsrc, wdst, r0, nr, c0, wd) in enumerate(q_chunks):
            @block.sync
            def _(s2, ci=ci, widx=widx, wsrc=wsrc, wdst=wdst, r0=r0, nr=nr, c0=c0, wd=wd):
                if ci >= 2:
                    s2.wait_ge(gwq[ci % 2].h, QST[ci - 2])
                QLD[ci] = swq[ci % 2].dma(s2.dma_start(
                    out=qin2[0][0:nr, 512 * (ci % 2):512 * (ci % 2) + wd],
                    in_=wsrc[r0:r0 + nr, c0:c0 + wd]))

            @block.vector
            def _(v, ci=ci, widx=widx, nr=nr, wd=wd):
                if ci == 0:
                    sv.wait(v, V_SWB)
                v.wait_ge(swq[ci % 2].h, QLD[ci])
                co = 512 * (ci % 2)
                v.tensor_scalar(out=qin2[0][0:nr, co:co + wd], in0=qin2[0][0:nr, co:co + wd],
                                scalar1=swb[0:nr, widx:widx + 1], scalar2=MAGIC,
                                op0=ALU.mult, op1=ALU.add)
                v.drain()
                v.tensor_scalar(out=qin2[0][0:nr, co:co + wd], in0=qin2[0][0:nr, co:co + wd],
                                scalar1=MAGIC, scalar2=None, op0=ALU.subtract)
                v.drain()
                QCP[ci] = sv.inc(v.tensor_scalar(out=qin2[0][0:nr, co:co + wd],
                                                 in0=qin2[0][0:nr, co:co + wd],
                                                 scalar1=1.0, scalar2=-1.0,
                                                 op0=ALU.min, op1=ALU.max))

            @block.gpsimd
            def _(g, ci=ci, wdst=wdst, r0=r0, nr=nr, c0=c0, wd=wd):
                sv.wait(g, QCP[ci])
                QST[ci] = gwq[ci % 2].dma(g.dma_start(
                    out=wdst[r0:r0 + nr, c0:c0 + wd],
                    in_=qin2[0][0:nr, 512 * (ci % 2):512 * (ci % 2) + wd]))

        @block.gpsimd
        def _(g):
            gwq[0].wait(g)
            gwq[1].wait(g)
            cc.inc(g.collective_compute(
                "AllGather", ALU.bypass, replica_groups=[list(range(NCORES))],
                ins=[w1q_in.ap().opt()], outs=[w1q.ap().opt()]))
            cc.inc(g.collective_compute(
                "AllGather", ALU.bypass, replica_groups=[list(range(NCORES))],
                ins=[w2q_in.ap().opt()], outs=[w2q.ap().opt()]))
            cc.inc(g.collective_compute(
                "AllGather", ALU.bypass, replica_groups=[list(range(NCORES))],
                ins=[w3q_in.ap().opt()], outs=[w3q.ap().opt()]))
        CC_W = cc.v


        stp2a.close()
        stp3 = contextlib.ExitStack()
        l_sb = stp3.enter_context(nc.sbuf_tensor("l_sb", [1, S], F32))
        scr8k = stp3.enter_context(nc.sbuf_tensor("scr8k", [128, S], BF16))
        PT = [stp3.enter_context(nc.sbuf_tensor(f"PT{i}", [128, 512], BF16)) for i in range(4)]
        PTA = [stp3.enter_context(nc.sbuf_tensor(f"PTA{i}", [128, 512], BF16)) for i in range(4)]
        Pq0 = stp3.enter_context(nc.sbuf_tensor("Pq0", [128, 16], BF16))
        q0row = stp3.enter_context(nc.sbuf_tensor("q0row", [1, 128], F32))

        strip_by = {}
        for (rho_, j_, alo_, ahi_, svid_) in strips:
            strip_by[(rho_, j_)] = (svid_, alo_, ahi_)
        piece_by = {}
        for (rhk_, kb_, vid_) in pieces:
            piece_by[(rhk_, kb_)] = vid_
        kbs_all = sorted({kb_ for (_, kb_, _) in pieces})

        V_P3INIT = V_AOT0

        for h in range(2):
            hS = h * S

            @block.vector
            def _(v, h=h):
                if h == 0:
                    sv.wait(v, V_P3INIT)
                    ssc.wait(v)
                    ste.wait(v)
                else:
                    ssc.wait(v)
                    ste.wait(v)
                    stm.wait(v)
                sv.inc(v.memset(l_sb[:, :], 0.0))
            V_LINIT = sv.v
            dp_mm = [0] * 16
            dp_ex = [0] * 16
            for grp in range(16):
                @block.tensor
                def _(te, h=h, grp=grp):
                    if grp <= 3:
                        if h == 0:
                            sv.wait(te, V_EB)
                        else:
                            sv.wait(te)
                            ssc.wait(te)
                    else:
                        ssc.wait(te, dp_ex[grp - 4])
                    dp_mm[grp] = ste.inc(te.matmul(pb[4 + grp % 4][0:1, 0:512], kt[:, hS:hS + 1],
                                                   qt[:, hS + 512 * grp:hS + 512 * (grp + 1)],
                                                   start=True, stop=True))

                @block.scalar
                def _(sc, h=h, grp=grp):
                    ste.wait(sc, dp_mm[grp])
                    if h == 0 and grp == 0:
                        sv.wait(sc, V_ATB)
                    dp_ex[grp] = ssc.inc(sc.activation(
                        out=scr8k[0:1, 512 * grp:512 * (grp + 1)],
                        in_=pb[4 + grp % 4][0:1, 0:512], func=AF.Exp,
                        bias=ebA[0:1, h:h + 1], scale=SC_ATTN))
            D_PREP = ssc.v

            exp_all = {}
            bev_sc = {}
            bev_v = {}
            for rho in range(4):
                rA = rho * A
                for g2 in range(4):
                    nC = rho * 4 + g2
                    po = pb[3]
                    pl = pb[4]
                    qc0 = hS + rA + 512 * g2

                    @block.tensor
                    def _(te, h=h, rho=rho, g2=g2, nC=nC, po=po, pl=pl, qc0=qc0):
                        if nC == 0:
                            ssc.wait(te, D_PREP)
                            sv.wait(te)
                        else:
                            ssc.wait(te, bev_sc[nC - 1])
                            sv.wait(te, bev_v[nC - 1])
                        te.matmul(po[:, :], vsb[0:1, hS:hS + 128],
                                  scr8k[0:1, rA + 512 * g2:rA + 512 * (g2 + 1)],
                                  start=True, stop=False)
                        ste.inc(te.matmul(pl[0:1, 0:512], one_bf[:, :],
                                          scr8k[0:1, rA + 512 * g2:rA + 512 * (g2 + 1)],
                                          start=True, stop=False))

                    exp_ct = {}

                    def emit_qk(j):
                        jj = 16 * nC + j
                        bk = (0, 1, 2, 5)[jj % 4]
                        pt = PT[jj % 4]

                        @block.tensor
                        def _(te, h=h, rho=rho, g2=g2, nC=nC, j=j, jj=jj, bk=bk, qc0=qc0):
                            if jj >= 4:
                                ssc.wait(te, exp_all[jj - 4])
                            elif bk == 5:
                                ssc.wait(te, D_PREP)
                            mmq = te.matmul(pb[bk][:, :],
                                            kt[:, hS + rA + 128 * j:hS + rA + 128 * (j + 1)],
                                            qt[:, qc0:qc0 + 512], start=True, stop=True)
                            if rho > 0 and g2 == 0 and "q0" not in P3_SKIP:
                                mmq = te.matmul(pb[6][:, j:j + 1],
                                                kt[:, hS + rA + 128 * j:hS + rA + 128 * (j + 1)],
                                                qt[:, hS:hS + 1], start=True, stop=True)
                            ste.inc(mmq)

                        sb_info = strip_by.get((rho, j))
                        ovl = None
                        if sb_info:
                            svid_, alo_, ahi_ = sb_info
                            lo = max(alo_, 512 * g2)
                            hi = min(ahi_, 512 * (g2 + 1))
                            if lo < hi:
                                ovl = (svid_, alo_, lo, hi)
                        if ovl is not None:
                            @block.vector
                            def _(v, ovl=ovl, g2=g2, bk=bk):
                                svid_, alo_, lo, hi = ovl
                                ste.wait(v)
                                sv.inc(v.tensor_tensor(
                                    out=pb[bk][:, lo - 512 * g2:hi - 512 * g2],
                                    in0=pb[bk][:, lo - 512 * g2:hi - 512 * g2],
                                    in1=ssb[:, SW * svid_ + lo - alo_:SW * svid_ + hi - alo_],
                                    op=ALU.add))
                        strip_v = sv.v if ovl is not None else None

                        @block.scalar
                        def _(sc, h=h, rho=rho, g2=g2, j=j, bk=bk, pt=pt, strip_v=strip_v):
                            if strip_v is not None:
                                sv.wait(sc, strip_v)
                            else:
                                ste.wait(sc)
                            eb_main = ebB0 if (rho == 0 and j == 0) else ebB
                            if rho == 0 and g2 == 0:
                                eb_c0 = ebA0 if j == 0 else ebA
                                sc.activation(out=pt[:, 1:512], in_=pb[bk][:, 1:512],
                                              func=AF.Exp, bias=eb_main[:, h:h + 1], scale=SC_ATTN)
                                ssc.inc(sc.activation(out=pt[:, 0:1], in_=pb[bk][:, 0:1],
                                                      func=AF.Exp, bias=eb_c0[:, h:h + 1],
                                                      scale=SC_ATTN))
                            else:
                                ssc.inc(sc.activation(out=pt[:, :], in_=pb[bk][:, :],
                                                      func=AF.Exp, bias=eb_main[:, h:h + 1],
                                                      scale=SC_ATTN))
                        exp_ct[j] = ssc.v
                        exp_all[jj] = ssc.v

                    def emit_av(j):
                        pt = PT[(16 * nC + j) % 4]

                        @block.tensor
                        def _(te, h=h, rho=rho, g2=g2, j=j, pt=pt, po=po, pl=pl):
                            ssc.wait(te, exp_ct[j])
                            te.matmul(po[:, :], vsb[:, hS + 128 * (16 * rho + j):hS + 128 * (16 * rho + j + 1)],
                                      pt[:, :], start=False, stop=(j == 15))
                            ste.inc(te.matmul(pl[0:1, 0:512], ones_bf[:, :], pt[:, :],
                                              start=False, stop=(j == 15)))

                    for j in range(16):
                        if j >= 4:
                            emit_av(j - 4)
                        emit_qk(j)
                    for jf in (12, 13, 14, 15):
                        emit_av(jf)

                    @block.scalar
                    def _(sc, h=h, rho=rho, g2=g2, pl=pl, qc0=qc0):
                        ste.wait(sc)
                        ssc.inc(sc.copy(out=l_sb[0:1, qc0 - hS:qc0 - hS + 512], in_=pl[0:1, 0:512]))
                    bev_sc[nC] = ssc.v

                    @block.vector
                    def _(v, h=h, rho=rho, g2=g2, po=po, qc0=qc0):
                        ste.wait(v)
                        sv.inc(v.tensor_copy(out=aoT[:, qc0:qc0 + 512], in_=po[:, :]))
                    bev_v[nC] = sv.v

                if rho > 0 and "q0" not in P3_SKIP:
                    @block.vector
                    def _(v, h=h, rho=rho):
                        ste.wait(v)
                        sv.inc(v.tensor_scalar(out=pb[6][0:16, 0:1], in0=pb[6][0:16, 0:1],
                                               scalar1=NEG, scalar2=None, op0=ALU.add))
                    V_Q0NEG = sv.v

                    @block.scalar
                    def _(sc, h=h, rho=rho):
                        sv.wait(sc, V_Q0NEG)
                        ssc.inc(sc.activation(out=Pq0[:, :], in_=pb[6][:, 0:16], func=AF.Exp,
                                              bias=ebA[:, h:h + 1], scale=SC_ATTN))
                    SC_PQ0 = ssc.v

                    @block.tensor
                    def _(te, h=h, rho=rho):
                        ssc.wait(te, SC_PQ0)
                        for j in range(16):
                            te.matmul(pb[6][0:1, 256:384],
                                      Pq0[:, j:j + 1],
                                      vsb[:, hS + 128 * (16 * rho + j):hS + 128 * (16 * rho + j + 1)],
                                      start=(j == 0), stop=(j == 15))
                        ste.inc(te.matmul(pb[6][0:1, 384:400], ones_bf[:, :], Pq0[:, :],
                                          start=True, stop=True))
                    TE_Q0 = ste.v

                    @block.scalar
                    def _(sc, h=h, rho=rho):
                        ste.wait(sc, TE_Q0)
                        ssc.inc(sc.copy(out=q0row[0:1, :], in_=pb[6][0:1, 256:384]))
                    SC_Q0R = ssc.v

                    @block.vector
                    def _(v, h=h, rho=rho):
                        ste.wait(v, TE_Q0)
                        v.reduce_sum(out=nrm[0:1, 7:8], in_=pb[6][0:1, 384:400], axis=AX.X)
                        v.drain()
                        sv.inc(v.tensor_tensor(out=l_sb[0:1, 0:1], in0=l_sb[0:1, 0:1],
                                               in1=nrm[0:1, 7:8], op=ALU.add))

                    @block.tensor
                    def _(te, h=h, rho=rho):
                        ssc.wait(te, SC_Q0R)
                        sv.wait(te)
                        ste.inc(te.matmul(pb[6][:, 400:401], q0row[:, :], one_f[:, :],
                                          start=True, stop=True))
                    TE_Q0T = ste.v

                    @block.vector
                    def _(v, h=h, rho=rho):
                        ste.wait(v, TE_Q0T)
                        sv.inc(v.tensor_tensor(out=aoT[:, hS:hS + 1], in0=aoT[:, hS:hS + 1],
                                               in1=pb[6][:, 400:401], op=ALU.add))

            @block.sync
            def _(s2, h=h, hS=hS):
                ssc.wait(s2)
                ste.wait(s2)
                s2.wait_ge(stm.h, STM_VRAW)
                lnt[h].dma(s2.dma_start(out=scr8k[0:64, :], in_=v_raw[64:128, hS:hS + S]))
                lnt[h].dma(s2.dma_start(out=scr8k[64:128, 0:S - 128],
                                        in_=v_raw[0:64, hS + 128:hS + S]))
            LNT_VSH = lnt[h].v

            kbs_use = [] if "A" in P3_SKIP else [kb for kb in kbs_all
                                                 if not (kb % 128 != 0 and "Ashift" in P3_SKIP)]
            apieces = [(pi, kb, rhk) for pi, kb in enumerate(kbs_use) for rhk in range(4)]
            a_exp = {}
            a_av = {}
            a_rmw = {}

            def a_qk(ip):
                pi, kb, rhk = apieces[ip]
                vid = piece_by[(rhk, kb)]

                @block.tensor
                def _(te, h=h, kb=kb, rhk=rhk, vid=vid, ip=ip):
                    if ip >= 3:
                        ssc.wait(te, a_exp[ip - 3])
                    else:
                        sv.wait(te)
                    if ip == 0:
                        sv.wait(te, V_ATB)
                        te.wait_ge(lnt[h].h, LNT_VSH)
                    qsl = qt.ap().rearrange("p (hh r a) -> p hh r a", hh=2, r=4)[:, h, :, kb:kb + 128]
                    te.matmul(pb[ip % 3][:, :], identb[:, :], atb[:, 512 * vid:512 * (vid + 1)],
                              start=True, stop=False)
                    ste.inc(te.matmul(pb[ip % 3][:, :],
                                      kt[:, hS + rhk * A + kb:hS + rhk * A + kb + 128],
                                      qsl, start=False, stop=True))

                @block.scalar
                def _(sc, h=h, ip=ip):
                    ste.wait(sc)
                    ssc.inc(sc.activation(out=PTA[ip % 4][:, :], in_=pb[ip % 3][:, :],
                                          func=AF.Exp, bias=ebA[:, h:h + 1], scale=SC_ATTN))
                a_exp[ip] = ssc.v

            def a_avf(ip):
                pi, kb, rhk = apieces[ip]
                pav = pb[3] if pi % 2 == 0 else pb[5]
                pla = pb[4] if pi % 2 == 0 else pb[7]

                @block.tensor
                def _(te, h=h, kb=kb, rhk=rhk, pi=pi, ip=ip, pav=pav, pla=pla):
                    ssc.wait(te, a_exp[ip])
                    if rhk == 0 and pi >= 2:
                        sv.wait(te, a_rmw[pi - 2])
                    cj = 16 * rhk + kb // 128
                    st_ = rhk == 0
                    sp = rhk == 3
                    if kb % 128 == 0:
                        te.matmul(pav[:, :], vsb[:, hS + 128 * cj:hS + 128 * (cj + 1)],
                                  PTA[ip % 4][:, :], start=st_, stop=sp)
                    else:
                        te.matmul(pav[:, :], scr8k[:, 128 * cj:128 * (cj + 1)],
                                  PTA[ip % 4][:, :], start=st_, stop=sp)
                    ste.inc(te.matmul(pla[0:1, 0:512], ones_bf[:, :], PTA[ip % 4][:, :],
                                      start=st_, stop=sp))
                a_av[ip] = ste.v

            def a_rmwf(ip):
                pi, kb, rhk = apieces[ip]
                if rhk != 3:
                    return
                pav = pb[3] if pi % 2 == 0 else pb[5]
                pla = pb[4] if pi % 2 == 0 else pb[7]

                @block.vector
                def _(v, h=h, kb=kb, pi=pi, ip=ip, pav=pav, pla=pla):
                    ste.wait(v, a_av[ip])
                    v.drain()
                    ao3 = aoT.ap().rearrange("p (hh r a) -> p hh r a", hh=2, r=4)[:, h, :, kb:kb + 128]
                    v.tensor_tensor(out=ao3, in0=ao3, in1=pav[:, :], op=ALU.add)
                    l3 = l_sb.ap().rearrange("o (r a) -> o r a", r=4)[:, :, kb:kb + 128]
                    pl3 = pla[0:1, 0:512].rearrange("o (r a) -> o r a", r=4)
                    sv.inc(v.tensor_tensor(out=l3, in0=l3, in1=pl3, op=ALU.add))
                a_rmw[pi] = sv.v

            for ip in range(len(apieces)):
                a_qk(ip)
                if ip >= 3:
                    a_avf(ip - 3)
                    a_rmwf(ip - 3)
            for ip in range(max(0, len(apieces) - 3), len(apieces)):
                a_avf(ip)
                a_rmwf(ip)
            if debug:
                @block.gpsimd
                def _(g, h=h, hS=hS):
                    sv.wait(g)
                    stm.dma(g.dma_start(out=dbg["l"][0:1, hS:hS + S], in_=l_sb[0:1, :]))

            for bq2 in range(4):
                @block.tensor
                def _(te, h=h, bq2=bq2):
                    sv.wait(te)
                    ssc.wait(te)
                    lastm = None
                    for b2 in range(16):
                        b3 = 16 * bq2 + b2
                        lastm = te.matmul(pb[6][:, 416 + b2:417 + b2],
                                          l_sb[0:1, 128 * b3:128 * (b3 + 1)],
                                          one_f[:, :], start=True, stop=True)
                    ste.inc(lastm)

                @block.scalar
                def _(sc, h=h, bq2=bq2):
                    ste.wait(sc)
                    ssc.inc(sc.copy(out=rcol[:, 64 * h + 16 * bq2:64 * h + 16 * (bq2 + 1)],
                                    in_=pb[6][:, 416:432]))
            RC_H = ssc.v

        @block.vector
        def _(v):
            ssc.wait(v)
            v.drain()
            sv.inc(v.reciprocal(out=rcol[:, :], in_=rcol[:, :]))
        V_ATTN = sv.v

        stp3.close()
        OP_EV = [0] * 64
        OP_V1 = [0] * 64
        woT_sb = stp2.enter_context(nc.sbuf_tensor("woT_sb", [128, 2 * D], BF16))
        obuf2 = [stp2.enter_context(nc.sbuf_tensor(f"obuf{i}", [128, D], F32)) for i in range(2)]

        @block.sync
        def _(s2):
            s2.wait_ge(gld.h, LD_WCAST)
            sv.wait(s2, V_ATTN)
            ssc.wait(s2)
            ste.wait(s2)
            stm.wait(s2)
            for q_ in lnt6:
                q_.wait(s2)
            ld.dma(s2.dma_start(out=woT_sb[:, 0:D], in_=wo_bf[0:128, :]))
            ld.dma(s2.dma_start(out=woT_sb[:, D:2 * D], in_=wo_bf[128:256, :]))
        LD_WOT = ld.v

        OP_SC0 = [0] * 64
        OP_TE1 = [0] * 64
        for bb in range(64):
            obuf = obuf2[bb % 2]

            @block.tensor
            def _(te, bb=bb):
                if bb == 0:
                    sv.wait(te, V_ATTN)
                    te.wait_ge(ld.h, LD_WOT)
                    stm.wait(te)
                else:
                    ssc.wait(te, OP_SC0[bb - 1])
                last = None
                for g3 in range(4):
                    last = te.matmul(pb[g3][:, :], aoT[:, 128 * bb:128 * (bb + 1)],
                                     woT_sb[:, 512 * g3:512 * (g3 + 1)], start=True, stop=True)
                ste.inc(last)
            TE_OP0 = ste.v

            @block.tensor
            def _(te, bb=bb):
                if bb >= 1:
                    sv.wait(te, OP_V1[bb - 1])
                last = None
                for g3 in range(4):
                    last = te.matmul(pb[4 + g3][:, :], aoT[:, S + 128 * bb:S + 128 * (bb + 1)],
                                     woT_sb[:, D + 512 * g3:D + 512 * (g3 + 1)],
                                     start=True, stop=True)
                ste.inc(last)
            OP_TE1[bb] = ste.v

            @block.scalar
            def _(sc, bb=bb, TE_OP0=TE_OP0, obuf=obuf):
                ste.wait(sc, TE_OP0)
                if bb >= 2:
                    sc.wait_ge(stx[bb % 2].h, OP_EV[bb - 2])
                last = None
                for g3 in range(4):
                    last = sc.mul(out=obuf[:, 512 * g3:512 * (g3 + 1)], in_=pb[g3][:, :],
                                  mul=rcol[:, bb:bb + 1])
                ssc.inc(last)
            OP_SC0[bb] = ssc.v

            @block.vector
            def _(v, bb=bb, obuf=obuf):
                ste.wait(v, OP_TE1[bb])
                ssc.wait(v, OP_SC0[bb])
                last = None
                for g3 in range(4):
                    last = v.scalar_tensor_tensor(
                        out=obuf[:, 512 * g3:512 * (g3 + 1)], in0=pb[4 + g3][:, :],
                        scalar=rcol[:, 64 + bb:64 + bb + 1],
                        in1=obuf[:, 512 * g3:512 * (g3 + 1)], op0=ALU.mult, op1=ALU.add)
                sv.inc(last)
            OP_V1[bb] = sv.v

            @block.gpsimd
            def _(g, bb=bb, obuf=obuf):
                sv.wait(g, OP_V1[bb])
                OP_EV[bb] = stx[bb % 2].dma(g.dma_start(out=aop_d[128 * bb:128 * (bb + 1), :],
                                                        in_=obuf[:, :]))

        @block.gpsimd
        def _(g):
            stm.wait(g)
            stx[0].wait(g)
            stx[1].wait(g)
            cc.inc(g.collective_compute(
                "ReduceScatter", ALU.add, replica_groups=[list(range(NCORES))],
                ins=[aop_d.ap().opt()], outs=[aor_d.ap().opt()]))
        CC_RS = cc.v

        if debug:
            @block.gpsimd
            def _(g):
                sv.wait(g)
                ssc.wait(g)
                ste.wait(g)
                stm.dma(g.dma_start(out=dbg["qt"][:, :], in_=qt[:, :]))
                stm.dma(g.dma_start(out=dbg["kt"][:, :], in_=kt[:, :]))
                stm.dma(g.dma_start(out=dbg["vsb"][:, :], in_=vsb[:, :]))
                stm.dma(g.dma_start(out=dbg["nrm"][:, :], in_=nrm[:, :]))
                stm.dma(g.dma_start(out=dbg["mhat"][:, :], in_=mhat[:, :]))
                stm.dma(g.dma_start(out=dbg["ebA"][:, :], in_=ebA[:, :]))
                stm.dma(g.dma_start(out=dbg["ebB"][:, :], in_=ebB[:, :]))
                if "aoT" in dbg:
                    stm.dma(g.dma_start(out=dbg["aoT"][:, :], in_=aoT[:, :]))
                    stm.dma(g.dma_start(out=dbg["rcol"][:, :], in_=rcol[:, :]))

        XQ3_Q = [0] * NBLK
        XQ3_T = [0] * NBLK
        MM3 = {}
        FIN_LD = {}
        stp2.close()
        stp4 = contextlib.ExitStack()
        macc = stp4.enter_context(nc.sbuf_tensor("macc", [128, NBLK], F32))
        sacc2 = stp4.enter_context(nc.sbuf_tensor("sacc2", [128, NBLK], F32))
        mwb = stp4.enter_context(nc.sbuf_tensor("mwb", [128, 3], F32))
        os1w = stp4.enter_context(nc.sbuf_tensor("os1w", [128, NBLK], F32))
        q3v = stp4.enter_context(nc.sbuf_tensor("q3v", [128, NBLK], F32))
        os3w = stp4.enter_context(nc.sbuf_tensor("os3w", [128, NBLK], F32))
        borep = stp4.enter_context(nc.sbuf_tensor("borep", [128, D], F32))
        bosb = stp4.enter_context(nc.sbuf_tensor("bosb", [1, D], F32))
        stmp = [stp4.enter_context(nc.sbuf_tensor(f"stmp{i}", [128, 1], F32)) for i in range(2)]
        stmp2 = [stp4.enter_context(nc.sbuf_tensor(f"stmp2_{i}", [128, 1], F32)) for i in range(2)]

        @block.sync
        def _(s2):
            stm.wait(s2)
            stx[0].wait(s2)
            stx[1].wait(s2)
            sv.wait(s2)
            ssc.wait(s2)
            ste.wait(s2)
            sg.inc(s2.nop())
        SG_P4 = sg.v

        @block.sync
        def _(s2):
            sg.wait(s2, SG_P4)
            ld.dma(s2.dma_start(out=bosb[:, :], in_=bo_in[:, :]))
        LD_BO = ld.v

        @block.tensor
        def _(te):
            te.wait_ge(ld.h, LD_BO)
            sg.wait(te, SG_P4)
            for g3 in range(4):
                te.matmul(pb[g3][:, :], onesr_f[:, :], bosb[0:1, 512 * g3:512 * (g3 + 1)],
                          start=True, stop=True)
            for c3 in range(3):
                te.matmul(pb[4][:, c3:c3 + 1], onesr_f[:, :], gsc[0:1, 2 + c3:3 + c3],
                          start=True, stop=True)
            ste.inc(te.matmul(pb[4][:, 3:4], onesr_f[:, :], gsc[0:1, 2:3],
                              start=True, stop=True))
        TE_BOREP = ste.v

        @block.vector
        def _(v):
            ste.wait(v, TE_BOREP)
            sg.wait(v, SG_P4)
            for g3 in range(4):
                v.tensor_copy(out=borep[:, 512 * g3:512 * (g3 + 1)], in_=pb[g3][:, :])
            v.tensor_copy(out=mwb[:, :], in_=pb[4][:, 0:3])
            v.memset(macc[:, :], 0.0)
            v.memset(sacc2[:, :], 0.0)
            v.drain()
            sv.inc(v.tensor_scalar(out=os1w[:, :], in0=os1v[:, :], scalar1=mwb[:, 0:1],
                                   scalar2=None, op0=ALU.mult))
        V_P5INIT = sv.v

        stp4a = contextlib.ExitStack()
        xq1t = stp4a.enter_context(nc.sbuf_tensor("xq1t", [128, 16 * TB], BF16))
        wqrt2 = [stp4a.enter_context(nc.sbuf_tensor(f"wqrt{i}", [128, 16 * 1408], BF16)) for i in range(2)]
        gq = stp4a.enter_context(nc.sbuf_tensor("gq", [128, NBLK * 1408], BF16))
        ubuf = [stp4a.enter_context(nc.sbuf_tensor(f"ubuf{i}", [128, 1408], BF16)) for i in range(2)]
        ubuf2 = [stp4a.enter_context(nc.sbuf_tensor(f"ubuf2_{i}", [128, 1408], BF16)) for i in range(2)]
        gub = [stp4a.enter_context(nc.sbuf_tensor(f"gub{i}", [128, 1408], BF16)) for i in range(2)]

        @block.sync
        def _(s2):
            sg.wait(s2, SG_P4)
            sv.wait(s2, V_P5INIT)
            ld.dma(s2.dma_start(out=xq1t[:, 0:8 * TB],
                                in_=xq1_d.ap().rearrange("(c p) t -> p c t", p=128)[:, 0:8, :]))
            ld2.dma(s2.dma_start(out=xq1t[:, 8 * TB:16 * TB],
                                 in_=xq1_d.ap().rearrange("(c p) t -> p c t", p=128)[:, 8:16, :]))
        LD_XQ1T = ld.v
        LD_XQ1T2 = ld2.v

        GUB_ST = {}
        V_GUS = {}
        GU_SQ = []

        SACC_V = []

        def emit_gusq(item):
            q4g, b4g, V_GUg = item
            gs = len(SACC_V)

            @block.scalar
            def _(sc, q4=q4g, b4=b4g, V_GU=V_GUg, gs=gs):
                sv.wait(sc, V_GU)
                if gs >= 2:
                    sv.wait(sc, SACC_V[gs - 2])
                ssc.inc(sc.activation(out=ubuf2[gs % 2][:, :], in_=gub[b4 % 2][:, :],
                                      func=AF.Square, accum_out=stmp2[gs % 2][:, :]))
            SC_SQ3 = ssc.v

            @block.vector
            def _(v, q4=q4g, b4=b4g, SC_SQ3=SC_SQ3, gs=gs):
                ssc.wait(v, SC_SQ3)
                v.drain()
                sv.inc(v.tensor_tensor(out=sacc2[:, b4:b4 + 1], in0=sacc2[:, b4:b4 + 1],
                                       in1=stmp2[gs % 2][:, :], op=ALU.add))
            SACC_V.append(sv.v)
        P5_EVS = []
        P5_TE_PASS = {}
        for q4 in range(4):
            for wpass, wsrc in ((0, w1q), (1, w2q)):
                pl = 2 * q4 + wpass
                wqrt = wqrt2[pl % 2]

                @block.sync
                def _(s2, q4=q4, wpass=wpass, wsrc=wsrc, pl=pl, wqrt=wqrt):
                    if q4 == 0 and wpass == 0:
                        s2.wait_ge(ld.h, LD_XQ1T)
                        s2.wait_ge(ld2.h, LD_XQ1T2)
                    if pl <= 1:
                        s2.wait_ge(cc.h, CC_W)
                    if pl >= 2:
                        ste.wait(s2, P5_TE_PASS[pl - 2])
                    lnt[pl % 2].dma(s2.dma_start(
                        out=wqrt[:, :],
                        in_=wsrc.ap().rearrange("(c p) f -> p c f", p=128)[:, :, 1408 * q4:1408 * (q4 + 1)]))
                LD_WQRT = lnt[pl % 2].v

                for b4 in range(NBLK):
                    pbase = 0 if b4 % 2 == 0 else 3
                    L = len(P5_EVS)

                    @block.tensor
                    def _(te, q4=q4, wpass=wpass, b4=b4, pbase=pbase, LD_WQRT=LD_WQRT,
                          pl=pl, L=L, wqrt=wqrt):
                        te.wait_ge(lnt[pl % 2].h, LD_WQRT)
                        if L >= 2:
                            ssc.wait(te, P5_EVS[L - 2])
                        last = None
                        for dch in range(16):
                            st_ = dch == 0
                            sp = dch == 15
                            lx = xq1t[:, TB * dch + 128 * b4:TB * dch + 128 * (b4 + 1)]
                            last = te.matmul(pb[pbase][:, :], lx,
                                             wqrt[:, 1408 * dch:1408 * dch + 512], start=st_, stop=sp)
                            last = te.matmul(pb[pbase + 1][:, :], lx,
                                             wqrt[:, 1408 * dch + 512:1408 * dch + 1024], start=st_, stop=sp)
                            last = te.matmul(pb[pbase + 2][:, 0:384], lx,
                                             wqrt[:, 1408 * dch + 1024:1408 * dch + 1408], start=st_, stop=sp)
                        ste.inc(last)
                    TE_BL = ste.v
                    if b4 == NBLK - 1:
                        P5_TE_PASS[pl] = TE_BL

                    if wpass == 0:
                        @block.scalar
                        def _(sc, q4=q4, b4=b4, pbase=pbase, TE_BL=TE_BL):
                            ste.wait(sc, TE_BL)
                            prevgu = V_GUS.get((q4 - 1, b4))
                            if prevgu:
                                sv.wait(sc, prevgu)
                            last = None
                            for g3, w3_ in ((0, 512), (1, 512), (2, 384)):
                                last = sc.activation(
                                    out=gq[:, 1408 * b4 + 512 * g3:1408 * b4 + 512 * g3 + w3_],
                                    in_=pb[pbase + g3][:, 0:w3_], func=AF.Silu,
                                    bias=0.0, scale=os1w[:, b4:b4 + 1])
                            ssc.inc(last)
                        P5_EVS.append(ssc.v)
                    else:
                        @block.scalar
                        def _(sc, q4=q4, b4=b4, pbase=pbase, TE_BL=TE_BL):
                            ste.wait(sc, TE_BL)
                            prevgu = V_GUS.get((q4, b4 - 2)) or V_GUS.get((q4 - 1, b4 + 6))
                            if prevgu:
                                sv.wait(sc, prevgu)
                            last = None
                            for g3, w3_ in ((0, 512), (1, 512), (2, 384)):
                                last = sc.mul(out=ubuf[b4 % 2][:, 512 * g3:512 * g3 + w3_],
                                              in_=pb[pbase + g3][:, 0:w3_],
                                              mul=os1w[:, b4:b4 + 1])
                            ssc.inc(last)
                        SC_UP = ssc.v
                        P5_EVS.append(SC_UP)

                        @block.vector
                        def _(v, q4=q4, b4=b4, SC_UP=SC_UP):
                            ssc.wait(v)
                            prevst = GUB_ST.get((q4, b4 - 2)) or GUB_ST.get((q4 - 1, b4 + 6))
                            if prevst:
                                v.wait_ge(stx[b4 % 2].h, prevst)
                            v.drain()
                            v.tensor_tensor(out=gub[b4 % 2][:, :],
                                            in0=gq[:, 1408 * b4:1408 * (b4 + 1)],
                                            in1=ubuf[b4 % 2][:, :], op=ALU.mult)
                            v.drain()
                            v.reduce_max(out=stmp[b4 % 2][:, :], in_=gub[b4 % 2][:, :], axis=AX.X,
                                         apply_absolute_value=True)
                            v.drain()
                            sv.inc(v.tensor_tensor(out=macc[:, b4:b4 + 1], in0=macc[:, b4:b4 + 1],
                                                   in1=stmp[b4 % 2][:, :], op=ALU.max))
                        V_GU = sv.v
                        V_GUS[(q4, b4)] = V_GU
                        GU_SQ.append((q4, b4, V_GU))

                        @block.gpsimd
                        def _(g, q4=q4, b4=b4, V_GU=V_GU):
                            sv.wait(g, V_GU)
                            GUB_ST[(q4, b4)] = stx[b4 % 2].dma(
                                g.dma_start(out=gu_d[128 * b4:128 * (b4 + 1),
                                                     1408 * q4:1408 * (q4 + 1)],
                                            in_=gub[b4 % 2][:, :]))

                        if len(GU_SQ) >= 2:
                            emit_gusq(GU_SQ[-2])
        if GU_SQ:
            emit_gusq(GU_SQ[-1])
        V_P5 = sv.v

        @block.sync
        def _(s2):
            sv.wait(s2)
            ssc.wait(s2)
            ste.wait(s2)
            stx[0].wait(s2)
            stx[1].wait(s2)
            sg.inc(s2.nop())
        SG_P5 = sg.v
        stp4a.close()

        stp4b = contextlib.ExitStack()
        xq3t = stp4b.enter_context(nc.sbuf_tensor("xq3t", [128, 44 * TB], BF16))
        gul = [stp4b.enter_context(nc.sbuf_tensor(f"gul{i}", [128, FF], BF16)) for i in range(2)]
        xq3b = stp4b.enter_context(nc.sbuf_tensor("xq3b", [128, FF], BF16))
        rnd3 = stp4b.enter_context(nc.sbuf_tensor("rnd3", [128, FF], F32))

        @block.vector
        def _(v):
            sg.wait(v, SG_P5)
            v.tensor_scalar(out=q3v[:, :], in0=sacc2[:, :], scalar1=INV_FF,
                            scalar2=EPS_RMS, op0=ALU.mult, op1=ALU.add)
            v.drain()
            sv.inc(v.reciprocal(out=q3v[:, :], in_=q3v[:, :]))
        V_Q3A = sv.v

        @block.scalar
        def _(sc):
            sv.wait(sc, V_Q3A)
            ssc.inc(sc.sqrt(out=q3v[:, :], in_=q3v[:, :]))
        SC_R3 = ssc.v

        @block.vector
        def _(v):
            ssc.wait(v, SC_R3)
            v.tensor_tensor(out=os3w[:, :], in0=q3v[:, :], in1=macc[:, :], op=ALU.mult)
            v.drain()
            v.tensor_scalar(out=os3w[:, :], in0=os3w[:, :], scalar1=EPS_Q,
                            scalar2=None, op0=ALU.max)
            v.drain()
            v.reciprocal(out=macc[:, :], in_=os3w[:, :])
            v.drain()
            v.tensor_scalar(out=macc[:, :], in0=macc[:, :], scalar1=127.0,
                            scalar2=None, op0=ALU.mult)
            v.drain()
            v.tensor_tensor(out=q3v[:, :], in0=q3v[:, :], in1=macc[:, :], op=ALU.mult)
            v.tensor_scalar(out=os3w[:, :], in0=os3w[:, :], scalar1=INV_127,
                            scalar2=None, op0=ALU.mult)
            v.drain()
            sv.inc(v.tensor_scalar(out=os3w[:, :], in0=os3w[:, :], scalar1=mwb[:, 2:3],
                                   scalar2=None, op0=ALU.mult))
        V_Q3 = sv.v

        GUL_LD = {}
        EV6 = []
        for b4 in range(NBLK):
            @block.sync
            def _(s2, b4=b4):
                sg.wait(s2, SG_P5)
                if b4 >= 2:
                    sv.wait(s2, XQ3_Q[b4 - 2])
                GUL_LD[b4] = lx[b4 % 2].dma(s2.dma_start(out=gul[b4 % 2][:, :],
                                                         in_=gu_d[128 * b4:128 * (b4 + 1), :]))

            @block.vector
            def _(v, b4=b4):
                v.wait_ge(lx[b4 % 2].h, GUL_LD[b4])
                sv.wait(v, V_Q3)
                if b4 >= 2:
                    ste.wait(v, XQ3_T[b4 - 2])
                v.drain()
                v.tensor_scalar(out=rnd3[:, :], in0=gul[b4 % 2][:, :], scalar1=q3v[:, b4:b4 + 1],
                                scalar2=MAGIC, op0=ALU.mult, op1=ALU.add)
                v.drain()
                sv.inc(v.tensor_scalar(out=xq3b[:, :], in0=rnd3[:, :], scalar1=MAGIC,
                                       scalar2=None, op0=ALU.subtract))
            XQ3_Q[b4] = sv.v

            for c44g in range(11):
                q6i = len(EV6)
                pbs = 0 if q6i % 2 == 0 else 4

                @block.tensor
                def _(te, b4=b4, c44g=c44g, q6i=q6i, pbs=pbs):
                    sv.wait(te, XQ3_Q[b4])
                    if q6i >= 2:
                        sv.wait(te, EV6[q6i - 2])
                    last = None
                    for c4i in range(4):
                        c44 = 4 * c44g + c4i
                        last = te.matmul(pb[pbs + c4i][:, 0:128],
                                         xq3b[:, 128 * c44:128 * (c44 + 1)],
                                         identb[:, :], start=True, stop=True)
                    ste.inc(last)

                @block.vector
                def _(v, b4=b4, c44g=c44g, pbs=pbs):
                    ste.wait(v)
                    last = None
                    for c4i in range(4):
                        c44 = 4 * c44g + c4i
                        last = v.tensor_copy(
                            out=xq3t[:, TB * c44 + 128 * b4:TB * c44 + 128 * (b4 + 1)],
                            in_=pb[pbs + c4i][:, 0:128])
                    sv.inc(last)
                EV6.append(sv.v)
            XQ3_T[b4] = ste.v
        V_XQ3 = sv.v

        if debug:
            @block.gpsimd
            def _(g):
                sv.wait(g, V_XQ3)
                g.wait_ge(cc.h, CC_RS)
                stm.dma(g.dma_start(out=dbg["gu"][:, :], in_=gu_d[:, :]))
                stm.dma(g.dma_start(out=dbg["aor"][:, :], in_=aor_d[:, :]))


        stp4c = contextlib.ExitStack()
        w3t = [stp4c.enter_context(nc.sbuf_tensor(f"w3t{i}", [128, 1024], BF16)) for i in range(4)]
        fbuf2 = [stp4c.enter_context(nc.sbuf_tensor(f"fbuf{i}", [128, 1024], F32)) for i in range(2)]
        xrb2 = [stp4c.enter_context(nc.sbuf_tensor(f"xrb{i}", [128, 1024], F32)) for i in range(2)]
        aorb2 = [stp4c.enter_context(nc.sbuf_tensor(f"aorb{i}", [128, 1024], F32)) for i in range(2)]
        FIN_VF = {}
        FIN_ST = {}

        SCF_LAST = {}
        for th in range(2):
            for dh in range(2):
                grp = 2 * th + dh
                W3_LD = {}
                for c44 in range(44):
                    @block.sync
                    def _(s2, th=th, dh=dh, c44=c44):
                        if c44 == 0 and th == 0 and dh == 0:
                            sv.wait(s2, V_XQ3)
                        if c44 >= 4:
                            ste.wait(s2, MM3[(th, dh, c44 - 4)])
                        elif (th, dh) != (0, 0):
                            pth, pdh = (th, dh - 1) if dh > 0 else (th - 1, 1)
                            ste.wait(s2, MM3[(pth, pdh, 40 + c44)])
                        W3_LD[c44] = lnt6[c44 % 4].dma(s2.dma_start(
                            out=w3t[c44 % 4][:, :],
                            in_=w3q[128 * c44:128 * (c44 + 1), 1024 * dh:1024 * (dh + 1)]))

                    @block.tensor
                    def _(te, th=th, dh=dh, c44=c44, grp=grp):
                        te.wait_ge(lnt6[c44 % 4].h, W3_LD[c44])
                        if c44 == 0:
                            if grp == 0:
                                sv.wait(te)
                                ssc.wait(te)
                            else:
                                ssc.wait(te, SCF_LAST[grp - 1])
                        last = None
                        for b4 in range(4):
                            blk = 4 * th + b4
                            lx = xq3t[:, TB * c44 + 128 * blk:TB * c44 + 128 * (blk + 1)]
                            st_ = c44 == 0
                            sp = c44 == 43
                            last = te.matmul(pb[2 * b4][:, :], lx, w3t[c44 % 4][:, 0:512],
                                             start=st_, stop=sp)
                            last = te.matmul(pb[2 * b4 + 1][:, :], lx, w3t[c44 % 4][:, 512:1024],
                                             start=st_, stop=sp)
                        MM3[(th, dh, c44)] = ste.inc(last)

                def fin_prefetch(th, dh, b4):
                    blk = 4 * th + b4
                    gidx = 4 * (2 * th + dh) + b4

                    @block.gpsimd
                    def _(g, th=th, dh=dh, b4=b4, blk=blk, gidx=gidx):
                        if gidx == 0:
                            g.wait_ge(cc.h, CC_RS)
                        if gidx >= 2:
                            sv.wait(g, FIN_VF[gidx - 2])
                        ldf[gidx % 2].dma(g.dma_start(out=xrb2[gidx % 2][:, :],
                                                      in_=x_sl[128 * blk:128 * (blk + 1),
                                                               1024 * dh:1024 * (dh + 1)]))
                        FIN_LD[(th, dh, b4)] = ldf[gidx % 2].dma(
                            g.dma_start(out=aorb2[gidx % 2][:, :],
                                        in_=aor_d[128 * blk:128 * (blk + 1),
                                                  1024 * dh:1024 * (dh + 1)]))

                fin_prefetch(th, dh, 0)
                fin_prefetch(th, dh, 1)

                for b4 in range(4):
                    blk = 4 * th + b4
                    gidx = 4 * grp + b4
                    fbuf = fbuf2[gidx % 2]

                    @block.scalar
                    def _(sc, th=th, dh=dh, b4=b4, blk=blk, gidx=gidx, fbuf=fbuf):
                        ste.wait(sc, MM3[(th, dh, 43)])
                        if gidx >= 2:
                            sv.wait(sc, FIN_VF[gidx - 2])
                            sc.wait_ge(stx[gidx % 2].h, FIN_ST[gidx - 2])
                        last = None
                        for ph in range(2):
                            last = sc.mul(out=fbuf[:, 512 * ph:512 * (ph + 1)],
                                          in_=pb[2 * b4 + ph][:, :], mul=os3w[:, blk:blk + 1])
                        ssc.inc(last)
                    SC_F = ssc.v
                    if b4 == 3:
                        SCF_LAST[grp] = SC_F

                    @block.vector
                    def _(v, th=th, dh=dh, b4=b4, gidx=gidx, SC_F=SC_F, fbuf=fbuf):
                        ssc.wait(v, SC_F)
                        v.wait_ge(ldf[gidx % 2].h, FIN_LD[(th, dh, b4)])
                        v.drain()
                        v.tensor_tensor(out=fbuf[:, :], in0=fbuf[:, :],
                                        in1=aorb2[gidx % 2][:, :], op=ALU.add)
                        v.drain()
                        v.tensor_tensor(out=fbuf[:, :], in0=fbuf[:, :],
                                        in1=xrb2[gidx % 2][:, :], op=ALU.add)
                        v.drain()
                        sv.inc(v.tensor_tensor(out=fbuf[:, :], in0=fbuf[:, :],
                                               in1=borep[:, 1024 * dh:1024 * (dh + 1)], op=ALU.add))
                    FIN_VF[gidx] = sv.v

                    @block.gpsimd
                    def _(g, th=th, dh=dh, b4=b4, blk=blk, gidx=gidx, fbuf=fbuf):
                        sv.wait(g, FIN_VF[gidx])
                        FIN_ST[gidx] = stx[gidx % 2].dma(
                            g.dma_start(out=out_d[128 * blk:128 * (blk + 1),
                                                  1024 * dh:1024 * (dh + 1)],
                                        in_=fbuf[:, :]))
                    if b4 < 2:
                        fin_prefetch(th, dh, b4 + 2)

        stp4c.close()
        stp4b.close()
        stp4.close()
        @block.sync
        def _(s2):
            stm.wait(s2)
            cc.wait(s2)
            stx[0].wait(s2)
            stx[1].wait(s2)
    return nc, dbg


def build_p2(env):
    pass


G2N = (np.arange(S) % A) * 4 + (np.arange(S) // A)


def prep_inputs(inp):
    pieces, atiles, strips, stiles = build_attn_plan()
    NT = len(atiles)
    SW = max(t.shape[1] for t in stiles)
    atl_loc = np.zeros((NT * 128, 512), np.float32)
    atl_str = np.zeros((NT * 128, 512), np.float32)
    for i, (lt, stt) in enumerate(atiles):
        atl_loc[128 * i:128 * (i + 1)] = lt
        atl_str[128 * i:128 * (i + 1)] = stt
    strip_arr = np.zeros((len(stiles) * 128, SW), np.float32)
    for i, t in enumerate(stiles):
        strip_arr[128 * i:128 * (i + 1), :t.shape[1]] = t
    bf = ml_dtypes.bfloat16
    x2 = np.asarray(inp["L0_attn_hidden_states"])[0]
    ident = np.eye(128, dtype=np.float32)
    in_maps = []
    for c in range(NCORES):
        sl = G2N[1024 * c:1024 * (c + 1)]
        m = {
            "x_sl": np.ascontiguousarray(x2[sl]),
            "wq_sl": np.ascontiguousarray(inp["L0_attn_Wq"][256 * c:256 * (c + 1)]),
            "wk_sl": np.ascontiguousarray(inp["L0_attn_Wk"][256 * c:256 * (c + 1)]),
            "wv_sl": np.ascontiguousarray(inp["L0_attn_Wv"][256 * c:256 * (c + 1)]),
            "wo_sl": np.ascontiguousarray(np.asarray(inp["L0_attn_Wo"])[:, 256 * c:256 * (c + 1)].T),
            "bq_sl": np.ascontiguousarray(inp["L0_attn_bq"][256 * c:256 * (c + 1)]).reshape(256, 1),
            "bk_sl": np.ascontiguousarray(inp["L0_attn_bk"][256 * c:256 * (c + 1)]).reshape(256, 1),
            "bv_sl": np.ascontiguousarray(inp["L0_attn_bv"][256 * c:256 * (c + 1)]).reshape(1, 256),
            "bo_in": np.ascontiguousarray(inp["L0_attn_bo"]).reshape(1, D),
            "w1_sh": np.ascontiguousarray(np.asarray(inp["L0_ffn_w1"])[:, 256 * c:256 * (c + 1)].T),
            "w2_sh": np.ascontiguousarray(np.asarray(inp["L0_ffn_w2"])[:, 256 * c:256 * (c + 1)].T),
            "w3_sh": np.ascontiguousarray(np.asarray(inp["L0_ffn_w3"])[:, FSH * c:FSH * (c + 1)].T),
            "atl_loc": atl_loc.astype(bf),
            "atl_str": atl_str.astype(bf),
            "strip_in": strip_arr.astype(bf),
            "identb_in": ident.astype(bf),
            "identf_in": ident,
        }
        m = {k: np.ascontiguousarray(v) for k, v in m.items()}
        in_maps.append(m)
    return in_maps



_CACHE = {}


def kernel(**inputs):
    inp = {k: np.asarray(v) for k, v in inputs.items()}
    if "nc" not in _CACHE:
        _CACHE["nc"] = build_nc(run_to="p7", debug=False)[0]
    nc = _CACHE["nc"]
    in_maps = prep_inputs(inp)
    from concourse.bass_utils import run_bass_kernel_spmd
    res = run_bass_kernel_spmd(nc, in_maps, core_ids=list(range(NCORES)))
    outs = np.concatenate([res.results[c]["out"] for c in range(NCORES)], 0)
    out = np.empty((S, D), np.float32)
    out[G2N] = outs
    return out[None]


# version 13
# speedup vs baseline: 1.0949x; 1.0949x over previous
import contextlib
import math
import numpy as np
import ml_dtypes
import concourse.bass as bass
import concourse.mybir as mybir

F32 = mybir.dt.float32
BF16 = mybir.dt.bfloat16
AF = mybir.ActivationFunctionType
ALU = mybir.AluOpType
AX = mybir.AxisListType

NCORES = 8
S = 8192
D = 2048
A = S // 4
HD = 128
FF = 5632
FSH = FF // 8
W3SH = D // 8
TB = 1024
NBLK = TB // 128
NEG = float(np.float32(-1.0e38))
MAGIC = float(np.float32(1.5 * 2 ** 23))
SC_ATTN = float(np.float32(1.0 / math.sqrt(HD)))
EPS_LN = 1e-5
EPS_RMS = 1e-6
EPS_Q = 1e-5
INV_D = float(np.float32(1.0 / D))
INV_FF = float(np.float32(1.0 / FF))
INV_WN = float(np.float32(1.0 / (FF * D)))
INV_SD = float(np.float32(1.0 / (S * D)))
INV_127 = float(np.float32(1.0 / 127.0))
CK = 128
J = A // CK
P3_SKIP = set()


def _masks_natural(i, j):
    di = i - j
    loc = (np.abs(di) <= 64) | (i == 0) | (j == 0)
    st = (di % 4 == 0) | (i == 0) | (j == 0)
    return loc, st


def _a_tile(rho_k, kb):
    b = kb + np.arange(CK)
    loc_t = np.full((CK, 4 * CK), NEG, np.float32)
    str_t = np.full((CK, 4 * CK), NEG, np.float32)
    for r in range(4):
        a = kb + np.arange(CK)
        i = 4 * a[None, :] + r
        jn = 4 * b[:, None] + rho_k
        band = np.abs(i - jn) <= 64
        owned = band if kb % CK == 0 else band & ((a[None, :] // CK) != (b[:, None] // CK))
        owned = owned & (jn != 0)
        loc_m, str_m = _masks_natural(i, jn)
        loc_t[:, r * CK:(r + 1) * CK] = np.where(owned & loc_m, 0.0, NEG)
        str_t[:, r * CK:(r + 1) * CK] = np.where(owned & str_m, 0.0, NEG)
    return loc_t, str_t


def build_attn_plan():
    kbs = [CK * j for j in range(J)] + [CK * j + 64 for j in range(J - 1)]
    pieces, tiles, tids = [], [], {}
    for rho_k in range(4):
        for kb in kbs:
            lt, st = _a_tile(rho_k, kb)
            key = lt.tobytes() + st.tobytes()
            vid = tids.get(key)
            if vid is None:
                vid = len(tiles)
                tids[key] = vid
                tiles.append((lt, st))
            pieces.append((rho_k, kb, vid))
    strips, stiles, stids = [], [], {}
    for rho in range(4):
        for j in range(J):
            a_lo = max(0, CK * j - 16)
            a_hi = min(A, CK * j + CK + 16)
            t = np.zeros((CK, a_hi - a_lo), np.float32)
            bb = CK * j + np.arange(CK)
            aa = a_lo + np.arange(a_hi - a_lo)
            t[np.abs(aa[None, :] - bb[:, None]) <= 16] = NEG
            key = (t.shape[1], t.tobytes())
            svid = stids.get(key)
            if svid is None:
                svid = len(stiles)
                stids[key] = svid
                stiles.append(t)
            strips.append((rho, j, a_lo, a_hi, svid))
    return pieces, tiles, strips, stiles


class Sem:
    def __init__(self, handle):
        self.h = handle
        self.v = 0

    def inc(self, instr, n=1):
        instr.then_inc(self.h, n)
        self.v += n
        return self.v

    def dma(self, instr):
        return self.inc(instr, 16)

    def wait(self, eng, v=None):
        eng.wait_ge(self.h, self.v if v is None else v)


def build_nc(run_to="p1", debug=True):
    LVL = {"p1": 1, "p2": 2, "p3": 3, "p4": 4, "p5": 5, "p6": 6, "p7": 7}[run_to]
    nc = bass.Bass("TRN2", target_bir_lowering=False)
    pieces, atiles, strips, stiles = build_attn_plan()
    NT = len(atiles)
    stile_w = [t.shape[1] for t in stiles]
    SW = max(stile_w)
    NS = len(stiles)

    x_sl = nc.dram_tensor("x_sl", [TB, D], F32, kind="ExternalInput")
    wq_sl = nc.dram_tensor("wq_sl", [256, D], F32, kind="ExternalInput")
    wk_sl = nc.dram_tensor("wk_sl", [256, D], F32, kind="ExternalInput")
    wv_sl = nc.dram_tensor("wv_sl", [256, D], F32, kind="ExternalInput")
    wo_sl = nc.dram_tensor("wo_sl", [256, D], F32, kind="ExternalInput")
    bq_sl = nc.dram_tensor("bq_sl", [256, 1], F32, kind="ExternalInput")
    bk_sl = nc.dram_tensor("bk_sl", [256, 1], F32, kind="ExternalInput")
    bv_sl = nc.dram_tensor("bv_sl", [1, 256], F32, kind="ExternalInput")
    bo_in = nc.dram_tensor("bo_in", [1, D], F32, kind="ExternalInput")
    w1_sh = nc.dram_tensor("w1_sh", [256, FF], F32, kind="ExternalInput")
    w2_sh = nc.dram_tensor("w2_sh", [256, FF], F32, kind="ExternalInput")
    w3_sh = nc.dram_tensor("w3_sh", [FSH, D], F32, kind="ExternalInput")
    atl_loc = nc.dram_tensor("atl_loc", [NT * 128, 512], BF16, kind="ExternalInput")
    atl_str = nc.dram_tensor("atl_str", [NT * 128, 512], BF16, kind="ExternalInput")
    strip_in = nc.dram_tensor("strip_in", [NS * 128, SW], BF16, kind="ExternalInput")
    identb_in = nc.dram_tensor("identb_in", [128, 128], BF16, kind="ExternalInput")
    identf_in = nc.dram_tensor("identf_in", [128, 128], F32, kind="ExternalInput")

    out_d = nc.dram_tensor("out", [TB, D], F32, kind="ExternalOutput")

    xn_ag_in = nc.dram_tensor("xn_ag_in", [D, TB], BF16)
    xn_full = nc.dram_tensor("xn_full", [8 * D, TB], BF16, addr_space="Shared")
    stats_in = nc.dram_tensor("stats_in", [1, 8], F32)
    stats_out = nc.dram_tensor("stats_out", [1, 8], F32)
    wq_bf = nc.dram_tensor("wq_bf", [256, D], BF16)
    wk_bf = nc.dram_tensor("wk_bf", [256, D], BF16)
    wv_bf = nc.dram_tensor("wv_bf", [256, D], BF16)
    wo_bf = nc.dram_tensor("wo_bf", [256, D], BF16)
    w1q_in = nc.dram_tensor("w1q_in", [256, FF], BF16)
    w2q_in = nc.dram_tensor("w2q_in", [256, FF], BF16)
    w3q_in = nc.dram_tensor("w3q_in", [FSH, D], BF16)
    w1q = nc.dram_tensor("w1q", [D, FF], BF16, addr_space="Shared")
    w2q = nc.dram_tensor("w2q", [D, FF], BF16, addr_space="Shared")
    w3q = nc.dram_tensor("w3q", [FF, D], BF16, addr_space="Shared")
    xq1_d = nc.dram_tensor("xq1_d", [D, TB], BF16)
    gu_d = nc.dram_tensor("gu_d", [TB, FF], BF16)
    xq3_d = nc.dram_tensor("xq3_d", [TB, FF], BF16)
    v_raw = nc.dram_tensor("v_raw", [128, 2 * S], BF16)
    aop_d = nc.dram_tensor("aop_d", [S, D], F32)
    aor_d = nc.dram_tensor("aor_d", [TB, D], F32)

    dbg = {}
    if debug:
        dbg["gu"] = nc.dram_tensor("dbg_gu", [TB, FF], BF16, kind="ExternalOutput")
        for name, shape in [("xn", [TB, D]), ("xq1", [TB, D]),
                            ("stats", [1, 8]), ("gsc", [1, 8]),
                            ("qt", [128, 2 * S]), ("kt", [128, 2 * S]),
                            ("vsb", [128, 2 * S]), ("nrm", [1, 8]), ("mhat", [1, 4]),
                            ("ebA", [128, 2]), ("ebB", [128, 2]),
                            ("aoT", [128, 2 * S]), ("l", [1, 2 * S]),
                            ("rcol", [128, 128]), ("q3", [128, 4 * NBLK]),
                            ("aor", [TB, D])]:
            dbg[name] = nc.dram_tensor("dbg_" + name, shape, F32, kind="ExternalOutput")

    st = contextlib.ExitStack()
    with st:
        pb = [st.enter_context(nc.psum_tensor(f"pb{i}", [128, 512], F32)) for i in range(8)]
        ones_bf = st.enter_context(nc.sbuf_tensor("ones_bf", [128, 1], BF16))
        onesw = st.enter_context(nc.sbuf_tensor("onesw", [128, 16], BF16))
        onesr_f = st.enter_context(nc.sbuf_tensor("onesr_f", [1, 128], F32))
        one_f = st.enter_context(nc.sbuf_tensor("one_f", [1, 1], F32))
        one_bf = st.enter_context(nc.sbuf_tensor("one_bf", [1, 1], BF16))
        identf = st.enter_context(nc.sbuf_tensor("identf", [128, 128], F32))
        identb = st.enter_context(nc.sbuf_tensor("identb", [128, 128], BF16))
        q1v = st.enter_context(nc.sbuf_tensor("q1v", [128, NBLK], F32))
        os1v = st.enter_context(nc.sbuf_tensor("os1v", [128, NBLK], F32))
        stats_sb = st.enter_context(nc.sbuf_tensor("stats_sb", [1, 8], F32))
        cntacc = st.enter_context(nc.sbuf_tensor("cntacc", [128, 4], F32))
        swb = st.enter_context(nc.sbuf_tensor("swb", [128, 5], F32))
        gsc = st.enter_context(nc.sbuf_tensor("gsc", [1, 8], F32))

        sems = {}
        for name in ["ld", "ld2", "ld3", "ld4", "stm", "cc", "sv", "ssc", "ste", "sg",
                     "lx0", "lx1", "lw0", "lw1", "ln0", "ln1", "ln2", "ln3", "ln4", "ln5", "sld", "gld", "sx0", "sx1", "sq0", "sq1", "gq0", "gq1", "lid", "lat"]:
            sems[name] = Sem(st.enter_context(nc.semaphore(name)))
        ld, ld2, ld3, stm, cc, sv, ssc, ste, sg = (sems[k] for k in
            ["ld", "ld2", "ld3", "stm", "cc", "sv", "ssc", "ste", "sg"])
        lx = [sems["lx0"], sems["lx1"]]
        lw = [sems["lw0"], sems["lw1"]]
        lnt = [sems["ln0"], sems["ln1"], sems["ln2"]]
        lnt6 = [sems[f"ln{i}"] for i in range(6)]
        sld = sems["sld"]
        gld = sems["gld"]
        stx = [sems["sx0"], sems["sx1"]]
        swq = [sems["sq0"], sems["sq1"]]
        gwq = [sems["gq0"], sems["gq1"]]
        lid = sems["lid"]
        lat = sems["lat"]
        ldf = [sems["ld3"], sems["ld4"]]

        block = st.enter_context(nc.Block())

        @block.vector
        def _(v):
            v.memset(ones_bf[:, :], 1.0)
            v.memset(onesw[:, :], 1.0)
            v.memset(onesr_f[:, :], 1.0)
            v.memset(one_f[:, :], 1.0)
            v.memset(one_bf[:, :], 1.0)
            sv.inc(v.memset(cntacc[:, :], 0.0))
        V_CONST = sv.v

        @block.sync
        def _(s):
            lid.dma(s.dma_start(out=identf[:, :], in_=identf_in[:, :]))
            lid.dma(s.dma_start(out=identb[:, :], in_=identb_in[:, :]))

        @block.gpsimd
        def _(g):
            gld.dma(g.dma_start(out=wq_bf[:, :], in_=wq_sl[:, :]))
            gld.dma(g.dma_start(out=wk_bf[:, :], in_=wk_sl[:, :]))
            gld.dma(g.dma_start(out=wv_bf[:, :], in_=wv_sl[:, :]))
            gld.dma(g.dma_start(out=wo_bf[:, :], in_=wo_sl[:, :]))
        LD_WCAST = gld.v

        stp1 = contextlib.ExitStack()
        xnT_sb = stp1.enter_context(nc.sbuf_tensor("xnT_sb", [128, 16 * TB], BF16))
        xq1T_sb = stp1.enter_context(nc.sbuf_tensor("xq1T_sb", [128, 16 * TB], BF16))
        xblk = [stp1.enter_context(nc.sbuf_tensor(f"xblk{i}", [128, D], F32)) for i in range(2)]
        xnblk = [stp1.enter_context(nc.sbuf_tensor(f"xnblk{i}", [128, D], BF16)) for i in range(2)]
        sqblk = stp1.enter_context(nc.sbuf_tensor("sqblk", [128, D], BF16))
        gtblk = stp1.enter_context(nc.sbuf_tensor("gtblk", [128, D], BF16))
        rndblk = stp1.enter_context(nc.sbuf_tensor("rndblk", [128, D], F32))
        xq1blk = [stp1.enter_context(nc.sbuf_tensor(f"xq1blk{i}", [128, D], BF16)) for i in range(2)]
        s_sum = stp1.enter_context(nc.sbuf_tensor("s_sum", [128, 1], F32))
        s_sq = stp1.enter_context(nc.sbuf_tensor("s_sq", [128, 1], F32))
        s_mean = stp1.enter_context(nc.sbuf_tensor("s_mean", [128, 1], F32))
        s_var = stp1.enter_context(nc.sbuf_tensor("s_var", [128, 1], F32))
        s_rstd = stp1.enter_context(nc.sbuf_tensor("s_rstd", [128, 1], F32))
        s_nmr = stp1.enter_context(nc.sbuf_tensor("s_nmr", [128, 1], F32))
        s_t0 = stp1.enter_context(nc.sbuf_tensor("s_t0", [128, 1], F32))
        s_t1 = stp1.enter_context(nc.sbuf_tensor("s_t1", [128, 1], F32))
        s_r1 = stp1.enter_context(nc.sbuf_tensor("s_r1", [128, 1], F32))
        s_m1 = stp1.enter_context(nc.sbuf_tensor("s_m1", [128, 1], F32))
        s_s1 = stp1.enter_context(nc.sbuf_tensor("s_s1", [128, 1], F32))

        xload_done = [0] * NBLK
        xn_store_done = [0] * NBLK
        T_XNT_DONE = [0] * NBLK

        def load_xblk(b):
            @block.sync
            def _(s):
                if b >= 2:
                    ssc.wait(s, SC_XN_DONE[b - 2])
                xload_done[b] = lx[b % 2].dma(
                    s.dma_start(out=xblk[b % 2][:, :], in_=x_sl[128 * b:128 * (b + 1), :]))

        SC_XN_DONE = [0] * NBLK
        load_xblk(0)
        load_xblk(1)

        for b in range(NBLK):
            xb = xblk[b % 2]
            xnb = xnblk[b % 2]
            xqb = xq1blk[b % 2]

            @block.scalar
            def _(sc, b=b, xb=xb):
                sc.wait_ge(lx[b % 2].h, xload_done[b])
                ssc.inc(sc.activation(out=sqblk[:, :], in_=xb[:, :], func=AF.Square,
                                      accum_out=s_sq[:, :]))

            @block.vector
            def _(v, b=b, xb=xb):
                ssc.wait(v)
                v.reduce_sum(out=s_sum[:, :], in_=xb[:, :], axis=AX.X)
                v.drain()
                v.tensor_scalar(out=s_mean[:, :], in0=s_sum[:, :], scalar1=INV_D,
                                scalar2=None, op0=ALU.mult)
                v.tensor_scalar(out=s_var[:, :], in0=s_sq[:, :], scalar1=INV_D,
                                scalar2=None, op0=ALU.mult)
                v.drain()
                v.tensor_tensor(out=s_t0[:, :], in0=s_mean[:, :], in1=s_mean[:, :], op=ALU.mult)
                v.drain()
                v.tensor_tensor(out=s_var[:, :], in0=s_var[:, :], in1=s_t0[:, :], op=ALU.subtract)
                v.drain()
                v.tensor_scalar(out=s_t0[:, :], in0=s_var[:, :], scalar1=EPS_LN,
                                scalar2=None, op0=ALU.add)
                v.drain()
                sv.inc(v.reciprocal(out=s_t1[:, :], in_=s_t0[:, :]))

            @block.scalar
            def _(sc, b=b):
                sv.wait(sc)
                ssc.inc(sc.sqrt(out=s_rstd[:, :], in_=s_t1[:, :]))

            @block.vector
            def _(v, b=b):
                ssc.wait(v)
                v.tensor_tensor(out=s_nmr[:, :], in0=s_mean[:, :], in1=s_rstd[:, :], op=ALU.mult)
                v.drain()
                sv.inc(v.tensor_scalar(out=s_nmr[:, :], in0=s_nmr[:, :], scalar1=-1.0,
                                       scalar2=None, op0=ALU.mult))

            @block.scalar
            def _(sc, b=b, xb=xb, xnb=xnb):
                sv.wait(sc)
                if b >= 2:
                    stx[b % 2].wait(sc, xn_store_done[b - 2])
                    ste.wait(sc, T_XNT_DONE[b - 2])
                sc.activation(out=xnb[:, :], in_=xb[:, :], func=AF.Identity,
                              bias=s_nmr[:, :], scale=s_rstd[:, :])
                ssc.inc(sc.activation(out=sqblk[:, :], in_=xb[:, :], func=AF.Square,
                                      bias=s_nmr[:, :], scale=s_rstd[:, :],
                                      accum_out=s_sq[:, :]))
                SC_XN_DONE[b] = ssc.v

            @block.vector
            def _(v, b=b, xnb=xnb, xqb=xqb):
                ssc.wait(v)
                v.tensor_scalar(out=gtblk[:, :], in0=sqblk[:, :], scalar1=0.01,
                                scalar2=None, op0=ALU.is_gt)
                v.drain()
                v.reduce_sum(out=s_t0[:, :], in_=gtblk[:, :], axis=AX.X)
                v.drain()
                v.tensor_tensor(out=cntacc[:, 0:1], in0=cntacc[:, 0:1], in1=s_t0[:, :], op=ALU.add)
                v.drain()
                v.tensor_scalar(out=s_t0[:, :], in0=s_sq[:, :], scalar1=INV_D,
                                scalar2=EPS_RMS, op0=ALU.mult, op1=ALU.add)
                v.drain()
                sv.inc(v.reciprocal(out=s_t1[:, :], in_=s_t0[:, :]))

            @block.scalar
            def _(sc, b=b):
                sv.wait(sc)
                ssc.inc(sc.sqrt(out=s_r1[:, :], in_=s_t1[:, :]))

            @block.vector
            def _(v, b=b, xnb=xnb, xqb=xqb):
                ssc.wait(v)
                if b >= 2:
                    stx[b % 2].wait(v, xn_store_done[b - 2])
                    ste.wait(v, T_XNT_DONE[b - 2])
                v.reduce_max(out=s_t0[:, :], in_=xnb[:, :], axis=AX.X,
                             apply_absolute_value=True)
                v.drain()
                v.tensor_tensor(out=s_m1[:, :], in0=s_r1[:, :], in1=s_t0[:, :], op=ALU.mult)
                v.drain()
                v.tensor_scalar(out=s_m1[:, :], in0=s_m1[:, :], scalar1=EPS_Q,
                                scalar2=None, op0=ALU.max)
                v.drain()
                v.reciprocal(out=s_s1[:, :], in_=s_m1[:, :])
                v.drain()
                v.tensor_scalar(out=s_s1[:, :], in0=s_s1[:, :], scalar1=127.0,
                                scalar2=None, op0=ALU.mult)
                v.drain()
                v.tensor_tensor(out=q1v[:, b:b + 1], in0=s_r1[:, :], in1=s_s1[:, :], op=ALU.mult)
                v.tensor_scalar(out=os1v[:, b:b + 1], in0=s_m1[:, :], scalar1=INV_127,
                                scalar2=None, op0=ALU.mult)
                v.drain()
                v.tensor_scalar(out=rndblk[:, :], in0=xnb[:, :], scalar1=q1v[:, b:b + 1],
                                scalar2=MAGIC, op0=ALU.mult, op1=ALU.add)
                v.drain()
                sv.inc(v.tensor_scalar(out=xqb[:, :], in0=rndblk[:, :], scalar1=MAGIC,
                                       scalar2=None, op0=ALU.subtract))

            @block.gpsimd
            def _(g, b=b, xnb=xnb, xqb=xqb):
                sv.wait(g)
                if debug:
                    stx[b % 2].dma(g.dma_start(out=dbg["xn"][128 * b:128 * (b + 1), :], in_=xnb[:, :]))
                    stx[b % 2].dma(g.dma_start(out=dbg["xq1"][128 * b:128 * (b + 1), :], in_=xqb[:, :]))
                xn_store_done[b] = stx[b % 2].v

            for dch in range(16):
                tb_ = pb[6] if dch % 2 == 0 else pb[7]
                tb2_ = pb[7] if dch % 2 == 0 else pb[6]

                @block.tensor
                def _(te, b=b, dch=dch, xnb=xnb, xqb=xqb, tb_=tb_, tb2_=tb2_):
                    if b == 0 and dch == 0:
                        lid.wait(te)
                    ssc.wait(te)
                    sv.wait(te)
                    te.matmul(tb_[:, 0:128], xnb[:, 128 * dch:128 * (dch + 1)], identb[:, :],
                              start=True, stop=True)
                    ste.inc(te.matmul(tb2_[:, 128:256], xqb[:, 128 * dch:128 * (dch + 1)],
                                      identb[:, :], start=True, stop=True))

                @block.vector
                def _(v, b=b, dch=dch, tb_=tb_, tb2_=tb2_):
                    ste.wait(v)
                    co = 1024 * dch + 128 * b
                    v.tensor_copy(out=xnT_sb[:, co:co + 128], in_=tb_[:, 0:128])
                    sv.inc(v.tensor_copy(out=xq1T_sb[:, co:co + 128], in_=tb2_[:, 128:256]))
            T_XNT_DONE[b] = ste.v

            if b + 2 < NBLK:
                load_xblk(b + 2)


        @block.gpsimd
        def _(g):
            sv.wait(g)
            stm.dma(g.dma_start(
                out=xn_ag_in.ap().rearrange("(c p) t -> p c t", p=128), in_=xnT_sb[:, :]))
            stm.dma(g.dma_start(
                out=xq1_d.ap().rearrange("(c p) t -> p c t", p=128), in_=xq1T_sb[:, :]))
        STM_XNT = stm.v

        @block.gpsimd
        def _(g):
            g.wait_ge(stm.h, STM_XNT)
            cc.inc(g.collective_compute(
                "AllGather", ALU.bypass, replica_groups=[list(range(NCORES))],
                ins=[xn_ag_in.ap().opt()], outs=[xn_full.ap().opt()]))
        CC_XN = cc.v

        @block.sync
        def _(s2):
            sv.wait(s2)
            ssc.wait(s2)
            ste.wait(s2)
            stm.wait(s2)
            stx[0].wait(s2)
            stx[1].wait(s2)
            sg.inc(s2.nop())
        SG_LN_DONE = sg.v
        stp1.close()

        stp1b = contextlib.ExitStack()
        wch = [stp1b.enter_context(nc.sbuf_tensor(f"wch{i}", [128, 5632], F32)) for i in range(2)]
        s_t0b = stp1b.enter_context(nc.sbuf_tensor("s_t0b", [128, 1], F32))

        wabs_chunks = []
        for rb in range(2):
            wabs_chunks.append((w1_sh, 128 * rb, 128, 0, FF, 1))
        for rb in range(2):
            wabs_chunks.append((w2_sh, 128 * rb, 128, 0, FF, 2))
        for rb in range(6):
            r0 = 128 * rb
            wabs_chunks.append((w3_sh, r0, min(128, FSH - r0), 0, D, 3))
        WCH_LD = [0] * len(wabs_chunks)
        WCH_RD = [0] * len(wabs_chunks)
        for ci, (src, r0, nr, c0, ncol, col) in enumerate(wabs_chunks):
            @block.sync
            def _(s2, ci=ci, src=src, r0=r0, nr=nr, c0=c0, ncol=ncol):
                if ci == 0:
                    sg.wait(s2, SG_LN_DONE)
                if ci >= 2:
                    sv.wait(s2, WCH_RD[ci - 2])
                WCH_LD[ci] = swq[ci % 2].dma(s2.dma_start(
                    out=wch[ci % 2][0:nr, 0:ncol], in_=src[r0:r0 + nr, c0:c0 + ncol]))

            @block.vector
            def _(v, ci=ci, nr=nr, ncol=ncol, col=col):
                v.wait_ge(swq[ci % 2].h, WCH_LD[ci])
                v.drain()
                v.reduce_sum(out=s_t0b[0:nr, :], in_=wch[ci % 2][0:nr, 0:ncol], axis=AX.X,
                             apply_absolute_value=True)
                v.drain()
                WCH_RD[ci] = sv.inc(v.tensor_tensor(out=cntacc[0:nr, col:col + 1],
                                                    in0=cntacc[0:nr, col:col + 1],
                                                    in1=s_t0b[0:nr, :], op=ALU.add))

        @block.vector
        def _(v):
            v.drain()
            sv.inc(v.tensor_copy(out=s_t0b[:, :], in_=cntacc[:, 0:1]))
        V_WABS = sv.v

        @block.tensor
        def _(te):
            sv.wait(te)
            lid.wait(te)
            for c2 in range(4):
                ste.inc(te.matmul(pb[0][0:1, 128 * c2:128 * (c2 + 1)],
                                  cntacc[:, c2:c2 + 1], identf[:, :], start=True, stop=True))

        @block.vector
        def _(v):
            ste.wait(v)
            for c2 in range(4):
                v.reduce_sum(out=stats_sb[0:1, c2:c2 + 1], in_=pb[0][0:1, 128 * c2:128 * (c2 + 1)],
                             axis=AX.X)
            sv.inc(v.memset(stats_sb[0:1, 4:8], 0.0))
        V_STATS = sv.v

        @block.gpsimd
        def _(g):
            sv.wait(g, V_STATS)
            stm.dma(g.dma_start(out=stats_in[:, :], in_=stats_sb[:, :]))
            stm.wait(g)
            cc.inc(g.collective_compute(
                "AllReduce", ALU.add, replica_groups=[list(range(NCORES))],
                ins=[stats_in.ap().opt()], outs=[stats_out.ap().opt()]))
        CC_STATS = cc.v

        if LVL >= 2:
            build_p2(locals())



        @block.sync
        def _(s2):
            sv.wait(s2)
            ssc.wait(s2)
            stm.wait(s2)
            swq[0].wait(s2)
            swq[1].wait(s2)
            stx[0].wait(s2)
            stx[1].wait(s2)
            sg.inc(s2.nop())
        SG_P1DONE = sg.v
        stp1b.close()

        stp2 = contextlib.ExitStack()
        stp2a = contextlib.ExitStack()
        qt = stp2.enter_context(nc.sbuf_tensor("qt", [128, 2 * S], BF16))
        kt = stp2.enter_context(nc.sbuf_tensor("kt", [128, 2 * S], BF16))
        vsb = stp2.enter_context(nc.sbuf_tensor("vsb", [128, 2 * S], BF16))
        nrm = stp2.enter_context(nc.sbuf_tensor("nrm", [1, 8], F32))
        nrmw = stp2.enter_context(nc.sbuf_tensor("nrmw", [16, 8], F32))
        nrmt = stp2.enter_context(nc.sbuf_tensor("nrmt", [4, 17], F32))
        mhat = stp2.enter_context(nc.sbuf_tensor("mhat", [1, 4], F32))
        ebA = stp2.enter_context(nc.sbuf_tensor("ebA", [128, 2], F32))
        ebB = stp2.enter_context(nc.sbuf_tensor("ebB", [128, 2], F32))
        ebA0 = stp2.enter_context(nc.sbuf_tensor("ebA0", [128, 2], F32))
        ebB0 = stp2.enter_context(nc.sbuf_tensor("ebB0", [128, 2], F32))
        atb = stp2.enter_context(nc.sbuf_tensor("atb", [128, NT * 512], BF16))
        ssb = stp2.enter_context(nc.sbuf_tensor("ssb", [128, NS * SW], BF16))
        qin2 = [stp2.enter_context(nc.sbuf_tensor("qin0", [128, 1024], F32))]
        wqt_all = stp2a.enter_context(nc.sbuf_tensor("wqt_all", [128, 16 * 256], BF16))
        wkt_all = stp2a.enter_context(nc.sbuf_tensor("wkt_all", [128, 16 * 256], BF16))
        wvt_all = stp2a.enter_context(nc.sbuf_tensor("wvt_all", [128, 16 * 256], BF16))
        xnt = [stp2a.enter_context(nc.sbuf_tensor(f"xnt{i}", [128, 512], BF16)) for i in range(6)]
        bqsb = stp2a.enter_context(nc.sbuf_tensor("bqsb", [128, 2], F32))
        bksb = stp2a.enter_context(nc.sbuf_tensor("bksb", [128, 2], F32))
        bvsb = stp2a.enter_context(nc.sbuf_tensor("bvsb", [1, 256], F32))
        bvrep = stp2a.enter_context(nc.sbuf_tensor("bvrep", [128, 256], F32))
        sqsc = [stp2a.enter_context(nc.sbuf_tensor(f"sqsc{i}", [128, 512], BF16)) for i in range(4)]
        atscr = stp2a.enter_context(nc.sbuf_tensor("atscr", [128, 2 * 512], BF16))
        whalf2 = [stp2a.enter_context(nc.sbuf_tensor(f"whalf{i}", [128, 2 * D], BF16)) for i in range(2)]

        WH = {}
        WT_EVAC = {}
        WT_TE = {}
        EVW = []
        for wi, (wsrc, wdst) in enumerate(((wq_bf, wqt_all), (wk_bf, wkt_all), (wv_bf, wvt_all))):
            @block.sync
            def _(s, wi=wi, wsrc=wsrc):
                if wi == 0:
                    sg.wait(s, SG_P1DONE)
                    s.wait_ge(gld.h, LD_WCAST)
                elif wi >= 2:
                    ste.wait(s, WT_TE[wi - 2])
                lw[wi % 2].dma(s.dma_start(out=whalf2[wi % 2][:, 0:D], in_=wsrc[0:128, :]))
                WH[wi] = lw[wi % 2].dma(s.dma_start(out=whalf2[wi % 2][:, D:2 * D],
                                                    in_=wsrc[128:256, :]))

            for h2 in range(2):
                for grp in range(4):
                    it = len(EVW)
                    pbs = 0 if it % 2 == 0 else 4

                    @block.tensor
                    def _(te, wi=wi, h2=h2, grp=grp, it=it, pbs=pbs):
                        te.wait_ge(lw[wi % 2].h, WH[wi])
                        if it < 2:
                            lid.wait(te)
                            sv.wait(te)
                        else:
                            sv.wait(te, EVW[it - 2])
                        last = None
                        for j4 in range(4):
                            dch = 4 * grp + j4
                            last = te.matmul(pb[pbs + j4][:, 0:128],
                                             whalf2[wi % 2][:, D * h2 + 128 * dch:D * h2 + 128 * (dch + 1)],
                                             identb[:, :], start=True, stop=True)
                        ste.inc(last)
                    if h2 == 1 and grp == 3:
                        WT_TE[wi] = ste.v

                    @block.vector
                    def _(v, wi=wi, h2=h2, grp=grp, pbs=pbs, wdst=wdst):
                        ste.wait(v)
                        lastv = None
                        for j4 in range(4):
                            dch = 4 * grp + j4
                            lastv = v.tensor_copy(
                                out=wdst[:, 256 * dch + 128 * h2:256 * dch + 128 * (h2 + 1)],
                                in_=pb[pbs + j4][:, 0:128])
                        sv.inc(lastv)
                    EVW.append(sv.v)
            WT_EVAC[wi] = sv.v

        @block.sync
        def _(s):
            sv.wait(s, WT_EVAC[2])
            ld.dma(s.dma_start(out=bqsb[:, 0:1], in_=bq_sl[0:128, :]))
            ld.dma(s.dma_start(out=bqsb[:, 1:2], in_=bq_sl[128:256, :]))
            ld.dma(s.dma_start(out=bksb[:, 0:1], in_=bk_sl[0:128, :]))
            ld.dma(s.dma_start(out=bksb[:, 1:2], in_=bk_sl[128:256, :]))
            ld.dma(s.dma_start(out=bvsb[:, :], in_=bv_sl[:, :]))
            for i2 in range(NS):
                ld.dma(s.dma_start(out=ssb[:, SW * i2:SW * (i2 + 1)],
                                   in_=strip_in[128 * i2:128 * (i2 + 1), :]))
        LD_WT = ld.v

        @block.tensor
        def _(te):
            ld.wait(te, LD_WT)
            sv.wait(te)
            ste.inc(te.matmul(pb[1][:, 0:256], onesr_f[:, :], bvsb[:, :], start=True, stop=True))
        TE_BV = ste.v

        @block.vector
        def _(v):
            ste.wait(v, TE_BV)
            sv.inc(v.tensor_copy(out=bvrep[:, :], in_=pb[1][:, 0:256]))
        V_BVREP = sv.v


        @block.vector
        def _(v):
            v.memset(mhat[:, :], 0.0)
            v.memset(nrm[:, :], 0.0)
            sv.inc(v.memset(nrmw[:, :], 0.0))
        V_NRM0 = sv.v

        xnt_loaded = [0] * (16 * 16)
        evac_done = [0] * 16
        vevac_done = [0] * 16
        qkevac_done = [0] * 16
        nrm_mm = [0] * 64
        nrm_rd = [0] * 64

        def load_xnt(idx):
            bb, dch = divmod(idx, 16)

            @block.scalar
            def _(sc, idx=idx, bb=bb, dch=dch):
                if idx >= 6:
                    ste.wait(sc, mm_done[idx - 6])
                if idx == 0:
                    sc.wait_ge(cc.h, CC_XN)
                    sg.wait(sc, SG_P1DONE)
                c2 = bb // 2
                toff = 512 * (bb % 2)
                xnt_loaded[idx] = lnt6[idx % 6].dma(
                    sc.dma_start(out=xnt[idx % 6][:, :],
                                 in_=xn_full[D * c2 + 128 * dch:D * c2 + 128 * (dch + 1),
                                             toff:toff + 512]))

        mm_done = [0] * (16 * 16)
        for _pre in range(6):
            load_xnt(_pre)
        for bb in range(16):
            for dch in range(16):
                idx = 16 * bb + dch

                @block.tensor
                def _(te, bb=bb, dch=dch, idx=idx):
                    te.wait_ge(lnt6[idx % 6].h, xnt_loaded[idx])
                    if dch == 0:
                        ld.wait(te, LD_WT)
                        if bb == 0:
                            sv.wait(te, V_BVREP)
                        if bb >= 1:
                            ssc.wait(te, qkevac_done[bb - 1])
                            sv.wait(te, vevac_done[bb - 1])
                    xt = xnt[idx % 6]
                    st_ = dch == 0
                    sp = dch == 15
                    te.matmul(pb[0][:, :], wqt_all[:, 256 * dch:256 * dch + 128], xt[:, :],
                              start=st_, stop=sp)
                    te.matmul(pb[1][:, :], wqt_all[:, 256 * dch + 128:256 * dch + 256], xt[:, :],
                              start=st_, stop=sp)
                    te.matmul(pb[2][:, :], wkt_all[:, 256 * dch:256 * dch + 128], xt[:, :],
                              start=st_, stop=sp)
                    te.matmul(pb[3][:, :], wkt_all[:, 256 * dch + 128:256 * dch + 256], xt[:, :],
                              start=st_, stop=sp)
                    lastmm = None
                    for sb2 in range(4):
                        lastmm = te.matmul(pb[4 + sb2][:, 0:256], xt[:, 128 * sb2:128 * (sb2 + 1)],
                                           wvt_all[:, 256 * dch:256 * (dch + 1)], start=st_, stop=sp)
                    mm_done[idx] = ste.inc(lastmm)
                if idx + 6 < 256:
                    load_xnt(idx + 6)

            @block.scalar
            def _(sc, bb=bb):
                ste.wait(sc, mm_done[16 * bb + 15])
                for hh, pbq in ((0, pb[0]), (1, pb[1])):
                    sc.activation(out=qt[:, S * hh + 512 * bb:S * hh + 512 * (bb + 1)],
                                  in_=pbq[:, :], func=AF.Identity,
                                  bias=bqsb[:, hh:hh + 1], scale=1.0)
                last = None
                for hh, pbk in ((0, pb[2]), (1, pb[3])):
                    last = sc.activation(out=kt[:, S * hh + 512 * bb:S * hh + 512 * (bb + 1)],
                                         in_=pbk[:, :], func=AF.Identity,
                                         bias=bksb[:, hh:hh + 1], scale=1.0)
                ssc.inc(last)
            qkevac_done[bb] = ssc.v

            @block.vector
            def _(v, bb=bb):
                ssc.wait(v)
                sv.wait(v, V_BVREP)
                last = None
                for sb2 in range(4):
                    cj = 4 * bb + sb2
                    for hh in range(2):
                        last = v.tensor_tensor(
                            out=vsb[:, S * hh + 128 * cj:S * hh + 128 * (cj + 1)],
                            in0=pb[4 + sb2][:, 128 * hh:128 * (hh + 1)],
                            in1=bvrep[:, 128 * hh:128 * (hh + 1)], op=ALU.add)
                sv.inc(last)
                vevac_done[bb] = sv.v
            evac_done[bb] = sv.v
        V_PROJ = sv.v

        @block.sync
        def _(s):
            s.wait_ge(cc.h, CC_STATS)
            sld.dma(s.dma_start(out=stats_sb[:, :], in_=stats_out[:, :]))
        LD_STATS = sld.v

        @block.vector
        def _(v):
            v.wait_ge(sld.h, LD_STATS)
            v.tensor_scalar(out=gsc[0:1, 0:1], in0=stats_sb[0:1, 0:1], scalar1=INV_SD,
                            scalar2=0.5, op0=ALU.mult, op1=ALU.is_gt)
            v.drain()
            v.tensor_scalar(out=gsc[0:1, 1:2], in0=gsc[0:1, 0:1], scalar1=NEG,
                            scalar2=None, op0=ALU.mult)
            for c2 in range(3):
                v.tensor_scalar(out=gsc[0:1, 2 + c2:3 + c2], in0=stats_sb[0:1, 1 + c2:2 + c2],
                                scalar1=INV_WN, scalar2=EPS_Q, op0=ALU.mult, op1=ALU.max)
            v.drain()
            sv.inc(v.reciprocal(out=gsc[0:1, 5:8], in_=gsc[0:1, 2:5]))
        V_G = sv.v

        if debug:
            @block.gpsimd
            def _(g):
                sv.wait(g, V_G)
                stm.dma(g.dma_start(out=dbg["stats"][:, :], in_=stats_sb[:, :]))
                stm.dma(g.dma_start(out=dbg["gsc"][:, :], in_=gsc[:, :]))



        for ti, src in enumerate((qt, qt, kt, kt)):
            hh = ti % 2
            for c16 in range(16):
                ni = 16 * ti + c16
                nrm_pb = pb[4 + ni % 4]

                @block.scalar
                def _(sc, ti=ti, hh=hh, c16=c16, ni=ni, src=src):
                    if ni == 0:
                        ssc.wait(sc)
                        sv.wait(sc, V_PROJ)
                    if ni >= 4:
                        ste.wait(sc, nrm_mm[ni - 4])
                    ssc.inc(sc.activation(out=sqsc[ni % 4][:, :],
                                          in_=src[:, S * hh + 512 * c16:S * hh + 512 * (c16 + 1)],
                                          func=AF.Square, bias=0.0, scale=1.0))

                @block.tensor
                def _(te, ni=ni, nrm_pb=nrm_pb):
                    ssc.wait(te)
                    if ni == 0:
                        sv.wait(te, V_PROJ)
                    if ni >= 4:
                        sv.wait(te, nrm_rd[ni - 4])
                    nrm_mm[ni] = ste.inc(te.matmul(nrm_pb[0:16, :], onesw[:, :],
                                                   sqsc[ni % 4][:, :], start=True, stop=True))

                @block.vector
                def _(v, ti=ti, ni=ni, nrm_pb=nrm_pb):
                    ste.wait(v, nrm_mm[ni])
                    if ni == 0:
                        sv.wait(v, V_NRM0)
                    v.drain()
                    v.reduce_max(out=nrmw[:, 4:5], in_=nrm_pb[0:16, :], axis=AX.X)
                    v.drain()
                    nrm_rd[ni] = sv.inc(v.tensor_tensor(out=nrmw[:, ti:ti + 1],
                                                        in0=nrmw[:, ti:ti + 1],
                                                        in1=nrmw[:, 4:5], op=ALU.max))
        V_PROJ = sv.v

        @block.tensor
        def _(te):
            sv.wait(te, V_PROJ)
            ste.inc(te.matmul(pb[0][0:4, 0:16], nrmw[:, 0:4], identf[0:16, 0:16],
                              start=True, stop=True))
        TE_NRMT = ste.v

        @block.vector
        def _(v):
            ste.wait(v, TE_NRMT)
            v.tensor_copy(out=nrmt[:, 0:16], in_=pb[0][0:4, 0:16])
            v.drain()
            sv.inc(v.reduce_max(out=nrmt[:, 16:17], in_=nrmt[:, 0:16], axis=AX.X))
        V_NRMF = sv.v

        @block.tensor
        def _(te):
            sv.wait(te, V_NRMF)
            ste.inc(te.matmul(pb[1][0:1, 0:4], nrmt[:, 16:17], identf[0:4, 0:4],
                              start=True, stop=True))
        TE_NRMR = ste.v

        @block.vector
        def _(v):
            ste.wait(v, TE_NRMR)
            sv.inc(v.tensor_copy(out=nrm[0:1, 0:4], in_=pb[1][0:1, 0:4]))
        V_PROJ = sv.v

        @block.tensor
        def _(te):
            sv.wait(te, V_PROJ)
            for i in range(3):
                ste.inc(te.matmul(pb[1][:, i:i + 1], onesr_f[:, :], gsc[0:1, 5 + i:6 + i],
                                  start=True, stop=True))
            ste.inc(te.matmul(pb[1][:, 3:4], onesr_f[:, :], gsc[0:1, 0:1],
                              start=True, stop=True))
        TE_SWB = ste.v

        @block.vector
        def _(v):
            ste.wait(v, TE_SWB)
            sv.inc(v.tensor_copy(out=swb[:, 0:4], in_=pb[1][:, 0:4]))
        V_SWB = sv.v

        @block.scalar
        def _(sc):
            sv.wait(sc, V_PROJ)
            ssc.inc(sc.sqrt(out=nrm[0:1, 0:4], in_=nrm[0:1, 0:4]))

        @block.vector
        def _(v):
            ssc.wait(v)
            v.tensor_tensor(out=mhat[0:1, 0:2], in0=nrm[0:1, 0:2], in1=nrm[0:1, 2:4], op=ALU.mult)
            v.drain()
            sv.inc(v.tensor_scalar(out=mhat[0:1, 0:2], in0=mhat[0:1, 0:2],
                                   scalar1=-SC_ATTN, scalar2=None, op0=ALU.mult))
        V_MHAT = sv.v

        @block.tensor
        def _(te):
            sv.wait(te, V_MHAT)
            ste.inc(te.matmul(pb[6][:, 0:2], onesr_f[:, :], mhat[0:1, 0:2], start=True, stop=True))
            ste.inc(te.matmul(pb[6][:, 2:3], onesr_f[:, :], gsc[0:1, 1:2], start=True, stop=True))
        TE_EB = ste.v

        @block.vector
        def _(v):
            ste.wait(v, TE_EB)
            v.tensor_copy(out=ebA[:, :], in_=pb[6][:, 0:2])
            v.drain()
            for hh in range(2):
                v.tensor_tensor(out=ebB[:, hh:hh + 1], in0=ebA[:, hh:hh + 1],
                                in1=pb[6][:, 2:3], op=ALU.add)
            v.drain()
            v.tensor_copy(out=ebA0[:, :], in_=ebA[:, :])
            v.tensor_copy(out=ebB0[:, :], in_=ebB[:, :])
            v.drain()
            v.tensor_scalar(out=ebA0[0:1, :], in0=ebA0[0:1, :], scalar1=NEG,
                            scalar2=None, op0=ALU.add)
            sv.inc(v.tensor_scalar(out=ebB0[0:1, :], in0=ebB0[0:1, :], scalar1=NEG,
                                   scalar2=None, op0=ALU.add))
        V_EB = sv.v

        @block.gpsimd
        def _(g):
            sv.wait(g, V_PROJ)
            stm.dma(g.dma_start(out=v_raw[:, :], in_=vsb[:, :]))
        STM_VRAW = stm.v

        @block.vector
        def _(v):
            sv.wait(v, V_SWB)
            sv.inc(v.tensor_scalar(out=swb[:, 4:5], in0=swb[:, 3:4], scalar1=-1.0,
                                   scalar2=1.0, op0=ALU.mult, op1=ALU.add))
        V_GB = sv.v

        atb_done = 0
        for i2 in range(NT):
            @block.sync
            def _(s, i2=i2):
                if i2 >= 1:
                    sv.wait(s, atb_done)
                lat.dma(s.dma_start(out=atscr[:, 0:512], in_=atl_loc[128 * i2:128 * (i2 + 1), :]))
                lat.dma(s.dma_start(out=atscr[:, 512:1024], in_=atl_str[128 * i2:128 * (i2 + 1), :]))
            ld_at = lat.v

            @block.vector
            def _(v, i2=i2, ld_at=ld_at):
                v.wait_ge(lat.h, ld_at)
                sv.wait(v, V_GB)
                v.tensor_scalar(out=atscr[:, 0:512], in0=atscr[:, 0:512],
                                scalar1=swb[:, 3:4], scalar2=None, op0=ALU.mult)
                v.tensor_scalar(out=atb[:, 512 * i2:512 * (i2 + 1)], in0=atscr[:, 512:1024],
                                scalar1=swb[:, 4:5], scalar2=None, op0=ALU.mult)
                v.drain()
                sv.inc(v.tensor_tensor(out=atb[:, 512 * i2:512 * (i2 + 1)],
                                       in0=atb[:, 512 * i2:512 * (i2 + 1)],
                                       in1=atscr[:, 0:512], op=ALU.add))
            atb_done = sv.v
        V_ATB = sv.v

        q_chunks = []
        for widx, (wsrc, wdst, nrows, ncols) in enumerate(
                ((w1_sh, w1q_in, 256, FF), (w2_sh, w2q_in, 256, FF), (w3_sh, w3q_in, FSH, D))):
            for r0 in range(0, nrows, 128):
                nr = min(128, nrows - r0)
                for c0 in range(0, ncols, 512):
                    wd = min(512, ncols - c0)
                    q_chunks.append((widx, wsrc, wdst, r0, nr, c0, wd))
        QLD = [0] * len(q_chunks)
        QCP = [0] * len(q_chunks)
        QST = [0] * len(q_chunks)
        for ci, (widx, wsrc, wdst, r0, nr, c0, wd) in enumerate(q_chunks):
            @block.sync
            def _(s2, ci=ci, widx=widx, wsrc=wsrc, wdst=wdst, r0=r0, nr=nr, c0=c0, wd=wd):
                if ci >= 2:
                    s2.wait_ge(gwq[ci % 2].h, QST[ci - 2])
                QLD[ci] = swq[ci % 2].dma(s2.dma_start(
                    out=qin2[0][0:nr, 512 * (ci % 2):512 * (ci % 2) + wd],
                    in_=wsrc[r0:r0 + nr, c0:c0 + wd]))

            @block.vector
            def _(v, ci=ci, widx=widx, nr=nr, wd=wd):
                if ci == 0:
                    sv.wait(v, V_SWB)
                v.wait_ge(swq[ci % 2].h, QLD[ci])
                co = 512 * (ci % 2)
                v.tensor_scalar(out=qin2[0][0:nr, co:co + wd], in0=qin2[0][0:nr, co:co + wd],
                                scalar1=swb[0:nr, widx:widx + 1], scalar2=MAGIC,
                                op0=ALU.mult, op1=ALU.add)
                v.drain()
                v.tensor_scalar(out=qin2[0][0:nr, co:co + wd], in0=qin2[0][0:nr, co:co + wd],
                                scalar1=MAGIC, scalar2=None, op0=ALU.subtract)
                v.drain()
                QCP[ci] = sv.inc(v.tensor_scalar(out=qin2[0][0:nr, co:co + wd],
                                                 in0=qin2[0][0:nr, co:co + wd],
                                                 scalar1=1.0, scalar2=-1.0,
                                                 op0=ALU.min, op1=ALU.max))

            @block.gpsimd
            def _(g, ci=ci, wdst=wdst, r0=r0, nr=nr, c0=c0, wd=wd):
                sv.wait(g, QCP[ci])
                QST[ci] = gwq[ci % 2].dma(g.dma_start(
                    out=wdst[r0:r0 + nr, c0:c0 + wd],
                    in_=qin2[0][0:nr, 512 * (ci % 2):512 * (ci % 2) + wd]))

        @block.gpsimd
        def _(g):
            gwq[0].wait(g)
            gwq[1].wait(g)
            cc.inc(g.collective_compute(
                "AllGather", ALU.bypass, replica_groups=[list(range(NCORES))],
                ins=[w1q_in.ap().opt()], outs=[w1q.ap().opt()]))
            cc.inc(g.collective_compute(
                "AllGather", ALU.bypass, replica_groups=[list(range(NCORES))],
                ins=[w2q_in.ap().opt()], outs=[w2q.ap().opt()]))
            cc.inc(g.collective_compute(
                "AllGather", ALU.bypass, replica_groups=[list(range(NCORES))],
                ins=[w3q_in.ap().opt()], outs=[w3q.ap().opt()]))
        CC_W = cc.v


        stp2a.close()
        aoT = stp2.enter_context(nc.sbuf_tensor("aoT", [128, 2 * S], BF16))
        rcol = stp2.enter_context(nc.sbuf_tensor("rcol", [128, 128], F32))
        stp3 = contextlib.ExitStack()
        l_sb = stp3.enter_context(nc.sbuf_tensor("l_sb", [1, S], F32))
        scr8k = stp3.enter_context(nc.sbuf_tensor("scr8k", [128, S], BF16))
        PT = [stp3.enter_context(nc.sbuf_tensor(f"PT{i}", [128, 512], BF16)) for i in range(4)]
        PTA = [stp3.enter_context(nc.sbuf_tensor(f"PTA{i}", [128, 512], BF16)) for i in range(4)]
        Pq0 = stp3.enter_context(nc.sbuf_tensor("Pq0", [128, 16], BF16))
        q0row = stp3.enter_context(nc.sbuf_tensor("q0row", [1, 128], F32))

        strip_by = {}
        for (rho_, j_, alo_, ahi_, svid_) in strips:
            strip_by[(rho_, j_)] = (svid_, alo_, ahi_)
        piece_by = {}
        for (rhk_, kb_, vid_) in pieces:
            piece_by[(rhk_, kb_)] = vid_
        kbs_all = sorted({kb_ for (_, kb_, _) in pieces})

        @block.vector
        def _(v):
            sv.wait(v, V_EB)
            ssc.wait(v)
            ste.wait(v)
            sv.inc(v.memset(aoT[:, :], 0.0))
        V_P3INIT = sv.v

        for h in range(2):
            hS = h * S

            @block.vector
            def _(v, h=h):
                if h == 0:
                    sv.wait(v, V_P3INIT)
                else:
                    ssc.wait(v)
                    ste.wait(v)
                    stm.wait(v)
                sv.inc(v.memset(l_sb[:, :], 0.0))
            V_LINIT = sv.v
            dp_mm = [0] * 16
            dp_ex = [0] * 16
            for grp in range(16):
                @block.tensor
                def _(te, h=h, grp=grp):
                    if grp <= 3:
                        if h == 0:
                            sv.wait(te, V_P3INIT)
                        else:
                            sv.wait(te)
                            ssc.wait(te)
                    else:
                        ssc.wait(te, dp_ex[grp - 4])
                    dp_mm[grp] = ste.inc(te.matmul(pb[4 + grp % 4][0:1, 0:512], kt[:, hS:hS + 1],
                                                   qt[:, hS + 512 * grp:hS + 512 * (grp + 1)],
                                                   start=True, stop=True))

                @block.scalar
                def _(sc, h=h, grp=grp):
                    ste.wait(sc, dp_mm[grp])
                    dp_ex[grp] = ssc.inc(sc.activation(
                        out=scr8k[0:1, 512 * grp:512 * (grp + 1)],
                        in_=pb[4 + grp % 4][0:1, 0:512], func=AF.Exp,
                        bias=ebA[0:1, h:h + 1], scale=SC_ATTN))
            D_PREP = ssc.v

            exp_all = {}
            bev_sc = {}
            bev_v = {}
            for rho in range(4):
                rA = rho * A
                for g2 in range(4):
                    nC = rho * 4 + g2
                    po = pb[3]
                    pl = pb[4]
                    qc0 = hS + rA + 512 * g2

                    @block.tensor
                    def _(te, h=h, rho=rho, g2=g2, nC=nC, po=po, pl=pl, qc0=qc0):
                        if nC == 0:
                            ssc.wait(te, D_PREP)
                            sv.wait(te)
                        else:
                            ssc.wait(te, bev_sc[nC - 1])
                            sv.wait(te, bev_v[nC - 1])
                        te.matmul(po[:, :], vsb[0:1, hS:hS + 128],
                                  scr8k[0:1, rA + 512 * g2:rA + 512 * (g2 + 1)],
                                  start=True, stop=False)
                        ste.inc(te.matmul(pl[0:1, 0:512], one_bf[:, :],
                                          scr8k[0:1, rA + 512 * g2:rA + 512 * (g2 + 1)],
                                          start=True, stop=False))

                    exp_ct = {}

                    def emit_qk(j):
                        jj = 16 * nC + j
                        bk = (0, 1, 2, 5)[jj % 4]
                        pt = PT[jj % 4]

                        @block.tensor
                        def _(te, h=h, rho=rho, g2=g2, nC=nC, j=j, jj=jj, bk=bk, qc0=qc0):
                            if jj >= 4:
                                ssc.wait(te, exp_all[jj - 4])
                            elif bk == 5:
                                ssc.wait(te, D_PREP)
                            mmq = te.matmul(pb[bk][:, :],
                                            kt[:, hS + rA + 128 * j:hS + rA + 128 * (j + 1)],
                                            qt[:, qc0:qc0 + 512], start=True, stop=True)
                            if rho > 0 and g2 == 0 and "q0" not in P3_SKIP:
                                mmq = te.matmul(pb[6][:, j:j + 1],
                                                kt[:, hS + rA + 128 * j:hS + rA + 128 * (j + 1)],
                                                qt[:, hS:hS + 1], start=True, stop=True)
                            ste.inc(mmq)

                        sb_info = strip_by.get((rho, j))
                        ovl = None
                        if sb_info:
                            svid_, alo_, ahi_ = sb_info
                            lo = max(alo_, 512 * g2)
                            hi = min(ahi_, 512 * (g2 + 1))
                            if lo < hi:
                                ovl = (svid_, alo_, lo, hi)
                        if ovl is not None:
                            @block.vector
                            def _(v, ovl=ovl, g2=g2, bk=bk):
                                svid_, alo_, lo, hi = ovl
                                ste.wait(v)
                                sv.inc(v.tensor_tensor(
                                    out=pb[bk][:, lo - 512 * g2:hi - 512 * g2],
                                    in0=pb[bk][:, lo - 512 * g2:hi - 512 * g2],
                                    in1=ssb[:, SW * svid_ + lo - alo_:SW * svid_ + hi - alo_],
                                    op=ALU.add))
                        strip_v = sv.v if ovl is not None else None

                        @block.scalar
                        def _(sc, h=h, rho=rho, g2=g2, j=j, bk=bk, pt=pt, strip_v=strip_v):
                            if strip_v is not None:
                                sv.wait(sc, strip_v)
                            else:
                                ste.wait(sc)
                            eb_main = ebB0 if (rho == 0 and j == 0) else ebB
                            if rho == 0 and g2 == 0:
                                eb_c0 = ebA0 if j == 0 else ebA
                                sc.activation(out=pt[:, 1:512], in_=pb[bk][:, 1:512],
                                              func=AF.Exp, bias=eb_main[:, h:h + 1], scale=SC_ATTN)
                                ssc.inc(sc.activation(out=pt[:, 0:1], in_=pb[bk][:, 0:1],
                                                      func=AF.Exp, bias=eb_c0[:, h:h + 1],
                                                      scale=SC_ATTN))
                            else:
                                ssc.inc(sc.activation(out=pt[:, :], in_=pb[bk][:, :],
                                                      func=AF.Exp, bias=eb_main[:, h:h + 1],
                                                      scale=SC_ATTN))
                        exp_ct[j] = ssc.v
                        exp_all[jj] = ssc.v

                    def emit_av(j):
                        pt = PT[(16 * nC + j) % 4]

                        @block.tensor
                        def _(te, h=h, rho=rho, g2=g2, j=j, pt=pt, po=po, pl=pl):
                            ssc.wait(te, exp_ct[j])
                            te.matmul(po[:, :], vsb[:, hS + 128 * (16 * rho + j):hS + 128 * (16 * rho + j + 1)],
                                      pt[:, :], start=False, stop=(j == 15))
                            ste.inc(te.matmul(pl[0:1, 0:512], ones_bf[:, :], pt[:, :],
                                              start=False, stop=(j == 15)))

                    for j in range(16):
                        if j >= 4:
                            emit_av(j - 4)
                        emit_qk(j)
                    for jf in (12, 13, 14, 15):
                        emit_av(jf)

                    @block.scalar
                    def _(sc, h=h, rho=rho, g2=g2, pl=pl, qc0=qc0):
                        ste.wait(sc)
                        ssc.inc(sc.copy(out=l_sb[0:1, qc0 - hS:qc0 - hS + 512], in_=pl[0:1, 0:512]))
                    bev_sc[nC] = ssc.v

                    @block.vector
                    def _(v, h=h, rho=rho, g2=g2, po=po, qc0=qc0):
                        ste.wait(v)
                        sv.inc(v.tensor_copy(out=aoT[:, qc0:qc0 + 512], in_=po[:, :]))
                    bev_v[nC] = sv.v

                if rho > 0 and "q0" not in P3_SKIP:
                    @block.vector
                    def _(v, h=h, rho=rho):
                        ste.wait(v)
                        sv.inc(v.tensor_scalar(out=pb[6][0:16, 0:1], in0=pb[6][0:16, 0:1],
                                               scalar1=NEG, scalar2=None, op0=ALU.add))
                    V_Q0NEG = sv.v

                    @block.scalar
                    def _(sc, h=h, rho=rho):
                        sv.wait(sc, V_Q0NEG)
                        ssc.inc(sc.activation(out=Pq0[:, :], in_=pb[6][:, 0:16], func=AF.Exp,
                                              bias=ebA[:, h:h + 1], scale=SC_ATTN))
                    SC_PQ0 = ssc.v

                    @block.tensor
                    def _(te, h=h, rho=rho):
                        ssc.wait(te, SC_PQ0)
                        for j in range(16):
                            te.matmul(pb[6][0:1, 256:384],
                                      Pq0[:, j:j + 1],
                                      vsb[:, hS + 128 * (16 * rho + j):hS + 128 * (16 * rho + j + 1)],
                                      start=(j == 0), stop=(j == 15))
                        ste.inc(te.matmul(pb[6][0:1, 384:400], ones_bf[:, :], Pq0[:, :],
                                          start=True, stop=True))
                    TE_Q0 = ste.v

                    @block.scalar
                    def _(sc, h=h, rho=rho):
                        ste.wait(sc, TE_Q0)
                        ssc.inc(sc.copy(out=q0row[0:1, :], in_=pb[6][0:1, 256:384]))
                    SC_Q0R = ssc.v

                    @block.vector
                    def _(v, h=h, rho=rho):
                        ste.wait(v, TE_Q0)
                        v.reduce_sum(out=nrm[0:1, 7:8], in_=pb[6][0:1, 384:400], axis=AX.X)
                        v.drain()
                        sv.inc(v.tensor_tensor(out=l_sb[0:1, 0:1], in0=l_sb[0:1, 0:1],
                                               in1=nrm[0:1, 7:8], op=ALU.add))

                    @block.tensor
                    def _(te, h=h, rho=rho):
                        ssc.wait(te, SC_Q0R)
                        sv.wait(te)
                        ste.inc(te.matmul(pb[6][:, 400:401], q0row[:, :], one_f[:, :],
                                          start=True, stop=True))
                    TE_Q0T = ste.v

                    @block.vector
                    def _(v, h=h, rho=rho):
                        ste.wait(v, TE_Q0T)
                        sv.inc(v.tensor_tensor(out=aoT[:, hS:hS + 1], in0=aoT[:, hS:hS + 1],
                                               in1=pb[6][:, 400:401], op=ALU.add))

            @block.sync
            def _(s2, h=h, hS=hS):
                ssc.wait(s2)
                ste.wait(s2)
                s2.wait_ge(stm.h, STM_VRAW)
                lnt[h].dma(s2.dma_start(out=scr8k[0:64, :], in_=v_raw[64:128, hS:hS + S]))
                lnt[h].dma(s2.dma_start(out=scr8k[64:128, 0:S - 128],
                                        in_=v_raw[0:64, hS + 128:hS + S]))
            LNT_VSH = lnt[h].v

            kbs_use = [] if "A" in P3_SKIP else [kb for kb in kbs_all
                                                 if not (kb % 128 != 0 and "Ashift" in P3_SKIP)]
            apieces = [(pi, kb, rhk) for pi, kb in enumerate(kbs_use) for rhk in range(4)]
            a_exp = {}
            a_av = {}
            a_rmw = {}

            def a_qk(ip):
                pi, kb, rhk = apieces[ip]
                vid = piece_by[(rhk, kb)]

                @block.tensor
                def _(te, h=h, kb=kb, rhk=rhk, vid=vid, ip=ip):
                    if ip >= 3:
                        ssc.wait(te, a_exp[ip - 3])
                    else:
                        sv.wait(te)
                    if ip == 0:
                        sv.wait(te, V_ATB)
                        te.wait_ge(lnt[h].h, LNT_VSH)
                    qsl = qt.ap().rearrange("p (hh r a) -> p hh r a", hh=2, r=4)[:, h, :, kb:kb + 128]
                    te.matmul(pb[ip % 3][:, :], identb[:, :], atb[:, 512 * vid:512 * (vid + 1)],
                              start=True, stop=False)
                    ste.inc(te.matmul(pb[ip % 3][:, :],
                                      kt[:, hS + rhk * A + kb:hS + rhk * A + kb + 128],
                                      qsl, start=False, stop=True))

                @block.scalar
                def _(sc, h=h, ip=ip):
                    ste.wait(sc)
                    ssc.inc(sc.activation(out=PTA[ip % 4][:, :], in_=pb[ip % 3][:, :],
                                          func=AF.Exp, bias=ebA[:, h:h + 1], scale=SC_ATTN))
                a_exp[ip] = ssc.v

            def a_avf(ip):
                pi, kb, rhk = apieces[ip]
                pav = pb[3] if pi % 2 == 0 else pb[5]
                pla = pb[4] if pi % 2 == 0 else pb[7]

                @block.tensor
                def _(te, h=h, kb=kb, rhk=rhk, pi=pi, ip=ip, pav=pav, pla=pla):
                    ssc.wait(te, a_exp[ip])
                    if rhk == 0 and pi >= 2:
                        sv.wait(te, a_rmw[pi - 2])
                    cj = 16 * rhk + kb // 128
                    st_ = rhk == 0
                    sp = rhk == 3
                    if kb % 128 == 0:
                        te.matmul(pav[:, :], vsb[:, hS + 128 * cj:hS + 128 * (cj + 1)],
                                  PTA[ip % 4][:, :], start=st_, stop=sp)
                    else:
                        te.matmul(pav[:, :], scr8k[:, 128 * cj:128 * (cj + 1)],
                                  PTA[ip % 4][:, :], start=st_, stop=sp)
                    ste.inc(te.matmul(pla[0:1, 0:512], ones_bf[:, :], PTA[ip % 4][:, :],
                                      start=st_, stop=sp))
                a_av[ip] = ste.v

            def a_rmwf(ip):
                pi, kb, rhk = apieces[ip]
                if rhk != 3:
                    return
                pav = pb[3] if pi % 2 == 0 else pb[5]
                pla = pb[4] if pi % 2 == 0 else pb[7]

                @block.vector
                def _(v, h=h, kb=kb, pi=pi, ip=ip, pav=pav, pla=pla):
                    ste.wait(v, a_av[ip])
                    v.drain()
                    ao3 = aoT.ap().rearrange("p (hh r a) -> p hh r a", hh=2, r=4)[:, h, :, kb:kb + 128]
                    v.tensor_tensor(out=ao3, in0=ao3, in1=pav[:, :], op=ALU.add)
                    l3 = l_sb.ap().rearrange("o (r a) -> o r a", r=4)[:, :, kb:kb + 128]
                    pl3 = pla[0:1, 0:512].rearrange("o (r a) -> o r a", r=4)
                    sv.inc(v.tensor_tensor(out=l3, in0=l3, in1=pl3, op=ALU.add))
                a_rmw[pi] = sv.v

            for ip in range(len(apieces)):
                a_qk(ip)
                if ip >= 3:
                    a_avf(ip - 3)
                    a_rmwf(ip - 3)
            for ip in range(max(0, len(apieces) - 3), len(apieces)):
                a_avf(ip)
                a_rmwf(ip)
            if debug:
                @block.gpsimd
                def _(g, h=h, hS=hS):
                    sv.wait(g)
                    stm.dma(g.dma_start(out=dbg["l"][0:1, hS:hS + S], in_=l_sb[0:1, :]))

            for bq2 in range(4):
                @block.tensor
                def _(te, h=h, bq2=bq2):
                    sv.wait(te)
                    ssc.wait(te)
                    lastm = None
                    for b2 in range(16):
                        b3 = 16 * bq2 + b2
                        lastm = te.matmul(pb[6][:, 416 + b2:417 + b2],
                                          l_sb[0:1, 128 * b3:128 * (b3 + 1)],
                                          one_f[:, :], start=True, stop=True)
                    ste.inc(lastm)

                @block.scalar
                def _(sc, h=h, bq2=bq2):
                    ste.wait(sc)
                    ssc.inc(sc.copy(out=rcol[:, 64 * h + 16 * bq2:64 * h + 16 * (bq2 + 1)],
                                    in_=pb[6][:, 416:432]))
            RC_H = ssc.v

        @block.vector
        def _(v):
            ssc.wait(v)
            v.drain()
            sv.inc(v.reciprocal(out=rcol[:, :], in_=rcol[:, :]))
        V_ATTN = sv.v

        stp3.close()
        OP_EV = [0] * 64
        OP_V1 = [0] * 64
        woT_sb = stp2.enter_context(nc.sbuf_tensor("woT_sb", [128, 2 * D], BF16))
        obuf2 = [stp2.enter_context(nc.sbuf_tensor(f"obuf{i}", [128, D], F32)) for i in range(2)]

        @block.sync
        def _(s2):
            s2.wait_ge(gld.h, LD_WCAST)
            sv.wait(s2, V_ATTN)
            ssc.wait(s2)
            ste.wait(s2)
            stm.wait(s2)
            for q_ in lnt6:
                q_.wait(s2)
            ld.dma(s2.dma_start(out=woT_sb[:, 0:D], in_=wo_bf[0:128, :]))
            ld.dma(s2.dma_start(out=woT_sb[:, D:2 * D], in_=wo_bf[128:256, :]))
        LD_WOT = ld.v

        OP_SC0 = [0] * 64
        OP_TE1 = [0] * 64
        for bb in range(64):
            obuf = obuf2[bb % 2]

            @block.tensor
            def _(te, bb=bb):
                if bb == 0:
                    sv.wait(te, V_ATTN)
                    te.wait_ge(ld.h, LD_WOT)
                    stm.wait(te)
                else:
                    ssc.wait(te, OP_SC0[bb - 1])
                last = None
                for g3 in range(4):
                    last = te.matmul(pb[g3][:, :], aoT[:, 128 * bb:128 * (bb + 1)],
                                     woT_sb[:, 512 * g3:512 * (g3 + 1)], start=True, stop=True)
                ste.inc(last)
            TE_OP0 = ste.v

            @block.tensor
            def _(te, bb=bb):
                if bb >= 1:
                    sv.wait(te, OP_V1[bb - 1])
                last = None
                for g3 in range(4):
                    last = te.matmul(pb[4 + g3][:, :], aoT[:, S + 128 * bb:S + 128 * (bb + 1)],
                                     woT_sb[:, D + 512 * g3:D + 512 * (g3 + 1)],
                                     start=True, stop=True)
                ste.inc(last)
            OP_TE1[bb] = ste.v

            @block.scalar
            def _(sc, bb=bb, TE_OP0=TE_OP0, obuf=obuf):
                ste.wait(sc, TE_OP0)
                if bb >= 2:
                    sc.wait_ge(stx[bb % 2].h, OP_EV[bb - 2])
                last = None
                for g3 in range(4):
                    last = sc.mul(out=obuf[:, 512 * g3:512 * (g3 + 1)], in_=pb[g3][:, :],
                                  mul=rcol[:, bb:bb + 1])
                ssc.inc(last)
            OP_SC0[bb] = ssc.v

            @block.vector
            def _(v, bb=bb, obuf=obuf):
                ste.wait(v, OP_TE1[bb])
                ssc.wait(v, OP_SC0[bb])
                last = None
                for g3 in range(4):
                    last = v.scalar_tensor_tensor(
                        out=obuf[:, 512 * g3:512 * (g3 + 1)], in0=pb[4 + g3][:, :],
                        scalar=rcol[:, 64 + bb:64 + bb + 1],
                        in1=obuf[:, 512 * g3:512 * (g3 + 1)], op0=ALU.mult, op1=ALU.add)
                sv.inc(last)
            OP_V1[bb] = sv.v

            @block.gpsimd
            def _(g, bb=bb, obuf=obuf):
                sv.wait(g, OP_V1[bb])
                OP_EV[bb] = stx[bb % 2].dma(g.dma_start(out=aop_d[128 * bb:128 * (bb + 1), :],
                                                        in_=obuf[:, :]))

        @block.gpsimd
        def _(g):
            stm.wait(g)
            stx[0].wait(g)
            stx[1].wait(g)
            cc.inc(g.collective_compute(
                "ReduceScatter", ALU.add, replica_groups=[list(range(NCORES))],
                ins=[aop_d.ap().opt()], outs=[aor_d.ap().opt()]))
        CC_RS = cc.v

        if debug:
            @block.gpsimd
            def _(g):
                sv.wait(g)
                ssc.wait(g)
                ste.wait(g)
                stm.dma(g.dma_start(out=dbg["qt"][:, :], in_=qt[:, :]))
                stm.dma(g.dma_start(out=dbg["kt"][:, :], in_=kt[:, :]))
                stm.dma(g.dma_start(out=dbg["vsb"][:, :], in_=vsb[:, :]))
                stm.dma(g.dma_start(out=dbg["nrm"][:, :], in_=nrm[:, :]))
                stm.dma(g.dma_start(out=dbg["mhat"][:, :], in_=mhat[:, :]))
                stm.dma(g.dma_start(out=dbg["ebA"][:, :], in_=ebA[:, :]))
                stm.dma(g.dma_start(out=dbg["ebB"][:, :], in_=ebB[:, :]))
                if "aoT" in dbg:
                    stm.dma(g.dma_start(out=dbg["aoT"][:, :], in_=aoT[:, :]))
                    stm.dma(g.dma_start(out=dbg["rcol"][:, :], in_=rcol[:, :]))

        XQ3_Q = [0] * NBLK
        XQ3_T = [0] * NBLK
        MM3 = {}
        FIN_LD = {}
        stp2.close()
        stp4 = contextlib.ExitStack()
        macc = stp4.enter_context(nc.sbuf_tensor("macc", [128, NBLK], F32))
        sacc2 = stp4.enter_context(nc.sbuf_tensor("sacc2", [128, NBLK], F32))
        mwb = stp4.enter_context(nc.sbuf_tensor("mwb", [128, 3], F32))
        os1w = stp4.enter_context(nc.sbuf_tensor("os1w", [128, NBLK], F32))
        q3v = stp4.enter_context(nc.sbuf_tensor("q3v", [128, NBLK], F32))
        os3w = stp4.enter_context(nc.sbuf_tensor("os3w", [128, NBLK], F32))
        borep = stp4.enter_context(nc.sbuf_tensor("borep", [128, D], F32))
        bosb = stp4.enter_context(nc.sbuf_tensor("bosb", [1, D], F32))
        stmp = [stp4.enter_context(nc.sbuf_tensor(f"stmp{i}", [128, 1], F32)) for i in range(2)]
        stmp2 = [stp4.enter_context(nc.sbuf_tensor(f"stmp2_{i}", [128, 1], F32)) for i in range(2)]

        @block.sync
        def _(s2):
            stm.wait(s2)
            stx[0].wait(s2)
            stx[1].wait(s2)
            sv.wait(s2)
            ssc.wait(s2)
            ste.wait(s2)
            sg.inc(s2.nop())
        SG_P4 = sg.v

        @block.sync
        def _(s2):
            sg.wait(s2, SG_P4)
            ld.dma(s2.dma_start(out=bosb[:, :], in_=bo_in[:, :]))
        LD_BO = ld.v

        @block.tensor
        def _(te):
            te.wait_ge(ld.h, LD_BO)
            sg.wait(te, SG_P4)
            for g3 in range(4):
                te.matmul(pb[g3][:, :], onesr_f[:, :], bosb[0:1, 512 * g3:512 * (g3 + 1)],
                          start=True, stop=True)
            for c3 in range(3):
                te.matmul(pb[4][:, c3:c3 + 1], onesr_f[:, :], gsc[0:1, 2 + c3:3 + c3],
                          start=True, stop=True)
            ste.inc(te.matmul(pb[4][:, 3:4], onesr_f[:, :], gsc[0:1, 2:3],
                              start=True, stop=True))
        TE_BOREP = ste.v

        @block.vector
        def _(v):
            ste.wait(v, TE_BOREP)
            sg.wait(v, SG_P4)
            for g3 in range(4):
                v.tensor_copy(out=borep[:, 512 * g3:512 * (g3 + 1)], in_=pb[g3][:, :])
            v.tensor_copy(out=mwb[:, :], in_=pb[4][:, 0:3])
            v.memset(macc[:, :], 0.0)
            v.memset(sacc2[:, :], 0.0)
            v.drain()
            sv.inc(v.tensor_scalar(out=os1w[:, :], in0=os1v[:, :], scalar1=mwb[:, 0:1],
                                   scalar2=None, op0=ALU.mult))
        V_P5INIT = sv.v

        stp4a = contextlib.ExitStack()
        xq1t = stp4a.enter_context(nc.sbuf_tensor("xq1t", [128, 16 * TB], BF16))
        wqrt2 = [stp4a.enter_context(nc.sbuf_tensor(f"wqrt{i}", [128, 16 * 1408], BF16)) for i in range(2)]
        gq = stp4a.enter_context(nc.sbuf_tensor("gq", [128, NBLK * 1408], BF16))
        ubuf = [stp4a.enter_context(nc.sbuf_tensor(f"ubuf{i}", [128, 1408], BF16)) for i in range(2)]
        ubuf2 = [stp4a.enter_context(nc.sbuf_tensor(f"ubuf2_{i}", [128, 1408], BF16)) for i in range(2)]
        gub = [stp4a.enter_context(nc.sbuf_tensor(f"gub{i}", [128, 1408], BF16)) for i in range(2)]

        @block.sync
        def _(s2):
            sg.wait(s2, SG_P4)
            sv.wait(s2, V_P5INIT)
            ld.dma(s2.dma_start(out=xq1t[:, :],
                                in_=xq1_d.ap().rearrange("(c p) t -> p c t", p=128)))
        LD_XQ1T = ld.v

        GUB_ST = {}
        V_GUS = {}
        GU_SQ = []

        SACC_V = []

        def emit_gusq(item):
            q4g, b4g, V_GUg = item
            gs = len(SACC_V)

            @block.scalar
            def _(sc, q4=q4g, b4=b4g, V_GU=V_GUg, gs=gs):
                sv.wait(sc, V_GU)
                if gs >= 2:
                    sv.wait(sc, SACC_V[gs - 2])
                ssc.inc(sc.activation(out=ubuf2[gs % 2][:, :], in_=gub[b4 % 2][:, :],
                                      func=AF.Square, accum_out=stmp2[gs % 2][:, :]))
            SC_SQ3 = ssc.v

            @block.vector
            def _(v, q4=q4g, b4=b4g, SC_SQ3=SC_SQ3, gs=gs):
                ssc.wait(v, SC_SQ3)
                v.drain()
                sv.inc(v.tensor_tensor(out=sacc2[:, b4:b4 + 1], in0=sacc2[:, b4:b4 + 1],
                                       in1=stmp2[gs % 2][:, :], op=ALU.add))
            SACC_V.append(sv.v)
        P5_EVS = []
        P5_TE_PASS = {}
        for q4 in range(4):
            for wpass, wsrc in ((0, w1q), (1, w2q)):
                pl = 2 * q4 + wpass
                wqrt = wqrt2[pl % 2]

                @block.sync
                def _(s2, q4=q4, wpass=wpass, wsrc=wsrc, pl=pl, wqrt=wqrt):
                    if q4 == 0 and wpass == 0:
                        s2.wait_ge(ld.h, LD_XQ1T)
                    if pl <= 1:
                        s2.wait_ge(cc.h, CC_W)
                    if pl >= 2:
                        ste.wait(s2, P5_TE_PASS[pl - 2])
                    lnt[pl % 2].dma(s2.dma_start(
                        out=wqrt[:, :],
                        in_=wsrc.ap().rearrange("(c p) f -> p c f", p=128)[:, :, 1408 * q4:1408 * (q4 + 1)]))
                LD_WQRT = lnt[pl % 2].v

                for b4 in range(NBLK):
                    pbase = 0 if b4 % 2 == 0 else 3
                    L = len(P5_EVS)

                    @block.tensor
                    def _(te, q4=q4, wpass=wpass, b4=b4, pbase=pbase, LD_WQRT=LD_WQRT,
                          pl=pl, L=L, wqrt=wqrt):
                        te.wait_ge(lnt[pl % 2].h, LD_WQRT)
                        if L >= 2:
                            ssc.wait(te, P5_EVS[L - 2])
                        last = None
                        for dch in range(16):
                            st_ = dch == 0
                            sp = dch == 15
                            lx = xq1t[:, TB * dch + 128 * b4:TB * dch + 128 * (b4 + 1)]
                            last = te.matmul(pb[pbase][:, :], lx,
                                             wqrt[:, 1408 * dch:1408 * dch + 512], start=st_, stop=sp)
                            last = te.matmul(pb[pbase + 1][:, :], lx,
                                             wqrt[:, 1408 * dch + 512:1408 * dch + 1024], start=st_, stop=sp)
                            last = te.matmul(pb[pbase + 2][:, 0:384], lx,
                                             wqrt[:, 1408 * dch + 1024:1408 * dch + 1408], start=st_, stop=sp)
                        ste.inc(last)
                    TE_BL = ste.v
                    if b4 == NBLK - 1:
                        P5_TE_PASS[pl] = TE_BL

                    if wpass == 0:
                        @block.scalar
                        def _(sc, q4=q4, b4=b4, pbase=pbase, TE_BL=TE_BL):
                            ste.wait(sc, TE_BL)
                            prevgu = V_GUS.get((q4 - 1, b4))
                            if prevgu:
                                sv.wait(sc, prevgu)
                            last = None
                            for g3, w3_ in ((0, 512), (1, 512), (2, 384)):
                                last = sc.activation(
                                    out=gq[:, 1408 * b4 + 512 * g3:1408 * b4 + 512 * g3 + w3_],
                                    in_=pb[pbase + g3][:, 0:w3_], func=AF.Silu,
                                    bias=0.0, scale=os1w[:, b4:b4 + 1])
                            ssc.inc(last)
                        P5_EVS.append(ssc.v)
                    else:
                        @block.scalar
                        def _(sc, q4=q4, b4=b4, pbase=pbase, TE_BL=TE_BL):
                            ste.wait(sc, TE_BL)
                            prevgu = V_GUS.get((q4, b4 - 2)) or V_GUS.get((q4 - 1, b4 + 6))
                            if prevgu:
                                sv.wait(sc, prevgu)
                            last = None
                            for g3, w3_ in ((0, 512), (1, 512), (2, 384)):
                                last = sc.mul(out=ubuf[b4 % 2][:, 512 * g3:512 * g3 + w3_],
                                              in_=pb[pbase + g3][:, 0:w3_],
                                              mul=os1w[:, b4:b4 + 1])
                            ssc.inc(last)
                        SC_UP = ssc.v
                        P5_EVS.append(SC_UP)

                        @block.vector
                        def _(v, q4=q4, b4=b4, SC_UP=SC_UP):
                            ssc.wait(v)
                            prevst = GUB_ST.get((q4, b4 - 2)) or GUB_ST.get((q4 - 1, b4 + 6))
                            if prevst:
                                v.wait_ge(stx[b4 % 2].h, prevst)
                            v.drain()
                            v.tensor_tensor(out=gub[b4 % 2][:, :],
                                            in0=gq[:, 1408 * b4:1408 * (b4 + 1)],
                                            in1=ubuf[b4 % 2][:, :], op=ALU.mult)
                            v.drain()
                            v.reduce_max(out=stmp[b4 % 2][:, :], in_=gub[b4 % 2][:, :], axis=AX.X,
                                         apply_absolute_value=True)
                            v.drain()
                            sv.inc(v.tensor_tensor(out=macc[:, b4:b4 + 1], in0=macc[:, b4:b4 + 1],
                                                   in1=stmp[b4 % 2][:, :], op=ALU.max))
                        V_GU = sv.v
                        V_GUS[(q4, b4)] = V_GU
                        GU_SQ.append((q4, b4, V_GU))

                        @block.gpsimd
                        def _(g, q4=q4, b4=b4, V_GU=V_GU):
                            sv.wait(g, V_GU)
                            GUB_ST[(q4, b4)] = stx[b4 % 2].dma(
                                g.dma_start(out=gu_d[128 * b4:128 * (b4 + 1),
                                                     1408 * q4:1408 * (q4 + 1)],
                                            in_=gub[b4 % 2][:, :]))

                        if len(GU_SQ) >= 2:
                            emit_gusq(GU_SQ[-2])
        if GU_SQ:
            emit_gusq(GU_SQ[-1])
        V_P5 = sv.v

        @block.sync
        def _(s2):
            sv.wait(s2)
            ssc.wait(s2)
            ste.wait(s2)
            stx[0].wait(s2)
            stx[1].wait(s2)
            sg.inc(s2.nop())
        SG_P5 = sg.v
        stp4a.close()

        stp4b = contextlib.ExitStack()
        xq3t = stp4b.enter_context(nc.sbuf_tensor("xq3t", [128, 44 * TB], BF16))
        gul = [stp4b.enter_context(nc.sbuf_tensor(f"gul{i}", [128, FF], BF16)) for i in range(2)]
        xq3b = stp4b.enter_context(nc.sbuf_tensor("xq3b", [128, FF], BF16))
        rnd3 = stp4b.enter_context(nc.sbuf_tensor("rnd3", [128, FF], F32))

        @block.vector
        def _(v):
            sg.wait(v, SG_P5)
            v.tensor_scalar(out=q3v[:, :], in0=sacc2[:, :], scalar1=INV_FF,
                            scalar2=EPS_RMS, op0=ALU.mult, op1=ALU.add)
            v.drain()
            sv.inc(v.reciprocal(out=q3v[:, :], in_=q3v[:, :]))
        V_Q3A = sv.v

        @block.scalar
        def _(sc):
            sv.wait(sc, V_Q3A)
            ssc.inc(sc.sqrt(out=q3v[:, :], in_=q3v[:, :]))
        SC_R3 = ssc.v

        @block.vector
        def _(v):
            ssc.wait(v, SC_R3)
            v.tensor_tensor(out=os3w[:, :], in0=q3v[:, :], in1=macc[:, :], op=ALU.mult)
            v.drain()
            v.tensor_scalar(out=os3w[:, :], in0=os3w[:, :], scalar1=EPS_Q,
                            scalar2=None, op0=ALU.max)
            v.drain()
            v.reciprocal(out=macc[:, :], in_=os3w[:, :])
            v.drain()
            v.tensor_scalar(out=macc[:, :], in0=macc[:, :], scalar1=127.0,
                            scalar2=None, op0=ALU.mult)
            v.drain()
            v.tensor_tensor(out=q3v[:, :], in0=q3v[:, :], in1=macc[:, :], op=ALU.mult)
            v.tensor_scalar(out=os3w[:, :], in0=os3w[:, :], scalar1=INV_127,
                            scalar2=None, op0=ALU.mult)
            v.drain()
            sv.inc(v.tensor_scalar(out=os3w[:, :], in0=os3w[:, :], scalar1=mwb[:, 2:3],
                                   scalar2=None, op0=ALU.mult))
        V_Q3 = sv.v

        GUL_LD = {}
        EV6 = []
        for b4 in range(NBLK):
            @block.sync
            def _(s2, b4=b4):
                sg.wait(s2, SG_P5)
                if b4 >= 2:
                    sv.wait(s2, XQ3_Q[b4 - 2])
                GUL_LD[b4] = lx[b4 % 2].dma(s2.dma_start(out=gul[b4 % 2][:, :],
                                                         in_=gu_d[128 * b4:128 * (b4 + 1), :]))

            @block.vector
            def _(v, b4=b4):
                v.wait_ge(lx[b4 % 2].h, GUL_LD[b4])
                sv.wait(v, V_Q3)
                if b4 >= 2:
                    ste.wait(v, XQ3_T[b4 - 2])
                v.drain()
                v.tensor_scalar(out=rnd3[:, :], in0=gul[b4 % 2][:, :], scalar1=q3v[:, b4:b4 + 1],
                                scalar2=MAGIC, op0=ALU.mult, op1=ALU.add)
                v.drain()
                sv.inc(v.tensor_scalar(out=xq3b[:, :], in0=rnd3[:, :], scalar1=MAGIC,
                                       scalar2=None, op0=ALU.subtract))
            XQ3_Q[b4] = sv.v

            for c44g in range(11):
                q6i = len(EV6)
                pbs = 0 if q6i % 2 == 0 else 4

                @block.tensor
                def _(te, b4=b4, c44g=c44g, q6i=q6i, pbs=pbs):
                    sv.wait(te, XQ3_Q[b4])
                    if q6i >= 2:
                        sv.wait(te, EV6[q6i - 2])
                    last = None
                    for c4i in range(4):
                        c44 = 4 * c44g + c4i
                        last = te.matmul(pb[pbs + c4i][:, 0:128],
                                         xq3b[:, 128 * c44:128 * (c44 + 1)],
                                         identb[:, :], start=True, stop=True)
                    ste.inc(last)

                @block.vector
                def _(v, b4=b4, c44g=c44g, pbs=pbs):
                    ste.wait(v)
                    last = None
                    for c4i in range(4):
                        c44 = 4 * c44g + c4i
                        last = v.tensor_copy(
                            out=xq3t[:, TB * c44 + 128 * b4:TB * c44 + 128 * (b4 + 1)],
                            in_=pb[pbs + c4i][:, 0:128])
                    sv.inc(last)
                EV6.append(sv.v)
            XQ3_T[b4] = ste.v
        V_XQ3 = sv.v

        if debug:
            @block.gpsimd
            def _(g):
                sv.wait(g, V_XQ3)
                g.wait_ge(cc.h, CC_RS)
                stm.dma(g.dma_start(out=dbg["gu"][:, :], in_=gu_d[:, :]))
                stm.dma(g.dma_start(out=dbg["aor"][:, :], in_=aor_d[:, :]))


        stp4c = contextlib.ExitStack()
        w3t = [stp4c.enter_context(nc.sbuf_tensor(f"w3t{i}", [128, 1024], BF16)) for i in range(4)]
        fbuf2 = [stp4c.enter_context(nc.sbuf_tensor(f"fbuf{i}", [128, 1024], F32)) for i in range(2)]
        xrb2 = [stp4c.enter_context(nc.sbuf_tensor(f"xrb{i}", [128, 1024], F32)) for i in range(2)]
        aorb2 = [stp4c.enter_context(nc.sbuf_tensor(f"aorb{i}", [128, 1024], F32)) for i in range(2)]
        FIN_VF = {}
        FIN_ST = {}

        SCF_LAST = {}
        for th in range(2):
            for dh in range(2):
                grp = 2 * th + dh
                W3_LD = {}
                for c44 in range(44):
                    @block.sync
                    def _(s2, th=th, dh=dh, c44=c44):
                        if c44 == 0 and th == 0 and dh == 0:
                            sv.wait(s2, V_XQ3)
                        if c44 >= 4:
                            ste.wait(s2, MM3[(th, dh, c44 - 4)])
                        elif (th, dh) != (0, 0):
                            pth, pdh = (th, dh - 1) if dh > 0 else (th - 1, 1)
                            ste.wait(s2, MM3[(pth, pdh, 40 + c44)])
                        W3_LD[c44] = lnt6[c44 % 4].dma(s2.dma_start(
                            out=w3t[c44 % 4][:, :],
                            in_=w3q[128 * c44:128 * (c44 + 1), 1024 * dh:1024 * (dh + 1)]))

                    @block.tensor
                    def _(te, th=th, dh=dh, c44=c44, grp=grp):
                        te.wait_ge(lnt6[c44 % 4].h, W3_LD[c44])
                        if c44 == 0:
                            if grp == 0:
                                sv.wait(te)
                                ssc.wait(te)
                            else:
                                ssc.wait(te, SCF_LAST[grp - 1])
                        last = None
                        for b4 in range(4):
                            blk = 4 * th + b4
                            lx = xq3t[:, TB * c44 + 128 * blk:TB * c44 + 128 * (blk + 1)]
                            st_ = c44 == 0
                            sp = c44 == 43
                            last = te.matmul(pb[2 * b4][:, :], lx, w3t[c44 % 4][:, 0:512],
                                             start=st_, stop=sp)
                            last = te.matmul(pb[2 * b4 + 1][:, :], lx, w3t[c44 % 4][:, 512:1024],
                                             start=st_, stop=sp)
                        MM3[(th, dh, c44)] = ste.inc(last)

                def fin_prefetch(th, dh, b4):
                    blk = 4 * th + b4
                    gidx = 4 * (2 * th + dh) + b4

                    @block.gpsimd
                    def _(g, th=th, dh=dh, b4=b4, blk=blk, gidx=gidx):
                        if gidx == 0:
                            g.wait_ge(cc.h, CC_RS)
                        if gidx >= 2:
                            sv.wait(g, FIN_VF[gidx - 2])
                        ldf[gidx % 2].dma(g.dma_start(out=xrb2[gidx % 2][:, :],
                                                      in_=x_sl[128 * blk:128 * (blk + 1),
                                                               1024 * dh:1024 * (dh + 1)]))
                        FIN_LD[(th, dh, b4)] = ldf[gidx % 2].dma(
                            g.dma_start(out=aorb2[gidx % 2][:, :],
                                        in_=aor_d[128 * blk:128 * (blk + 1),
                                                  1024 * dh:1024 * (dh + 1)]))

                fin_prefetch(th, dh, 0)
                fin_prefetch(th, dh, 1)

                for b4 in range(4):
                    blk = 4 * th + b4
                    gidx = 4 * grp + b4
                    fbuf = fbuf2[gidx % 2]

                    @block.scalar
                    def _(sc, th=th, dh=dh, b4=b4, blk=blk, gidx=gidx, fbuf=fbuf):
                        ste.wait(sc, MM3[(th, dh, 43)])
                        if gidx >= 2:
                            sv.wait(sc, FIN_VF[gidx - 2])
                            sc.wait_ge(stx[gidx % 2].h, FIN_ST[gidx - 2])
                        last = None
                        for ph in range(2):
                            last = sc.mul(out=fbuf[:, 512 * ph:512 * (ph + 1)],
                                          in_=pb[2 * b4 + ph][:, :], mul=os3w[:, blk:blk + 1])
                        ssc.inc(last)
                    SC_F = ssc.v
                    if b4 == 3:
                        SCF_LAST[grp] = SC_F

                    @block.vector
                    def _(v, th=th, dh=dh, b4=b4, gidx=gidx, SC_F=SC_F, fbuf=fbuf):
                        ssc.wait(v, SC_F)
                        v.wait_ge(ldf[gidx % 2].h, FIN_LD[(th, dh, b4)])
                        v.drain()
                        v.tensor_tensor(out=fbuf[:, :], in0=fbuf[:, :],
                                        in1=aorb2[gidx % 2][:, :], op=ALU.add)
                        v.drain()
                        v.tensor_tensor(out=fbuf[:, :], in0=fbuf[:, :],
                                        in1=xrb2[gidx % 2][:, :], op=ALU.add)
                        v.drain()
                        sv.inc(v.tensor_tensor(out=fbuf[:, :], in0=fbuf[:, :],
                                               in1=borep[:, 1024 * dh:1024 * (dh + 1)], op=ALU.add))
                    FIN_VF[gidx] = sv.v

                    @block.gpsimd
                    def _(g, th=th, dh=dh, b4=b4, blk=blk, gidx=gidx, fbuf=fbuf):
                        sv.wait(g, FIN_VF[gidx])
                        FIN_ST[gidx] = stx[gidx % 2].dma(
                            g.dma_start(out=out_d[128 * blk:128 * (blk + 1),
                                                  1024 * dh:1024 * (dh + 1)],
                                        in_=fbuf[:, :]))
                    if b4 < 2:
                        fin_prefetch(th, dh, b4 + 2)

        stp4c.close()
        stp4b.close()
        stp4.close()
        @block.sync
        def _(s2):
            stm.wait(s2)
            cc.wait(s2)
            stx[0].wait(s2)
            stx[1].wait(s2)
    return nc, dbg


def build_p2(env):
    pass


G2N = (np.arange(S) % A) * 4 + (np.arange(S) // A)


def prep_inputs(inp):
    pieces, atiles, strips, stiles = build_attn_plan()
    NT = len(atiles)
    SW = max(t.shape[1] for t in stiles)
    atl_loc = np.zeros((NT * 128, 512), np.float32)
    atl_str = np.zeros((NT * 128, 512), np.float32)
    for i, (lt, stt) in enumerate(atiles):
        atl_loc[128 * i:128 * (i + 1)] = lt
        atl_str[128 * i:128 * (i + 1)] = stt
    strip_arr = np.zeros((len(stiles) * 128, SW), np.float32)
    for i, t in enumerate(stiles):
        strip_arr[128 * i:128 * (i + 1), :t.shape[1]] = t
    bf = ml_dtypes.bfloat16
    x2 = np.asarray(inp["L0_attn_hidden_states"])[0]
    ident = np.eye(128, dtype=np.float32)
    in_maps = []
    for c in range(NCORES):
        sl = G2N[1024 * c:1024 * (c + 1)]
        m = {
            "x_sl": np.ascontiguousarray(x2[sl]),
            "wq_sl": np.ascontiguousarray(inp["L0_attn_Wq"][256 * c:256 * (c + 1)]),
            "wk_sl": np.ascontiguousarray(inp["L0_attn_Wk"][256 * c:256 * (c + 1)]),
            "wv_sl": np.ascontiguousarray(inp["L0_attn_Wv"][256 * c:256 * (c + 1)]),
            "wo_sl": np.ascontiguousarray(np.asarray(inp["L0_attn_Wo"])[:, 256 * c:256 * (c + 1)].T),
            "bq_sl": np.ascontiguousarray(inp["L0_attn_bq"][256 * c:256 * (c + 1)]).reshape(256, 1),
            "bk_sl": np.ascontiguousarray(inp["L0_attn_bk"][256 * c:256 * (c + 1)]).reshape(256, 1),
            "bv_sl": np.ascontiguousarray(inp["L0_attn_bv"][256 * c:256 * (c + 1)]).reshape(1, 256),
            "bo_in": np.ascontiguousarray(inp["L0_attn_bo"]).reshape(1, D),
            "w1_sh": np.ascontiguousarray(np.asarray(inp["L0_ffn_w1"])[:, 256 * c:256 * (c + 1)].T),
            "w2_sh": np.ascontiguousarray(np.asarray(inp["L0_ffn_w2"])[:, 256 * c:256 * (c + 1)].T),
            "w3_sh": np.ascontiguousarray(np.asarray(inp["L0_ffn_w3"])[:, FSH * c:FSH * (c + 1)].T),
            "atl_loc": atl_loc.astype(bf),
            "atl_str": atl_str.astype(bf),
            "strip_in": strip_arr.astype(bf),
            "identb_in": ident.astype(bf),
            "identf_in": ident,
        }
        m = {k: np.ascontiguousarray(v) for k, v in m.items()}
        in_maps.append(m)
    return in_maps



_CACHE = {}


def kernel(**inputs):
    inp = {k: np.asarray(v) for k, v in inputs.items()}
    if "nc" not in _CACHE:
        _CACHE["nc"] = build_nc(run_to="p7", debug=False)[0]
    nc = _CACHE["nc"]
    in_maps = prep_inputs(inp)
    from concourse.bass_utils import run_bass_kernel_spmd
    res = run_bass_kernel_spmd(nc, in_maps, core_ids=list(range(NCORES)))
    outs = np.concatenate([res.results[c]["out"] for c in range(NCORES)], 0)
    out = np.empty((S, D), np.float32)
    out[G2N] = outs
    return out[None]


# version 14 (final)
# speedup vs baseline: 1.1294x; 1.0315x over previous
import contextlib
import math
import numpy as np
import ml_dtypes
import concourse.bass as bass
import concourse.mybir as mybir

F32 = mybir.dt.float32
BF16 = mybir.dt.bfloat16
AF = mybir.ActivationFunctionType
ALU = mybir.AluOpType
AX = mybir.AxisListType

NCORES = 8
S = 8192
D = 2048
A = S // 4
HD = 128
FF = 5632
FSH = FF // 8
W3SH = D // 8
TB = 1024
NBLK = TB // 128
NEG = float(np.float32(-1.0e38))
MAGIC = float(np.float32(1.5 * 2 ** 23))
SC_ATTN = float(np.float32(1.0 / math.sqrt(HD)))
EPS_LN = 1e-5
EPS_RMS = 1e-6
EPS_Q = 1e-5
INV_D = float(np.float32(1.0 / D))
INV_FF = float(np.float32(1.0 / FF))
INV_WN = float(np.float32(1.0 / (FF * D)))
INV_SD = float(np.float32(1.0 / (S * D)))
INV_127 = float(np.float32(1.0 / 127.0))
CK = 128
J = A // CK
P3_SKIP = set()


def _masks_natural(i, j):
    di = i - j
    loc = (np.abs(di) <= 64) | (i == 0) | (j == 0)
    st = (di % 4 == 0) | (i == 0) | (j == 0)
    return loc, st


def _a_tile(rho_k, kb):
    b = kb + np.arange(CK)
    loc_t = np.full((CK, 4 * CK), NEG, np.float32)
    str_t = np.full((CK, 4 * CK), NEG, np.float32)
    for r in range(4):
        a = kb + np.arange(CK)
        i = 4 * a[None, :] + r
        jn = 4 * b[:, None] + rho_k
        band = np.abs(i - jn) <= 64
        owned = band if kb % CK == 0 else band & ((a[None, :] // CK) != (b[:, None] // CK))
        owned = owned & (jn != 0)
        loc_m, str_m = _masks_natural(i, jn)
        loc_t[:, r * CK:(r + 1) * CK] = np.where(owned & loc_m, 0.0, NEG)
        str_t[:, r * CK:(r + 1) * CK] = np.where(owned & str_m, 0.0, NEG)
    return loc_t, str_t


def build_attn_plan():
    kbs = [CK * j for j in range(J)] + [CK * j + 64 for j in range(J - 1)]
    pieces, tiles, tids = [], [], {}
    for rho_k in range(4):
        for kb in kbs:
            lt, st = _a_tile(rho_k, kb)
            key = lt.tobytes() + st.tobytes()
            vid = tids.get(key)
            if vid is None:
                vid = len(tiles)
                tids[key] = vid
                tiles.append((lt, st))
            pieces.append((rho_k, kb, vid))
    strips, stiles, stids = [], [], {}
    for rho in range(4):
        for j in range(J):
            a_lo = max(0, CK * j - 16)
            a_hi = min(A, CK * j + CK + 16)
            t = np.zeros((CK, a_hi - a_lo), np.float32)
            bb = CK * j + np.arange(CK)
            aa = a_lo + np.arange(a_hi - a_lo)
            t[np.abs(aa[None, :] - bb[:, None]) <= 16] = NEG
            key = (t.shape[1], t.tobytes())
            svid = stids.get(key)
            if svid is None:
                svid = len(stiles)
                stids[key] = svid
                stiles.append(t)
            strips.append((rho, j, a_lo, a_hi, svid))
    return pieces, tiles, strips, stiles


class Sem:
    def __init__(self, handle):
        self.h = handle
        self.v = 0

    def inc(self, instr, n=1):
        instr.then_inc(self.h, n)
        self.v += n
        return self.v

    def dma(self, instr):
        return self.inc(instr, 16)

    def wait(self, eng, v=None):
        eng.wait_ge(self.h, self.v if v is None else v)


def build_nc(run_to="p1", debug=True):
    LVL = {"p1": 1, "p2": 2, "p3": 3, "p4": 4, "p5": 5, "p6": 6, "p7": 7}[run_to]
    nc = bass.Bass("TRN2", target_bir_lowering=False)
    pieces, atiles, strips, stiles = build_attn_plan()
    NT = len(atiles)
    stile_w = [t.shape[1] for t in stiles]
    SW = max(stile_w)
    NS = len(stiles)

    x_sl = nc.dram_tensor("x_sl", [TB, D], F32, kind="ExternalInput")
    wq_sl = nc.dram_tensor("wq_sl", [256, D], F32, kind="ExternalInput")
    wk_sl = nc.dram_tensor("wk_sl", [256, D], F32, kind="ExternalInput")
    wv_sl = nc.dram_tensor("wv_sl", [256, D], F32, kind="ExternalInput")
    wo_sl = nc.dram_tensor("wo_sl", [256, D], F32, kind="ExternalInput")
    bq_sl = nc.dram_tensor("bq_sl", [256, 1], F32, kind="ExternalInput")
    bk_sl = nc.dram_tensor("bk_sl", [256, 1], F32, kind="ExternalInput")
    bv_sl = nc.dram_tensor("bv_sl", [1, 256], F32, kind="ExternalInput")
    bo_in = nc.dram_tensor("bo_in", [1, D], F32, kind="ExternalInput")
    w1_sh = nc.dram_tensor("w1_sh", [256, FF], F32, kind="ExternalInput")
    w2_sh = nc.dram_tensor("w2_sh", [256, FF], F32, kind="ExternalInput")
    w3_sh = nc.dram_tensor("w3_sh", [FSH, D], F32, kind="ExternalInput")
    atl_loc = nc.dram_tensor("atl_loc", [NT * 128, 512], BF16, kind="ExternalInput")
    atl_str = nc.dram_tensor("atl_str", [NT * 128, 512], BF16, kind="ExternalInput")
    strip_in = nc.dram_tensor("strip_in", [NS * 128, SW], BF16, kind="ExternalInput")
    identb_in = nc.dram_tensor("identb_in", [128, 128], BF16, kind="ExternalInput")
    identf_in = nc.dram_tensor("identf_in", [128, 128], F32, kind="ExternalInput")

    out_d = nc.dram_tensor("out", [TB, D], F32, kind="ExternalOutput")

    xn_ag_in0 = nc.dram_tensor("xn_ag_in0", [D, TB // 2], BF16)
    xn_ag_in1 = nc.dram_tensor("xn_ag_in1", [D, TB // 2], BF16)
    xn_full0 = nc.dram_tensor("xn_full0", [8 * D, TB // 2], BF16, addr_space="Shared")
    xn_full1 = nc.dram_tensor("xn_full1", [8 * D, TB // 2], BF16, addr_space="Shared")
    stats_in = nc.dram_tensor("stats_in", [1, 8], F32)
    stats_out = nc.dram_tensor("stats_out", [1, 8], F32)
    wq_bf = nc.dram_tensor("wq_bf", [256, D], BF16)
    wk_bf = nc.dram_tensor("wk_bf", [256, D], BF16)
    wv_bf = nc.dram_tensor("wv_bf", [256, D], BF16)
    wo_bf = nc.dram_tensor("wo_bf", [256, D], BF16)
    w1q_in = nc.dram_tensor("w1q_in", [256, FF], BF16)
    w2q_in = nc.dram_tensor("w2q_in", [256, FF], BF16)
    w3q_in = nc.dram_tensor("w3q_in", [FSH, D], BF16)
    w1q = nc.dram_tensor("w1q", [D, FF], BF16, addr_space="Shared")
    w2q = nc.dram_tensor("w2q", [D, FF], BF16, addr_space="Shared")
    w3q = nc.dram_tensor("w3q", [FF, D], BF16, addr_space="Shared")
    xq1_d = nc.dram_tensor("xq1_d", [D, TB], BF16)
    gu_d = nc.dram_tensor("gu_d", [TB, FF], BF16)
    xq3_d = nc.dram_tensor("xq3_d", [TB, FF], BF16)
    v_raw = nc.dram_tensor("v_raw", [128, 2 * S], BF16)
    aop_d = nc.dram_tensor("aop_d", [S, D], F32)
    aor_d = nc.dram_tensor("aor_d", [TB, D], F32)

    dbg = {}
    if debug:
        dbg["gu"] = nc.dram_tensor("dbg_gu", [TB, FF], BF16, kind="ExternalOutput")
        for name, shape in [("xn", [TB, D]), ("xq1", [TB, D]),
                            ("stats", [1, 8]), ("gsc", [1, 8]),
                            ("qt", [128, 2 * S]), ("kt", [128, 2 * S]),
                            ("vsb", [128, 2 * S]), ("nrm", [1, 8]), ("mhat", [1, 4]),
                            ("ebA", [128, 2]), ("ebB", [128, 2]),
                            ("aoT", [128, 2 * S]), ("l", [1, 2 * S]),
                            ("rcol", [128, 128]), ("q3", [128, 4 * NBLK]),
                            ("aor", [TB, D])]:
            dbg[name] = nc.dram_tensor("dbg_" + name, shape, F32, kind="ExternalOutput")

    st = contextlib.ExitStack()
    with st:
        pb = [st.enter_context(nc.psum_tensor(f"pb{i}", [128, 512], F32)) for i in range(8)]
        ones_bf = st.enter_context(nc.sbuf_tensor("ones_bf", [128, 1], BF16))
        onesw = st.enter_context(nc.sbuf_tensor("onesw", [128, 16], BF16))
        onesr_f = st.enter_context(nc.sbuf_tensor("onesr_f", [1, 128], F32))
        one_f = st.enter_context(nc.sbuf_tensor("one_f", [1, 1], F32))
        one_bf = st.enter_context(nc.sbuf_tensor("one_bf", [1, 1], BF16))
        identf = st.enter_context(nc.sbuf_tensor("identf", [128, 128], F32))
        identb = st.enter_context(nc.sbuf_tensor("identb", [128, 128], BF16))
        q1v = st.enter_context(nc.sbuf_tensor("q1v", [128, NBLK], F32))
        os1v = st.enter_context(nc.sbuf_tensor("os1v", [128, NBLK], F32))
        stats_sb = st.enter_context(nc.sbuf_tensor("stats_sb", [1, 8], F32))
        cntacc = st.enter_context(nc.sbuf_tensor("cntacc", [128, 4], F32))
        swb = st.enter_context(nc.sbuf_tensor("swb", [128, 5], F32))
        gsc = st.enter_context(nc.sbuf_tensor("gsc", [1, 8], F32))

        sems = {}
        for name in ["ld", "ld2", "ld3", "ld4", "stm", "cc", "sv", "ssc", "ste", "sg",
                     "lx0", "lx1", "lw0", "lw1", "ln0", "ln1", "ln2", "ln3", "ln4", "ln5", "sld", "gld", "sx0", "sx1", "sq0", "sq1", "gq0", "gq1", "lid", "lat"]:
            sems[name] = Sem(st.enter_context(nc.semaphore(name)))
        ld, ld2, ld3, stm, cc, sv, ssc, ste, sg = (sems[k] for k in
            ["ld", "ld2", "ld3", "stm", "cc", "sv", "ssc", "ste", "sg"])
        lx = [sems["lx0"], sems["lx1"]]
        lw = [sems["lw0"], sems["lw1"]]
        lnt = [sems["ln0"], sems["ln1"], sems["ln2"]]
        lnt6 = [sems[f"ln{i}"] for i in range(6)]
        sld = sems["sld"]
        gld = sems["gld"]
        stx = [sems["sx0"], sems["sx1"]]
        swq = [sems["sq0"], sems["sq1"]]
        gwq = [sems["gq0"], sems["gq1"]]
        lid = sems["lid"]
        lat = sems["lat"]
        ldf = [sems["ld3"], sems["ld4"]]

        block = st.enter_context(nc.Block())

        @block.vector
        def _(v):
            v.memset(ones_bf[:, :], 1.0)
            v.memset(onesw[:, :], 1.0)
            v.memset(onesr_f[:, :], 1.0)
            v.memset(one_f[:, :], 1.0)
            v.memset(one_bf[:, :], 1.0)
            sv.inc(v.memset(cntacc[:, :], 0.0))
        V_CONST = sv.v

        @block.sync
        def _(s):
            lid.dma(s.dma_start(out=identf[:, :], in_=identf_in[:, :]))
            lid.dma(s.dma_start(out=identb[:, :], in_=identb_in[:, :]))

        @block.gpsimd
        def _(g):
            gld.dma(g.dma_start(out=wq_bf[:, :], in_=wq_sl[:, :]))
            gld.dma(g.dma_start(out=wk_bf[:, :], in_=wk_sl[:, :]))
            gld.dma(g.dma_start(out=wv_bf[:, :], in_=wv_sl[:, :]))
            gld.dma(g.dma_start(out=wo_bf[:, :], in_=wo_sl[:, :]))
        LD_WCAST = gld.v

        stp1 = contextlib.ExitStack()
        xnT_sb = stp1.enter_context(nc.sbuf_tensor("xnT_sb", [128, 16 * TB], BF16))
        xq1T_sb = stp1.enter_context(nc.sbuf_tensor("xq1T_sb", [128, 16 * TB], BF16))
        xblk = [stp1.enter_context(nc.sbuf_tensor(f"xblk{i}", [128, D], F32)) for i in range(2)]
        xnblk = [stp1.enter_context(nc.sbuf_tensor(f"xnblk{i}", [128, D], BF16)) for i in range(2)]
        sqblk = stp1.enter_context(nc.sbuf_tensor("sqblk", [128, D], BF16))
        gtblk = stp1.enter_context(nc.sbuf_tensor("gtblk", [128, D], BF16))
        rndblk = stp1.enter_context(nc.sbuf_tensor("rndblk", [128, D], F32))
        xq1blk = [stp1.enter_context(nc.sbuf_tensor(f"xq1blk{i}", [128, D], BF16)) for i in range(2)]
        s_sum = stp1.enter_context(nc.sbuf_tensor("s_sum", [128, 1], F32))
        s_sq = stp1.enter_context(nc.sbuf_tensor("s_sq", [128, 1], F32))
        s_mean = stp1.enter_context(nc.sbuf_tensor("s_mean", [128, 1], F32))
        s_var = stp1.enter_context(nc.sbuf_tensor("s_var", [128, 1], F32))
        s_rstd = stp1.enter_context(nc.sbuf_tensor("s_rstd", [128, 1], F32))
        s_nmr = stp1.enter_context(nc.sbuf_tensor("s_nmr", [128, 1], F32))
        s_t0 = stp1.enter_context(nc.sbuf_tensor("s_t0", [128, 1], F32))
        s_t1 = stp1.enter_context(nc.sbuf_tensor("s_t1", [128, 1], F32))
        s_r1 = stp1.enter_context(nc.sbuf_tensor("s_r1", [128, 1], F32))
        s_m1 = stp1.enter_context(nc.sbuf_tensor("s_m1", [128, 1], F32))
        s_s1 = stp1.enter_context(nc.sbuf_tensor("s_s1", [128, 1], F32))

        xload_done = [0] * NBLK
        xn_store_done = [0] * NBLK
        T_XNT_DONE = [0] * NBLK

        def load_xblk(b):
            @block.sync
            def _(s):
                if b >= 2:
                    ssc.wait(s, SC_XN_DONE[b - 2])
                xload_done[b] = lx[b % 2].dma(
                    s.dma_start(out=xblk[b % 2][:, :], in_=x_sl[128 * b:128 * (b + 1), :]))

        SC_XN_DONE = [0] * NBLK
        load_xblk(0)
        load_xblk(1)

        for b in range(NBLK):
            xb = xblk[b % 2]
            xnb = xnblk[b % 2]
            xqb = xq1blk[b % 2]

            @block.scalar
            def _(sc, b=b, xb=xb):
                sc.wait_ge(lx[b % 2].h, xload_done[b])
                ssc.inc(sc.activation(out=sqblk[:, :], in_=xb[:, :], func=AF.Square,
                                      accum_out=s_sq[:, :]))

            @block.vector
            def _(v, b=b, xb=xb):
                ssc.wait(v)
                v.reduce_sum(out=s_sum[:, :], in_=xb[:, :], axis=AX.X)
                v.drain()
                v.tensor_scalar(out=s_mean[:, :], in0=s_sum[:, :], scalar1=INV_D,
                                scalar2=None, op0=ALU.mult)
                v.tensor_scalar(out=s_var[:, :], in0=s_sq[:, :], scalar1=INV_D,
                                scalar2=None, op0=ALU.mult)
                v.drain()
                v.tensor_tensor(out=s_t0[:, :], in0=s_mean[:, :], in1=s_mean[:, :], op=ALU.mult)
                v.drain()
                v.tensor_tensor(out=s_var[:, :], in0=s_var[:, :], in1=s_t0[:, :], op=ALU.subtract)
                v.drain()
                v.tensor_scalar(out=s_t0[:, :], in0=s_var[:, :], scalar1=EPS_LN,
                                scalar2=None, op0=ALU.add)
                v.drain()
                sv.inc(v.reciprocal(out=s_t1[:, :], in_=s_t0[:, :]))

            @block.scalar
            def _(sc, b=b):
                sv.wait(sc)
                ssc.inc(sc.sqrt(out=s_rstd[:, :], in_=s_t1[:, :]))

            @block.vector
            def _(v, b=b):
                ssc.wait(v)
                v.tensor_tensor(out=s_nmr[:, :], in0=s_mean[:, :], in1=s_rstd[:, :], op=ALU.mult)
                v.drain()
                sv.inc(v.tensor_scalar(out=s_nmr[:, :], in0=s_nmr[:, :], scalar1=-1.0,
                                       scalar2=None, op0=ALU.mult))

            @block.scalar
            def _(sc, b=b, xb=xb, xnb=xnb):
                sv.wait(sc)
                if b >= 2:
                    stx[b % 2].wait(sc, xn_store_done[b - 2])
                    ste.wait(sc, T_XNT_DONE[b - 2])
                sc.activation(out=xnb[:, :], in_=xb[:, :], func=AF.Identity,
                              bias=s_nmr[:, :], scale=s_rstd[:, :])
                ssc.inc(sc.activation(out=sqblk[:, :], in_=xb[:, :], func=AF.Square,
                                      bias=s_nmr[:, :], scale=s_rstd[:, :],
                                      accum_out=s_sq[:, :]))
                SC_XN_DONE[b] = ssc.v

            @block.vector
            def _(v, b=b, xnb=xnb, xqb=xqb):
                ssc.wait(v)
                v.tensor_scalar(out=gtblk[:, :], in0=sqblk[:, :], scalar1=0.01,
                                scalar2=None, op0=ALU.is_gt)
                v.drain()
                v.reduce_sum(out=s_t0[:, :], in_=gtblk[:, :], axis=AX.X)
                v.drain()
                v.tensor_tensor(out=cntacc[:, 0:1], in0=cntacc[:, 0:1], in1=s_t0[:, :], op=ALU.add)
                v.drain()
                v.tensor_scalar(out=s_t0[:, :], in0=s_sq[:, :], scalar1=INV_D,
                                scalar2=EPS_RMS, op0=ALU.mult, op1=ALU.add)
                v.drain()
                sv.inc(v.reciprocal(out=s_t1[:, :], in_=s_t0[:, :]))

            @block.scalar
            def _(sc, b=b):
                sv.wait(sc)
                ssc.inc(sc.sqrt(out=s_r1[:, :], in_=s_t1[:, :]))

            @block.vector
            def _(v, b=b, xnb=xnb, xqb=xqb):
                ssc.wait(v)
                if b >= 2:
                    stx[b % 2].wait(v, xn_store_done[b - 2])
                    ste.wait(v, T_XNT_DONE[b - 2])
                v.reduce_max(out=s_t0[:, :], in_=xnb[:, :], axis=AX.X,
                             apply_absolute_value=True)
                v.drain()
                v.tensor_tensor(out=s_m1[:, :], in0=s_r1[:, :], in1=s_t0[:, :], op=ALU.mult)
                v.drain()
                v.tensor_scalar(out=s_m1[:, :], in0=s_m1[:, :], scalar1=EPS_Q,
                                scalar2=None, op0=ALU.max)
                v.drain()
                v.reciprocal(out=s_s1[:, :], in_=s_m1[:, :])
                v.drain()
                v.tensor_scalar(out=s_s1[:, :], in0=s_s1[:, :], scalar1=127.0,
                                scalar2=None, op0=ALU.mult)
                v.drain()
                v.tensor_tensor(out=q1v[:, b:b + 1], in0=s_r1[:, :], in1=s_s1[:, :], op=ALU.mult)
                v.tensor_scalar(out=os1v[:, b:b + 1], in0=s_m1[:, :], scalar1=INV_127,
                                scalar2=None, op0=ALU.mult)
                v.drain()
                v.tensor_scalar(out=rndblk[:, :], in0=xnb[:, :], scalar1=q1v[:, b:b + 1],
                                scalar2=MAGIC, op0=ALU.mult, op1=ALU.add)
                v.drain()
                sv.inc(v.tensor_scalar(out=xqb[:, :], in0=rndblk[:, :], scalar1=MAGIC,
                                       scalar2=None, op0=ALU.subtract))

            @block.gpsimd
            def _(g, b=b, xnb=xnb, xqb=xqb):
                sv.wait(g)
                if debug:
                    stx[b % 2].dma(g.dma_start(out=dbg["xn"][128 * b:128 * (b + 1), :], in_=xnb[:, :]))
                    stx[b % 2].dma(g.dma_start(out=dbg["xq1"][128 * b:128 * (b + 1), :], in_=xqb[:, :]))
                xn_store_done[b] = stx[b % 2].v

            for dch in range(16):
                tb_ = pb[6] if dch % 2 == 0 else pb[7]
                tb2_ = pb[7] if dch % 2 == 0 else pb[6]

                @block.tensor
                def _(te, b=b, dch=dch, xnb=xnb, xqb=xqb, tb_=tb_, tb2_=tb2_):
                    if b == 0 and dch == 0:
                        lid.wait(te)
                    ssc.wait(te)
                    sv.wait(te)
                    te.matmul(tb_[:, 0:128], xnb[:, 128 * dch:128 * (dch + 1)], identb[:, :],
                              start=True, stop=True)
                    ste.inc(te.matmul(tb2_[:, 128:256], xqb[:, 128 * dch:128 * (dch + 1)],
                                      identb[:, :], start=True, stop=True))

                @block.vector
                def _(v, b=b, dch=dch, tb_=tb_, tb2_=tb2_):
                    ste.wait(v)
                    co = 1024 * dch + 128 * b
                    v.tensor_copy(out=xnT_sb[:, co:co + 128], in_=tb_[:, 0:128])
                    sv.inc(v.tensor_copy(out=xq1T_sb[:, co:co + 128], in_=tb2_[:, 128:256]))
            T_XNT_DONE[b] = ste.v
            if b == 3:
                SV_HALF0 = sv.v

                @block.gpsimd
                def _(g):
                    sv.wait(g, SV_HALF0)
                    stm.dma(g.dma_start(
                        out=xn_ag_in0.ap().rearrange("(c p) t -> p c t", p=128),
                        in_=xnT_sb.ap().rearrange("p (c t) -> p c t", c=16)[:, :, 0:512]))
                STM_XNT0 = stm.v

                @block.gpsimd
                def _(g, STM_XNT0=STM_XNT0):
                    g.wait_ge(stm.h, STM_XNT0)
                    cc.inc(g.collective_compute(
                        "AllGather", ALU.bypass, replica_groups=[list(range(NCORES))],
                        ins=[xn_ag_in0.ap().opt()], outs=[xn_full0.ap().opt()]))
                CC_XN0 = cc.v

            if b + 2 < NBLK:
                load_xblk(b + 2)


        @block.gpsimd
        def _(g):
            sv.wait(g)
            stm.dma(g.dma_start(
                out=xn_ag_in1.ap().rearrange("(c p) t -> p c t", p=128),
                in_=xnT_sb.ap().rearrange("p (c t) -> p c t", c=16)[:, :, 512:1024]))
            stm.dma(g.dma_start(
                out=xq1_d.ap().rearrange("(c p) t -> p c t", p=128), in_=xq1T_sb[:, :]))
        STM_XNT = stm.v

        @block.gpsimd
        def _(g):
            g.wait_ge(stm.h, STM_XNT)
            cc.inc(g.collective_compute(
                "AllGather", ALU.bypass, replica_groups=[list(range(NCORES))],
                ins=[xn_ag_in1.ap().opt()], outs=[xn_full1.ap().opt()]))
        CC_XN1 = cc.v

        @block.sync
        def _(s2):
            sv.wait(s2)
            ssc.wait(s2)
            ste.wait(s2)
            stm.wait(s2)
            stx[0].wait(s2)
            stx[1].wait(s2)
            sg.inc(s2.nop())
        SG_LN_DONE = sg.v
        stp1.close()

        stp1b = contextlib.ExitStack()
        wch = [stp1b.enter_context(nc.sbuf_tensor(f"wch{i}", [128, 5632], F32)) for i in range(2)]
        s_t0b = stp1b.enter_context(nc.sbuf_tensor("s_t0b", [128, 1], F32))

        wabs_chunks = []
        for rb in range(2):
            wabs_chunks.append((w1_sh, 128 * rb, 128, 0, FF, 1))
        for rb in range(2):
            wabs_chunks.append((w2_sh, 128 * rb, 128, 0, FF, 2))
        for rb in range(6):
            r0 = 128 * rb
            wabs_chunks.append((w3_sh, r0, min(128, FSH - r0), 0, D, 3))
        WCH_LD = [0] * len(wabs_chunks)
        WCH_RD = [0] * len(wabs_chunks)
        for ci, (src, r0, nr, c0, ncol, col) in enumerate(wabs_chunks):
            @block.sync
            def _(s2, ci=ci, src=src, r0=r0, nr=nr, c0=c0, ncol=ncol):
                if ci == 0:
                    sg.wait(s2, SG_LN_DONE)
                if ci >= 2:
                    sv.wait(s2, WCH_RD[ci - 2])
                WCH_LD[ci] = swq[ci % 2].dma(s2.dma_start(
                    out=wch[ci % 2][0:nr, 0:ncol], in_=src[r0:r0 + nr, c0:c0 + ncol]))

            @block.vector
            def _(v, ci=ci, nr=nr, ncol=ncol, col=col):
                v.wait_ge(swq[ci % 2].h, WCH_LD[ci])
                v.drain()
                v.reduce_sum(out=s_t0b[0:nr, :], in_=wch[ci % 2][0:nr, 0:ncol], axis=AX.X,
                             apply_absolute_value=True)
                v.drain()
                WCH_RD[ci] = sv.inc(v.tensor_tensor(out=cntacc[0:nr, col:col + 1],
                                                    in0=cntacc[0:nr, col:col + 1],
                                                    in1=s_t0b[0:nr, :], op=ALU.add))

        @block.vector
        def _(v):
            v.drain()
            sv.inc(v.tensor_copy(out=s_t0b[:, :], in_=cntacc[:, 0:1]))
        V_WABS = sv.v

        @block.tensor
        def _(te):
            sv.wait(te)
            lid.wait(te)
            for c2 in range(4):
                ste.inc(te.matmul(pb[0][0:1, 128 * c2:128 * (c2 + 1)],
                                  cntacc[:, c2:c2 + 1], identf[:, :], start=True, stop=True))

        @block.vector
        def _(v):
            ste.wait(v)
            for c2 in range(4):
                v.reduce_sum(out=stats_sb[0:1, c2:c2 + 1], in_=pb[0][0:1, 128 * c2:128 * (c2 + 1)],
                             axis=AX.X)
            sv.inc(v.memset(stats_sb[0:1, 4:8], 0.0))
        V_STATS = sv.v

        @block.gpsimd
        def _(g):
            sv.wait(g, V_STATS)
            stm.dma(g.dma_start(out=stats_in[:, :], in_=stats_sb[:, :]))
            stm.wait(g)
            cc.inc(g.collective_compute(
                "AllReduce", ALU.add, replica_groups=[list(range(NCORES))],
                ins=[stats_in.ap().opt()], outs=[stats_out.ap().opt()]))
        CC_STATS = cc.v

        if LVL >= 2:
            build_p2(locals())



        @block.sync
        def _(s2):
            sv.wait(s2)
            ssc.wait(s2)
            stm.wait(s2)
            swq[0].wait(s2)
            swq[1].wait(s2)
            stx[0].wait(s2)
            stx[1].wait(s2)
            sg.inc(s2.nop())
        SG_P1DONE = sg.v
        stp1b.close()

        stp2 = contextlib.ExitStack()
        stp2a = contextlib.ExitStack()
        qt = stp2.enter_context(nc.sbuf_tensor("qt", [128, 2 * S], BF16))
        kt = stp2.enter_context(nc.sbuf_tensor("kt", [128, 2 * S], BF16))
        vsb = stp2.enter_context(nc.sbuf_tensor("vsb", [128, 2 * S], BF16))
        nrm = stp2.enter_context(nc.sbuf_tensor("nrm", [1, 8], F32))
        nrmw = stp2.enter_context(nc.sbuf_tensor("nrmw", [16, 8], F32))
        nrmt = stp2.enter_context(nc.sbuf_tensor("nrmt", [4, 17], F32))
        mhat = stp2.enter_context(nc.sbuf_tensor("mhat", [1, 4], F32))
        ebA = stp2.enter_context(nc.sbuf_tensor("ebA", [128, 2], F32))
        ebB = stp2.enter_context(nc.sbuf_tensor("ebB", [128, 2], F32))
        ebA0 = stp2.enter_context(nc.sbuf_tensor("ebA0", [128, 2], F32))
        ebB0 = stp2.enter_context(nc.sbuf_tensor("ebB0", [128, 2], F32))
        atb = stp2.enter_context(nc.sbuf_tensor("atb", [128, NT * 512], BF16))
        ssb = stp2.enter_context(nc.sbuf_tensor("ssb", [128, NS * SW], BF16))
        qin2 = [stp2.enter_context(nc.sbuf_tensor("qin0", [128, 1024], F32))]
        wqt_all = stp2a.enter_context(nc.sbuf_tensor("wqt_all", [128, 16 * 256], BF16))
        wkt_all = stp2a.enter_context(nc.sbuf_tensor("wkt_all", [128, 16 * 256], BF16))
        wvt_all = stp2a.enter_context(nc.sbuf_tensor("wvt_all", [128, 16 * 256], BF16))
        xnt = [stp2a.enter_context(nc.sbuf_tensor(f"xnt{i}", [128, 512], BF16)) for i in range(6)]
        bqsb = stp2a.enter_context(nc.sbuf_tensor("bqsb", [128, 2], F32))
        bksb = stp2a.enter_context(nc.sbuf_tensor("bksb", [128, 2], F32))
        bvsb = stp2a.enter_context(nc.sbuf_tensor("bvsb", [1, 256], F32))
        bvrep = stp2a.enter_context(nc.sbuf_tensor("bvrep", [128, 256], F32))
        sqsc = [stp2a.enter_context(nc.sbuf_tensor(f"sqsc{i}", [128, 512], BF16)) for i in range(4)]
        atscr = stp2a.enter_context(nc.sbuf_tensor("atscr", [128, 2 * 512], BF16))
        whalf2 = [stp2a.enter_context(nc.sbuf_tensor(f"whalf{i}", [128, 2 * D], BF16)) for i in range(2)]

        WH = {}
        WT_EVAC = {}
        WT_TE = {}
        EVW = []
        for wi, (wsrc, wdst) in enumerate(((wq_bf, wqt_all), (wk_bf, wkt_all), (wv_bf, wvt_all))):
            @block.sync
            def _(s, wi=wi, wsrc=wsrc):
                if wi == 0:
                    sg.wait(s, SG_P1DONE)
                    s.wait_ge(gld.h, LD_WCAST)
                elif wi >= 2:
                    ste.wait(s, WT_TE[wi - 2])
                lw[wi % 2].dma(s.dma_start(out=whalf2[wi % 2][:, 0:D], in_=wsrc[0:128, :]))
                WH[wi] = lw[wi % 2].dma(s.dma_start(out=whalf2[wi % 2][:, D:2 * D],
                                                    in_=wsrc[128:256, :]))

            for h2 in range(2):
                for grp in range(4):
                    it = len(EVW)
                    pbs = 0 if it % 2 == 0 else 4

                    @block.tensor
                    def _(te, wi=wi, h2=h2, grp=grp, it=it, pbs=pbs):
                        te.wait_ge(lw[wi % 2].h, WH[wi])
                        if it < 2:
                            lid.wait(te)
                            sv.wait(te)
                        else:
                            sv.wait(te, EVW[it - 2])
                        last = None
                        for j4 in range(4):
                            dch = 4 * grp + j4
                            last = te.matmul(pb[pbs + j4][:, 0:128],
                                             whalf2[wi % 2][:, D * h2 + 128 * dch:D * h2 + 128 * (dch + 1)],
                                             identb[:, :], start=True, stop=True)
                        ste.inc(last)
                    if h2 == 1 and grp == 3:
                        WT_TE[wi] = ste.v

                    @block.vector
                    def _(v, wi=wi, h2=h2, grp=grp, pbs=pbs, wdst=wdst):
                        ste.wait(v)
                        lastv = None
                        for j4 in range(4):
                            dch = 4 * grp + j4
                            lastv = v.tensor_copy(
                                out=wdst[:, 256 * dch + 128 * h2:256 * dch + 128 * (h2 + 1)],
                                in_=pb[pbs + j4][:, 0:128])
                        sv.inc(lastv)
                    EVW.append(sv.v)
            WT_EVAC[wi] = sv.v

        @block.sync
        def _(s):
            sv.wait(s, WT_EVAC[2])
            ld.dma(s.dma_start(out=bqsb[:, 0:1], in_=bq_sl[0:128, :]))
            ld.dma(s.dma_start(out=bqsb[:, 1:2], in_=bq_sl[128:256, :]))
            ld.dma(s.dma_start(out=bksb[:, 0:1], in_=bk_sl[0:128, :]))
            ld.dma(s.dma_start(out=bksb[:, 1:2], in_=bk_sl[128:256, :]))
            ld.dma(s.dma_start(out=bvsb[:, :], in_=bv_sl[:, :]))
            for i2 in range(NS):
                ld.dma(s.dma_start(out=ssb[:, SW * i2:SW * (i2 + 1)],
                                   in_=strip_in[128 * i2:128 * (i2 + 1), :]))
        LD_WT = ld.v

        @block.tensor
        def _(te):
            ld.wait(te, LD_WT)
            sv.wait(te)
            ste.inc(te.matmul(pb[1][:, 0:256], onesr_f[:, :], bvsb[:, :], start=True, stop=True))
        TE_BV = ste.v

        @block.vector
        def _(v):
            ste.wait(v, TE_BV)
            sv.inc(v.tensor_copy(out=bvrep[:, :], in_=pb[1][:, 0:256]))
        V_BVREP = sv.v


        @block.vector
        def _(v):
            v.memset(mhat[:, :], 0.0)
            v.memset(nrm[:, :], 0.0)
            sv.inc(v.memset(nrmw[:, :], 0.0))
        V_NRM0 = sv.v

        xnt_loaded = [0] * (16 * 16)
        evac_done = [0] * 16
        vevac_done = [0] * 16
        qkevac_done = [0] * 16
        nrm_mm = [0] * 64
        nrm_rd = [0] * 64

        def load_xnt(idx):
            pos_, dch = divmod(idx, 16)
            bb = BB_ORDER[pos_]

            @block.scalar
            def _(sc, idx=idx, bb=bb, dch=dch):
                if idx >= 6:
                    ste.wait(sc, mm_done[idx - 6])
                if idx == 0:
                    sc.wait_ge(cc.h, CC_XN0)
                    sg.wait(sc, SG_P1DONE)
                if idx == 128:
                    sc.wait_ge(cc.h, CC_XN1)
                c2 = bb // 2
                src = xn_full0 if bb % 2 == 0 else xn_full1
                xnt_loaded[idx] = lnt6[idx % 6].dma(
                    sc.dma_start(out=xnt[idx % 6][:, :],
                                 in_=src[D * c2 + 128 * dch:D * c2 + 128 * (dch + 1), :]))

        mm_done = [0] * (16 * 16)
        BB_ORDER = [0, 2, 4, 6, 8, 10, 12, 14, 1, 3, 5, 7, 9, 11, 13, 15]
        for _pre in range(6):
            load_xnt(_pre)
        for pos in range(16):
            bb = BB_ORDER[pos]
            for dch in range(16):
                idx = 16 * pos + dch

                @block.tensor
                def _(te, bb=bb, dch=dch, idx=idx, pos=pos):
                    te.wait_ge(lnt6[idx % 6].h, xnt_loaded[idx])
                    if dch == 0:
                        ld.wait(te, LD_WT)
                        if pos == 0:
                            sv.wait(te, V_BVREP)
                        if pos >= 1:
                            prevb = BB_ORDER[pos - 1]
                            ssc.wait(te, qkevac_done[prevb])
                            sv.wait(te, vevac_done[prevb])
                    xt = xnt[idx % 6]
                    st_ = dch == 0
                    sp = dch == 15
                    te.matmul(pb[0][:, :], wqt_all[:, 256 * dch:256 * dch + 128], xt[:, :],
                              start=st_, stop=sp)
                    te.matmul(pb[1][:, :], wqt_all[:, 256 * dch + 128:256 * dch + 256], xt[:, :],
                              start=st_, stop=sp)
                    te.matmul(pb[2][:, :], wkt_all[:, 256 * dch:256 * dch + 128], xt[:, :],
                              start=st_, stop=sp)
                    te.matmul(pb[3][:, :], wkt_all[:, 256 * dch + 128:256 * dch + 256], xt[:, :],
                              start=st_, stop=sp)
                    lastmm = None
                    for sb2 in range(4):
                        lastmm = te.matmul(pb[4 + sb2][:, 0:256], xt[:, 128 * sb2:128 * (sb2 + 1)],
                                           wvt_all[:, 256 * dch:256 * (dch + 1)], start=st_, stop=sp)
                    mm_done[idx] = ste.inc(lastmm)
                if idx + 6 < 256:
                    load_xnt(idx + 6)

            @block.scalar
            def _(sc, bb=bb):
                ste.wait(sc, mm_done[16 * bb + 15])
                for hh, pbq in ((0, pb[0]), (1, pb[1])):
                    sc.activation(out=qt[:, S * hh + 512 * bb:S * hh + 512 * (bb + 1)],
                                  in_=pbq[:, :], func=AF.Identity,
                                  bias=bqsb[:, hh:hh + 1], scale=1.0)
                last = None
                for hh, pbk in ((0, pb[2]), (1, pb[3])):
                    last = sc.activation(out=kt[:, S * hh + 512 * bb:S * hh + 512 * (bb + 1)],
                                         in_=pbk[:, :], func=AF.Identity,
                                         bias=bksb[:, hh:hh + 1], scale=1.0)
                ssc.inc(last)
            qkevac_done[bb] = ssc.v

            @block.vector
            def _(v, bb=bb):
                ssc.wait(v)
                sv.wait(v, V_BVREP)
                last = None
                for sb2 in range(4):
                    cj = 4 * bb + sb2
                    for hh in range(2):
                        last = v.tensor_tensor(
                            out=vsb[:, S * hh + 128 * cj:S * hh + 128 * (cj + 1)],
                            in0=pb[4 + sb2][:, 128 * hh:128 * (hh + 1)],
                            in1=bvrep[:, 128 * hh:128 * (hh + 1)], op=ALU.add)
                sv.inc(last)
                vevac_done[bb] = sv.v
            evac_done[bb] = sv.v
        V_PROJ = sv.v

        @block.sync
        def _(s):
            s.wait_ge(cc.h, CC_STATS)
            sld.dma(s.dma_start(out=stats_sb[:, :], in_=stats_out[:, :]))
        LD_STATS = sld.v

        @block.vector
        def _(v):
            v.wait_ge(sld.h, LD_STATS)
            v.tensor_scalar(out=gsc[0:1, 0:1], in0=stats_sb[0:1, 0:1], scalar1=INV_SD,
                            scalar2=0.5, op0=ALU.mult, op1=ALU.is_gt)
            v.drain()
            v.tensor_scalar(out=gsc[0:1, 1:2], in0=gsc[0:1, 0:1], scalar1=NEG,
                            scalar2=None, op0=ALU.mult)
            for c2 in range(3):
                v.tensor_scalar(out=gsc[0:1, 2 + c2:3 + c2], in0=stats_sb[0:1, 1 + c2:2 + c2],
                                scalar1=INV_WN, scalar2=EPS_Q, op0=ALU.mult, op1=ALU.max)
            v.drain()
            sv.inc(v.reciprocal(out=gsc[0:1, 5:8], in_=gsc[0:1, 2:5]))
        V_G = sv.v

        if debug:
            @block.gpsimd
            def _(g):
                sv.wait(g, V_G)
                stm.dma(g.dma_start(out=dbg["stats"][:, :], in_=stats_sb[:, :]))
                stm.dma(g.dma_start(out=dbg["gsc"][:, :], in_=gsc[:, :]))



        for ti, src in enumerate((qt, qt, kt, kt)):
            hh = ti % 2
            for c16 in range(16):
                ni = 16 * ti + c16
                nrm_pb = pb[4 + ni % 4]

                @block.scalar
                def _(sc, ti=ti, hh=hh, c16=c16, ni=ni, src=src):
                    if ni == 0:
                        ssc.wait(sc)
                        sv.wait(sc, V_PROJ)
                    if ni >= 4:
                        ste.wait(sc, nrm_mm[ni - 4])
                    ssc.inc(sc.activation(out=sqsc[ni % 4][:, :],
                                          in_=src[:, S * hh + 512 * c16:S * hh + 512 * (c16 + 1)],
                                          func=AF.Square, bias=0.0, scale=1.0))

                @block.tensor
                def _(te, ni=ni, nrm_pb=nrm_pb):
                    ssc.wait(te)
                    if ni == 0:
                        sv.wait(te, V_PROJ)
                    if ni >= 4:
                        sv.wait(te, nrm_rd[ni - 4])
                    nrm_mm[ni] = ste.inc(te.matmul(nrm_pb[0:16, :], onesw[:, :],
                                                   sqsc[ni % 4][:, :], start=True, stop=True))

                @block.vector
                def _(v, ti=ti, ni=ni, nrm_pb=nrm_pb):
                    ste.wait(v, nrm_mm[ni])
                    if ni == 0:
                        sv.wait(v, V_NRM0)
                    v.drain()
                    v.reduce_max(out=nrmw[:, 4:5], in_=nrm_pb[0:16, :], axis=AX.X)
                    v.drain()
                    nrm_rd[ni] = sv.inc(v.tensor_tensor(out=nrmw[:, ti:ti + 1],
                                                        in0=nrmw[:, ti:ti + 1],
                                                        in1=nrmw[:, 4:5], op=ALU.max))
        V_PROJ = sv.v

        @block.tensor
        def _(te):
            sv.wait(te, V_PROJ)
            ste.inc(te.matmul(pb[0][0:4, 0:16], nrmw[:, 0:4], identf[0:16, 0:16],
                              start=True, stop=True))
        TE_NRMT = ste.v

        @block.vector
        def _(v):
            ste.wait(v, TE_NRMT)
            v.tensor_copy(out=nrmt[:, 0:16], in_=pb[0][0:4, 0:16])
            v.drain()
            sv.inc(v.reduce_max(out=nrmt[:, 16:17], in_=nrmt[:, 0:16], axis=AX.X))
        V_NRMF = sv.v

        @block.tensor
        def _(te):
            sv.wait(te, V_NRMF)
            ste.inc(te.matmul(pb[1][0:1, 0:4], nrmt[:, 16:17], identf[0:4, 0:4],
                              start=True, stop=True))
        TE_NRMR = ste.v

        @block.vector
        def _(v):
            ste.wait(v, TE_NRMR)
            sv.inc(v.tensor_copy(out=nrm[0:1, 0:4], in_=pb[1][0:1, 0:4]))
        V_PROJ = sv.v

        @block.tensor
        def _(te):
            sv.wait(te, V_PROJ)
            for i in range(3):
                ste.inc(te.matmul(pb[1][:, i:i + 1], onesr_f[:, :], gsc[0:1, 5 + i:6 + i],
                                  start=True, stop=True))
            ste.inc(te.matmul(pb[1][:, 3:4], onesr_f[:, :], gsc[0:1, 0:1],
                              start=True, stop=True))
        TE_SWB = ste.v

        @block.vector
        def _(v):
            ste.wait(v, TE_SWB)
            sv.inc(v.tensor_copy(out=swb[:, 0:4], in_=pb[1][:, 0:4]))
        V_SWB = sv.v

        @block.scalar
        def _(sc):
            sv.wait(sc, V_PROJ)
            ssc.inc(sc.sqrt(out=nrm[0:1, 0:4], in_=nrm[0:1, 0:4]))

        @block.vector
        def _(v):
            ssc.wait(v)
            v.tensor_tensor(out=mhat[0:1, 0:2], in0=nrm[0:1, 0:2], in1=nrm[0:1, 2:4], op=ALU.mult)
            v.drain()
            sv.inc(v.tensor_scalar(out=mhat[0:1, 0:2], in0=mhat[0:1, 0:2],
                                   scalar1=-SC_ATTN, scalar2=None, op0=ALU.mult))
        V_MHAT = sv.v

        @block.tensor
        def _(te):
            sv.wait(te, V_MHAT)
            ste.inc(te.matmul(pb[6][:, 0:2], onesr_f[:, :], mhat[0:1, 0:2], start=True, stop=True))
            ste.inc(te.matmul(pb[6][:, 2:3], onesr_f[:, :], gsc[0:1, 1:2], start=True, stop=True))
        TE_EB = ste.v

        @block.vector
        def _(v):
            ste.wait(v, TE_EB)
            v.tensor_copy(out=ebA[:, :], in_=pb[6][:, 0:2])
            v.drain()
            for hh in range(2):
                v.tensor_tensor(out=ebB[:, hh:hh + 1], in0=ebA[:, hh:hh + 1],
                                in1=pb[6][:, 2:3], op=ALU.add)
            v.drain()
            v.tensor_copy(out=ebA0[:, :], in_=ebA[:, :])
            v.tensor_copy(out=ebB0[:, :], in_=ebB[:, :])
            v.drain()
            v.tensor_scalar(out=ebA0[0:1, :], in0=ebA0[0:1, :], scalar1=NEG,
                            scalar2=None, op0=ALU.add)
            sv.inc(v.tensor_scalar(out=ebB0[0:1, :], in0=ebB0[0:1, :], scalar1=NEG,
                                   scalar2=None, op0=ALU.add))
        V_EB = sv.v

        @block.gpsimd
        def _(g):
            sv.wait(g, V_PROJ)
            stm.dma(g.dma_start(out=v_raw[:, :], in_=vsb[:, :]))
        STM_VRAW = stm.v

        @block.vector
        def _(v):
            sv.wait(v, V_SWB)
            sv.inc(v.tensor_scalar(out=swb[:, 4:5], in0=swb[:, 3:4], scalar1=-1.0,
                                   scalar2=1.0, op0=ALU.mult, op1=ALU.add))
        V_GB = sv.v

        atb_done = 0
        for i2 in range(NT):
            @block.sync
            def _(s, i2=i2):
                if i2 >= 1:
                    sv.wait(s, atb_done)
                lat.dma(s.dma_start(out=atscr[:, 0:512], in_=atl_loc[128 * i2:128 * (i2 + 1), :]))
                lat.dma(s.dma_start(out=atscr[:, 512:1024], in_=atl_str[128 * i2:128 * (i2 + 1), :]))
            ld_at = lat.v

            @block.vector
            def _(v, i2=i2, ld_at=ld_at):
                v.wait_ge(lat.h, ld_at)
                sv.wait(v, V_GB)
                v.tensor_scalar(out=atscr[:, 0:512], in0=atscr[:, 0:512],
                                scalar1=swb[:, 3:4], scalar2=None, op0=ALU.mult)
                v.tensor_scalar(out=atb[:, 512 * i2:512 * (i2 + 1)], in0=atscr[:, 512:1024],
                                scalar1=swb[:, 4:5], scalar2=None, op0=ALU.mult)
                v.drain()
                sv.inc(v.tensor_tensor(out=atb[:, 512 * i2:512 * (i2 + 1)],
                                       in0=atb[:, 512 * i2:512 * (i2 + 1)],
                                       in1=atscr[:, 0:512], op=ALU.add))
            atb_done = sv.v
        V_ATB = sv.v

        q_chunks = []
        for widx, (wsrc, wdst, nrows, ncols) in enumerate(
                ((w1_sh, w1q_in, 256, FF), (w2_sh, w2q_in, 256, FF), (w3_sh, w3q_in, FSH, D))):
            for r0 in range(0, nrows, 128):
                nr = min(128, nrows - r0)
                for c0 in range(0, ncols, 512):
                    wd = min(512, ncols - c0)
                    q_chunks.append((widx, wsrc, wdst, r0, nr, c0, wd))
        QLD = [0] * len(q_chunks)
        QCP = [0] * len(q_chunks)
        QST = [0] * len(q_chunks)
        for ci, (widx, wsrc, wdst, r0, nr, c0, wd) in enumerate(q_chunks):
            @block.sync
            def _(s2, ci=ci, widx=widx, wsrc=wsrc, wdst=wdst, r0=r0, nr=nr, c0=c0, wd=wd):
                if ci >= 2:
                    s2.wait_ge(gwq[ci % 2].h, QST[ci - 2])
                QLD[ci] = swq[ci % 2].dma(s2.dma_start(
                    out=qin2[0][0:nr, 512 * (ci % 2):512 * (ci % 2) + wd],
                    in_=wsrc[r0:r0 + nr, c0:c0 + wd]))

            @block.vector
            def _(v, ci=ci, widx=widx, nr=nr, wd=wd):
                if ci == 0:
                    sv.wait(v, V_SWB)
                v.wait_ge(swq[ci % 2].h, QLD[ci])
                co = 512 * (ci % 2)
                v.tensor_scalar(out=qin2[0][0:nr, co:co + wd], in0=qin2[0][0:nr, co:co + wd],
                                scalar1=swb[0:nr, widx:widx + 1], scalar2=MAGIC,
                                op0=ALU.mult, op1=ALU.add)
                v.drain()
                v.tensor_scalar(out=qin2[0][0:nr, co:co + wd], in0=qin2[0][0:nr, co:co + wd],
                                scalar1=MAGIC, scalar2=None, op0=ALU.subtract)
                v.drain()
                QCP[ci] = sv.inc(v.tensor_scalar(out=qin2[0][0:nr, co:co + wd],
                                                 in0=qin2[0][0:nr, co:co + wd],
                                                 scalar1=1.0, scalar2=-1.0,
                                                 op0=ALU.min, op1=ALU.max))

            @block.gpsimd
            def _(g, ci=ci, wdst=wdst, r0=r0, nr=nr, c0=c0, wd=wd):
                sv.wait(g, QCP[ci])
                QST[ci] = gwq[ci % 2].dma(g.dma_start(
                    out=wdst[r0:r0 + nr, c0:c0 + wd],
                    in_=qin2[0][0:nr, 512 * (ci % 2):512 * (ci % 2) + wd]))

        @block.gpsimd
        def _(g):
            gwq[0].wait(g)
            gwq[1].wait(g)
            cc.inc(g.collective_compute(
                "AllGather", ALU.bypass, replica_groups=[list(range(NCORES))],
                ins=[w1q_in.ap().opt()], outs=[w1q.ap().opt()]))
            cc.inc(g.collective_compute(
                "AllGather", ALU.bypass, replica_groups=[list(range(NCORES))],
                ins=[w2q_in.ap().opt()], outs=[w2q.ap().opt()]))
            cc.inc(g.collective_compute(
                "AllGather", ALU.bypass, replica_groups=[list(range(NCORES))],
                ins=[w3q_in.ap().opt()], outs=[w3q.ap().opt()]))
        CC_W = cc.v


        stp2a.close()
        aoT = stp2.enter_context(nc.sbuf_tensor("aoT", [128, 2 * S], BF16))
        rcol = stp2.enter_context(nc.sbuf_tensor("rcol", [128, 128], F32))
        stp3 = contextlib.ExitStack()
        l_sb = stp3.enter_context(nc.sbuf_tensor("l_sb", [1, S], F32))
        scr8k = stp3.enter_context(nc.sbuf_tensor("scr8k", [128, S], BF16))
        PT = [stp3.enter_context(nc.sbuf_tensor(f"PT{i}", [128, 512], BF16)) for i in range(4)]
        PTA = [stp3.enter_context(nc.sbuf_tensor(f"PTA{i}", [128, 512], BF16)) for i in range(4)]
        Pq0 = stp3.enter_context(nc.sbuf_tensor("Pq0", [128, 16], BF16))
        q0row = stp3.enter_context(nc.sbuf_tensor("q0row", [1, 128], F32))

        strip_by = {}
        for (rho_, j_, alo_, ahi_, svid_) in strips:
            strip_by[(rho_, j_)] = (svid_, alo_, ahi_)
        piece_by = {}
        for (rhk_, kb_, vid_) in pieces:
            piece_by[(rhk_, kb_)] = vid_
        kbs_all = sorted({kb_ for (_, kb_, _) in pieces})

        @block.vector
        def _(v):
            sv.wait(v, V_EB)
            ssc.wait(v)
            ste.wait(v)
            sv.inc(v.memset(aoT[:, :], 0.0))
        V_P3INIT = sv.v

        for h in range(2):
            hS = h * S

            @block.vector
            def _(v, h=h):
                if h == 0:
                    sv.wait(v, V_P3INIT)
                else:
                    ssc.wait(v)
                    ste.wait(v)
                    stm.wait(v)
                sv.inc(v.memset(l_sb[:, :], 0.0))
            V_LINIT = sv.v
            dp_mm = [0] * 16
            dp_ex = [0] * 16
            for grp in range(16):
                @block.tensor
                def _(te, h=h, grp=grp):
                    if grp <= 3:
                        if h == 0:
                            sv.wait(te, V_P3INIT)
                        else:
                            sv.wait(te)
                            ssc.wait(te)
                    else:
                        ssc.wait(te, dp_ex[grp - 4])
                    dp_mm[grp] = ste.inc(te.matmul(pb[4 + grp % 4][0:1, 0:512], kt[:, hS:hS + 1],
                                                   qt[:, hS + 512 * grp:hS + 512 * (grp + 1)],
                                                   start=True, stop=True))

                @block.scalar
                def _(sc, h=h, grp=grp):
                    ste.wait(sc, dp_mm[grp])
                    dp_ex[grp] = ssc.inc(sc.activation(
                        out=scr8k[0:1, 512 * grp:512 * (grp + 1)],
                        in_=pb[4 + grp % 4][0:1, 0:512], func=AF.Exp,
                        bias=ebA[0:1, h:h + 1], scale=SC_ATTN))
            D_PREP = ssc.v

            exp_all = {}
            bev_sc = {}
            bev_v = {}
            for rho in range(4):
                rA = rho * A
                for g2 in range(4):
                    nC = rho * 4 + g2
                    po = pb[3]
                    pl = pb[4]
                    qc0 = hS + rA + 512 * g2

                    @block.tensor
                    def _(te, h=h, rho=rho, g2=g2, nC=nC, po=po, pl=pl, qc0=qc0):
                        if nC == 0:
                            ssc.wait(te, D_PREP)
                            sv.wait(te)
                        else:
                            ssc.wait(te, bev_sc[nC - 1])
                            sv.wait(te, bev_v[nC - 1])
                        te.matmul(po[:, :], vsb[0:1, hS:hS + 128],
                                  scr8k[0:1, rA + 512 * g2:rA + 512 * (g2 + 1)],
                                  start=True, stop=False)
                        ste.inc(te.matmul(pl[0:1, 0:512], one_bf[:, :],
                                          scr8k[0:1, rA + 512 * g2:rA + 512 * (g2 + 1)],
                                          start=True, stop=False))

                    exp_ct = {}

                    def emit_qk(j):
                        jj = 16 * nC + j
                        bk = (0, 1, 2, 5)[jj % 4]
                        pt = PT[jj % 4]

                        @block.tensor
                        def _(te, h=h, rho=rho, g2=g2, nC=nC, j=j, jj=jj, bk=bk, qc0=qc0):
                            if jj >= 4:
                                ssc.wait(te, exp_all[jj - 4])
                            elif bk == 5:
                                ssc.wait(te, D_PREP)
                            mmq = te.matmul(pb[bk][:, :],
                                            kt[:, hS + rA + 128 * j:hS + rA + 128 * (j + 1)],
                                            qt[:, qc0:qc0 + 512], start=True, stop=True)
                            if rho > 0 and g2 == 0 and "q0" not in P3_SKIP:
                                mmq = te.matmul(pb[6][:, j:j + 1],
                                                kt[:, hS + rA + 128 * j:hS + rA + 128 * (j + 1)],
                                                qt[:, hS:hS + 1], start=True, stop=True)
                            ste.inc(mmq)

                        sb_info = strip_by.get((rho, j))
                        ovl = None
                        if sb_info:
                            svid_, alo_, ahi_ = sb_info
                            lo = max(alo_, 512 * g2)
                            hi = min(ahi_, 512 * (g2 + 1))
                            if lo < hi:
                                ovl = (svid_, alo_, lo, hi)
                        if ovl is not None:
                            @block.vector
                            def _(v, ovl=ovl, g2=g2, bk=bk):
                                svid_, alo_, lo, hi = ovl
                                ste.wait(v)
                                sv.inc(v.tensor_tensor(
                                    out=pb[bk][:, lo - 512 * g2:hi - 512 * g2],
                                    in0=pb[bk][:, lo - 512 * g2:hi - 512 * g2],
                                    in1=ssb[:, SW * svid_ + lo - alo_:SW * svid_ + hi - alo_],
                                    op=ALU.add))
                        strip_v = sv.v if ovl is not None else None

                        @block.scalar
                        def _(sc, h=h, rho=rho, g2=g2, j=j, bk=bk, pt=pt, strip_v=strip_v):
                            if strip_v is not None:
                                sv.wait(sc, strip_v)
                            else:
                                ste.wait(sc)
                            eb_main = ebB0 if (rho == 0 and j == 0) else ebB
                            if rho == 0 and g2 == 0:
                                eb_c0 = ebA0 if j == 0 else ebA
                                sc.activation(out=pt[:, 1:512], in_=pb[bk][:, 1:512],
                                              func=AF.Exp, bias=eb_main[:, h:h + 1], scale=SC_ATTN)
                                ssc.inc(sc.activation(out=pt[:, 0:1], in_=pb[bk][:, 0:1],
                                                      func=AF.Exp, bias=eb_c0[:, h:h + 1],
                                                      scale=SC_ATTN))
                            else:
                                ssc.inc(sc.activation(out=pt[:, :], in_=pb[bk][:, :],
                                                      func=AF.Exp, bias=eb_main[:, h:h + 1],
                                                      scale=SC_ATTN))
                        exp_ct[j] = ssc.v
                        exp_all[jj] = ssc.v

                    def emit_av(j):
                        pt = PT[(16 * nC + j) % 4]

                        @block.tensor
                        def _(te, h=h, rho=rho, g2=g2, j=j, pt=pt, po=po, pl=pl):
                            ssc.wait(te, exp_ct[j])
                            te.matmul(po[:, :], vsb[:, hS + 128 * (16 * rho + j):hS + 128 * (16 * rho + j + 1)],
                                      pt[:, :], start=False, stop=(j == 15))
                            ste.inc(te.matmul(pl[0:1, 0:512], ones_bf[:, :], pt[:, :],
                                              start=False, stop=(j == 15)))

                    for j in range(16):
                        if j >= 4:
                            emit_av(j - 4)
                        emit_qk(j)
                    for jf in (12, 13, 14, 15):
                        emit_av(jf)

                    @block.scalar
                    def _(sc, h=h, rho=rho, g2=g2, pl=pl, qc0=qc0):
                        ste.wait(sc)
                        ssc.inc(sc.copy(out=l_sb[0:1, qc0 - hS:qc0 - hS + 512], in_=pl[0:1, 0:512]))
                    bev_sc[nC] = ssc.v

                    @block.vector
                    def _(v, h=h, rho=rho, g2=g2, po=po, qc0=qc0):
                        ste.wait(v)
                        sv.inc(v.tensor_copy(out=aoT[:, qc0:qc0 + 512], in_=po[:, :]))
                    bev_v[nC] = sv.v

                if rho > 0 and "q0" not in P3_SKIP:
                    @block.vector
                    def _(v, h=h, rho=rho):
                        ste.wait(v)
                        sv.inc(v.tensor_scalar(out=pb[6][0:16, 0:1], in0=pb[6][0:16, 0:1],
                                               scalar1=NEG, scalar2=None, op0=ALU.add))
                    V_Q0NEG = sv.v

                    @block.scalar
                    def _(sc, h=h, rho=rho):
                        sv.wait(sc, V_Q0NEG)
                        ssc.inc(sc.activation(out=Pq0[:, :], in_=pb[6][:, 0:16], func=AF.Exp,
                                              bias=ebA[:, h:h + 1], scale=SC_ATTN))
                    SC_PQ0 = ssc.v

                    @block.tensor
                    def _(te, h=h, rho=rho):
                        ssc.wait(te, SC_PQ0)
                        for j in range(16):
                            te.matmul(pb[6][0:1, 256:384],
                                      Pq0[:, j:j + 1],
                                      vsb[:, hS + 128 * (16 * rho + j):hS + 128 * (16 * rho + j + 1)],
                                      start=(j == 0), stop=(j == 15))
                        ste.inc(te.matmul(pb[6][0:1, 384:400], ones_bf[:, :], Pq0[:, :],
                                          start=True, stop=True))
                    TE_Q0 = ste.v

                    @block.scalar
                    def _(sc, h=h, rho=rho):
                        ste.wait(sc, TE_Q0)
                        ssc.inc(sc.copy(out=q0row[0:1, :], in_=pb[6][0:1, 256:384]))
                    SC_Q0R = ssc.v

                    @block.vector
                    def _(v, h=h, rho=rho):
                        ste.wait(v, TE_Q0)
                        v.reduce_sum(out=nrm[0:1, 7:8], in_=pb[6][0:1, 384:400], axis=AX.X)
                        v.drain()
                        sv.inc(v.tensor_tensor(out=l_sb[0:1, 0:1], in0=l_sb[0:1, 0:1],
                                               in1=nrm[0:1, 7:8], op=ALU.add))

                    @block.tensor
                    def _(te, h=h, rho=rho):
                        ssc.wait(te, SC_Q0R)
                        sv.wait(te)
                        ste.inc(te.matmul(pb[6][:, 400:401], q0row[:, :], one_f[:, :],
                                          start=True, stop=True))
                    TE_Q0T = ste.v

                    @block.vector
                    def _(v, h=h, rho=rho):
                        ste.wait(v, TE_Q0T)
                        sv.inc(v.tensor_tensor(out=aoT[:, hS:hS + 1], in0=aoT[:, hS:hS + 1],
                                               in1=pb[6][:, 400:401], op=ALU.add))

            @block.sync
            def _(s2, h=h, hS=hS):
                ssc.wait(s2)
                ste.wait(s2)
                s2.wait_ge(stm.h, STM_VRAW)
                lnt[h].dma(s2.dma_start(out=scr8k[0:64, :], in_=v_raw[64:128, hS:hS + S]))
                lnt[h].dma(s2.dma_start(out=scr8k[64:128, 0:S - 128],
                                        in_=v_raw[0:64, hS + 128:hS + S]))
            LNT_VSH = lnt[h].v

            kbs_use = [] if "A" in P3_SKIP else [kb for kb in kbs_all
                                                 if not (kb % 128 != 0 and "Ashift" in P3_SKIP)]
            apieces = [(pi, kb, rhk) for pi, kb in enumerate(kbs_use) for rhk in range(4)]
            a_exp = {}
            a_av = {}
            a_rmw = {}

            def a_qk(ip):
                pi, kb, rhk = apieces[ip]
                vid = piece_by[(rhk, kb)]

                @block.tensor
                def _(te, h=h, kb=kb, rhk=rhk, vid=vid, ip=ip):
                    if ip >= 3:
                        ssc.wait(te, a_exp[ip - 3])
                    else:
                        sv.wait(te)
                    if ip == 0:
                        sv.wait(te, V_ATB)
                        te.wait_ge(lnt[h].h, LNT_VSH)
                    qsl = qt.ap().rearrange("p (hh r a) -> p hh r a", hh=2, r=4)[:, h, :, kb:kb + 128]
                    te.matmul(pb[ip % 3][:, :], identb[:, :], atb[:, 512 * vid:512 * (vid + 1)],
                              start=True, stop=False)
                    ste.inc(te.matmul(pb[ip % 3][:, :],
                                      kt[:, hS + rhk * A + kb:hS + rhk * A + kb + 128],
                                      qsl, start=False, stop=True))

                @block.scalar
                def _(sc, h=h, ip=ip):
                    ste.wait(sc)
                    ssc.inc(sc.activation(out=PTA[ip % 4][:, :], in_=pb[ip % 3][:, :],
                                          func=AF.Exp, bias=ebA[:, h:h + 1], scale=SC_ATTN))
                a_exp[ip] = ssc.v

            def a_avf(ip):
                pi, kb, rhk = apieces[ip]
                pav = pb[3] if pi % 2 == 0 else pb[5]
                pla = pb[4] if pi % 2 == 0 else pb[7]

                @block.tensor
                def _(te, h=h, kb=kb, rhk=rhk, pi=pi, ip=ip, pav=pav, pla=pla):
                    ssc.wait(te, a_exp[ip])
                    if rhk == 0 and pi >= 2:
                        sv.wait(te, a_rmw[pi - 2])
                    cj = 16 * rhk + kb // 128
                    st_ = rhk == 0
                    sp = rhk == 3
                    if kb % 128 == 0:
                        te.matmul(pav[:, :], vsb[:, hS + 128 * cj:hS + 128 * (cj + 1)],
                                  PTA[ip % 4][:, :], start=st_, stop=sp)
                    else:
                        te.matmul(pav[:, :], scr8k[:, 128 * cj:128 * (cj + 1)],
                                  PTA[ip % 4][:, :], start=st_, stop=sp)
                    ste.inc(te.matmul(pla[0:1, 0:512], ones_bf[:, :], PTA[ip % 4][:, :],
                                      start=st_, stop=sp))
                a_av[ip] = ste.v

            def a_rmwf(ip):
                pi, kb, rhk = apieces[ip]
                if rhk != 3:
                    return
                pav = pb[3] if pi % 2 == 0 else pb[5]
                pla = pb[4] if pi % 2 == 0 else pb[7]

                @block.vector
                def _(v, h=h, kb=kb, pi=pi, ip=ip, pav=pav, pla=pla):
                    ste.wait(v, a_av[ip])
                    v.drain()
                    ao3 = aoT.ap().rearrange("p (hh r a) -> p hh r a", hh=2, r=4)[:, h, :, kb:kb + 128]
                    v.tensor_tensor(out=ao3, in0=ao3, in1=pav[:, :], op=ALU.add)
                    l3 = l_sb.ap().rearrange("o (r a) -> o r a", r=4)[:, :, kb:kb + 128]
                    pl3 = pla[0:1, 0:512].rearrange("o (r a) -> o r a", r=4)
                    sv.inc(v.tensor_tensor(out=l3, in0=l3, in1=pl3, op=ALU.add))
                a_rmw[pi] = sv.v

            for ip in range(len(apieces)):
                a_qk(ip)
                if ip >= 3:
                    a_avf(ip - 3)
                    a_rmwf(ip - 3)
            for ip in range(max(0, len(apieces) - 3), len(apieces)):
                a_avf(ip)
                a_rmwf(ip)
            if debug:
                @block.gpsimd
                def _(g, h=h, hS=hS):
                    sv.wait(g)
                    stm.dma(g.dma_start(out=dbg["l"][0:1, hS:hS + S], in_=l_sb[0:1, :]))

            for bq2 in range(4):
                @block.tensor
                def _(te, h=h, bq2=bq2):
                    sv.wait(te)
                    ssc.wait(te)
                    lastm = None
                    for b2 in range(16):
                        b3 = 16 * bq2 + b2
                        lastm = te.matmul(pb[6][:, 416 + b2:417 + b2],
                                          l_sb[0:1, 128 * b3:128 * (b3 + 1)],
                                          one_f[:, :], start=True, stop=True)
                    ste.inc(lastm)

                @block.scalar
                def _(sc, h=h, bq2=bq2):
                    ste.wait(sc)
                    ssc.inc(sc.copy(out=rcol[:, 64 * h + 16 * bq2:64 * h + 16 * (bq2 + 1)],
                                    in_=pb[6][:, 416:432]))
            RC_H = ssc.v

        @block.vector
        def _(v):
            ssc.wait(v)
            v.drain()
            sv.inc(v.reciprocal(out=rcol[:, :], in_=rcol[:, :]))
        V_ATTN = sv.v

        stp3.close()
        OP_EV = [0] * 64
        OP_V1 = [0] * 64
        woT_sb = stp2.enter_context(nc.sbuf_tensor("woT_sb", [128, 2 * D], BF16))
        obuf2 = [stp2.enter_context(nc.sbuf_tensor(f"obuf{i}", [128, D], F32)) for i in range(2)]

        @block.sync
        def _(s2):
            s2.wait_ge(gld.h, LD_WCAST)
            sv.wait(s2, V_ATTN)
            ssc.wait(s2)
            ste.wait(s2)
            stm.wait(s2)
            for q_ in lnt6:
                q_.wait(s2)
            ld.dma(s2.dma_start(out=woT_sb[:, 0:D], in_=wo_bf[0:128, :]))
            ld.dma(s2.dma_start(out=woT_sb[:, D:2 * D], in_=wo_bf[128:256, :]))
        LD_WOT = ld.v

        OP_SC0 = [0] * 64
        OP_TE1 = [0] * 64
        for bb in range(64):
            obuf = obuf2[bb % 2]

            @block.tensor
            def _(te, bb=bb):
                if bb == 0:
                    sv.wait(te, V_ATTN)
                    te.wait_ge(ld.h, LD_WOT)
                    stm.wait(te)
                else:
                    ssc.wait(te, OP_SC0[bb - 1])
                last = None
                for g3 in range(4):
                    last = te.matmul(pb[g3][:, :], aoT[:, 128 * bb:128 * (bb + 1)],
                                     woT_sb[:, 512 * g3:512 * (g3 + 1)], start=True, stop=True)
                ste.inc(last)
            TE_OP0 = ste.v

            @block.tensor
            def _(te, bb=bb):
                if bb >= 1:
                    sv.wait(te, OP_V1[bb - 1])
                last = None
                for g3 in range(4):
                    last = te.matmul(pb[4 + g3][:, :], aoT[:, S + 128 * bb:S + 128 * (bb + 1)],
                                     woT_sb[:, D + 512 * g3:D + 512 * (g3 + 1)],
                                     start=True, stop=True)
                ste.inc(last)
            OP_TE1[bb] = ste.v

            @block.scalar
            def _(sc, bb=bb, TE_OP0=TE_OP0, obuf=obuf):
                ste.wait(sc, TE_OP0)
                if bb >= 2:
                    sc.wait_ge(stx[bb % 2].h, OP_EV[bb - 2])
                last = None
                for g3 in range(4):
                    last = sc.mul(out=obuf[:, 512 * g3:512 * (g3 + 1)], in_=pb[g3][:, :],
                                  mul=rcol[:, bb:bb + 1])
                ssc.inc(last)
            OP_SC0[bb] = ssc.v

            @block.vector
            def _(v, bb=bb, obuf=obuf):
                ste.wait(v, OP_TE1[bb])
                ssc.wait(v, OP_SC0[bb])
                last = None
                for g3 in range(4):
                    last = v.scalar_tensor_tensor(
                        out=obuf[:, 512 * g3:512 * (g3 + 1)], in0=pb[4 + g3][:, :],
                        scalar=rcol[:, 64 + bb:64 + bb + 1],
                        in1=obuf[:, 512 * g3:512 * (g3 + 1)], op0=ALU.mult, op1=ALU.add)
                sv.inc(last)
            OP_V1[bb] = sv.v

            @block.gpsimd
            def _(g, bb=bb, obuf=obuf):
                sv.wait(g, OP_V1[bb])
                OP_EV[bb] = stx[bb % 2].dma(g.dma_start(out=aop_d[128 * bb:128 * (bb + 1), :],
                                                        in_=obuf[:, :]))

        @block.gpsimd
        def _(g):
            stm.wait(g)
            stx[0].wait(g)
            stx[1].wait(g)
            cc.inc(g.collective_compute(
                "ReduceScatter", ALU.add, replica_groups=[list(range(NCORES))],
                ins=[aop_d.ap().opt()], outs=[aor_d.ap().opt()]))
        CC_RS = cc.v

        if debug:
            @block.gpsimd
            def _(g):
                sv.wait(g)
                ssc.wait(g)
                ste.wait(g)
                stm.dma(g.dma_start(out=dbg["qt"][:, :], in_=qt[:, :]))
                stm.dma(g.dma_start(out=dbg["kt"][:, :], in_=kt[:, :]))
                stm.dma(g.dma_start(out=dbg["vsb"][:, :], in_=vsb[:, :]))
                stm.dma(g.dma_start(out=dbg["nrm"][:, :], in_=nrm[:, :]))
                stm.dma(g.dma_start(out=dbg["mhat"][:, :], in_=mhat[:, :]))
                stm.dma(g.dma_start(out=dbg["ebA"][:, :], in_=ebA[:, :]))
                stm.dma(g.dma_start(out=dbg["ebB"][:, :], in_=ebB[:, :]))
                if "aoT" in dbg:
                    stm.dma(g.dma_start(out=dbg["aoT"][:, :], in_=aoT[:, :]))
                    stm.dma(g.dma_start(out=dbg["rcol"][:, :], in_=rcol[:, :]))

        XQ3_Q = [0] * NBLK
        XQ3_T = [0] * NBLK
        MM3 = {}
        FIN_LD = {}
        stp2.close()
        stp4 = contextlib.ExitStack()
        macc = stp4.enter_context(nc.sbuf_tensor("macc", [128, NBLK], F32))
        sacc2 = stp4.enter_context(nc.sbuf_tensor("sacc2", [128, NBLK], F32))
        mwb = stp4.enter_context(nc.sbuf_tensor("mwb", [128, 3], F32))
        os1w = stp4.enter_context(nc.sbuf_tensor("os1w", [128, NBLK], F32))
        q3v = stp4.enter_context(nc.sbuf_tensor("q3v", [128, NBLK], F32))
        os3w = stp4.enter_context(nc.sbuf_tensor("os3w", [128, NBLK], F32))
        borep = stp4.enter_context(nc.sbuf_tensor("borep", [128, D], F32))
        bosb = stp4.enter_context(nc.sbuf_tensor("bosb", [1, D], F32))
        stmp = [stp4.enter_context(nc.sbuf_tensor(f"stmp{i}", [128, 1], F32)) for i in range(2)]
        stmp2 = [stp4.enter_context(nc.sbuf_tensor(f"stmp2_{i}", [128, 1], F32)) for i in range(2)]

        @block.sync
        def _(s2):
            stm.wait(s2)
            stx[0].wait(s2)
            stx[1].wait(s2)
            sv.wait(s2)
            ssc.wait(s2)
            ste.wait(s2)
            sg.inc(s2.nop())
        SG_P4 = sg.v

        @block.sync
        def _(s2):
            sg.wait(s2, SG_P4)
            ld.dma(s2.dma_start(out=bosb[:, :], in_=bo_in[:, :]))
        LD_BO = ld.v

        @block.tensor
        def _(te):
            te.wait_ge(ld.h, LD_BO)
            sg.wait(te, SG_P4)
            for g3 in range(4):
                te.matmul(pb[g3][:, :], onesr_f[:, :], bosb[0:1, 512 * g3:512 * (g3 + 1)],
                          start=True, stop=True)
            for c3 in range(3):
                te.matmul(pb[4][:, c3:c3 + 1], onesr_f[:, :], gsc[0:1, 2 + c3:3 + c3],
                          start=True, stop=True)
            ste.inc(te.matmul(pb[4][:, 3:4], onesr_f[:, :], gsc[0:1, 2:3],
                              start=True, stop=True))
        TE_BOREP = ste.v

        @block.vector
        def _(v):
            ste.wait(v, TE_BOREP)
            sg.wait(v, SG_P4)
            for g3 in range(4):
                v.tensor_copy(out=borep[:, 512 * g3:512 * (g3 + 1)], in_=pb[g3][:, :])
            v.tensor_copy(out=mwb[:, :], in_=pb[4][:, 0:3])
            v.memset(macc[:, :], 0.0)
            v.memset(sacc2[:, :], 0.0)
            v.drain()
            sv.inc(v.tensor_scalar(out=os1w[:, :], in0=os1v[:, :], scalar1=mwb[:, 0:1],
                                   scalar2=None, op0=ALU.mult))
        V_P5INIT = sv.v

        stp4a = contextlib.ExitStack()
        xq1t = stp4a.enter_context(nc.sbuf_tensor("xq1t", [128, 16 * TB], BF16))
        wqrt2 = [stp4a.enter_context(nc.sbuf_tensor(f"wqrt{i}", [128, 16 * 1408], BF16)) for i in range(2)]
        gq = stp4a.enter_context(nc.sbuf_tensor("gq", [128, NBLK * 1408], BF16))
        ubuf = [stp4a.enter_context(nc.sbuf_tensor(f"ubuf{i}", [128, 1408], BF16)) for i in range(2)]
        ubuf2 = [stp4a.enter_context(nc.sbuf_tensor(f"ubuf2_{i}", [128, 1408], BF16)) for i in range(2)]
        gub = [stp4a.enter_context(nc.sbuf_tensor(f"gub{i}", [128, 1408], BF16)) for i in range(2)]

        @block.sync
        def _(s2):
            sg.wait(s2, SG_P4)
            sv.wait(s2, V_P5INIT)
            ld.dma(s2.dma_start(out=xq1t[:, :],
                                in_=xq1_d.ap().rearrange("(c p) t -> p c t", p=128)))
        LD_XQ1T = ld.v

        GUB_ST = {}
        V_GUS = {}
        GU_SQ = []

        SACC_V = []

        def emit_gusq(item):
            q4g, b4g, V_GUg = item
            gs = len(SACC_V)

            @block.scalar
            def _(sc, q4=q4g, b4=b4g, V_GU=V_GUg, gs=gs):
                sv.wait(sc, V_GU)
                if gs >= 2:
                    sv.wait(sc, SACC_V[gs - 2])
                ssc.inc(sc.activation(out=ubuf2[gs % 2][:, :], in_=gub[b4 % 2][:, :],
                                      func=AF.Square, accum_out=stmp2[gs % 2][:, :]))
            SC_SQ3 = ssc.v

            @block.vector
            def _(v, q4=q4g, b4=b4g, SC_SQ3=SC_SQ3, gs=gs):
                ssc.wait(v, SC_SQ3)
                v.drain()
                sv.inc(v.tensor_tensor(out=sacc2[:, b4:b4 + 1], in0=sacc2[:, b4:b4 + 1],
                                       in1=stmp2[gs % 2][:, :], op=ALU.add))
            SACC_V.append(sv.v)
        P5_EVS = []
        P5_TE_PASS = {}
        for q4 in range(4):
            for wpass, wsrc in ((0, w1q), (1, w2q)):
                pl = 2 * q4 + wpass
                wqrt = wqrt2[pl % 2]

                @block.sync
                def _(s2, q4=q4, wpass=wpass, wsrc=wsrc, pl=pl, wqrt=wqrt):
                    if q4 == 0 and wpass == 0:
                        s2.wait_ge(ld.h, LD_XQ1T)
                    if pl <= 1:
                        s2.wait_ge(cc.h, CC_W)
                    if pl >= 2:
                        ste.wait(s2, P5_TE_PASS[pl - 2])
                    lnt[pl % 2].dma(s2.dma_start(
                        out=wqrt[:, :],
                        in_=wsrc.ap().rearrange("(c p) f -> p c f", p=128)[:, :, 1408 * q4:1408 * (q4 + 1)]))
                LD_WQRT = lnt[pl % 2].v

                for b4 in range(NBLK):
                    pbase = 0 if b4 % 2 == 0 else 3
                    L = len(P5_EVS)

                    @block.tensor
                    def _(te, q4=q4, wpass=wpass, b4=b4, pbase=pbase, LD_WQRT=LD_WQRT,
                          pl=pl, L=L, wqrt=wqrt):
                        te.wait_ge(lnt[pl % 2].h, LD_WQRT)
                        if L >= 2:
                            ssc.wait(te, P5_EVS[L - 2])
                        last = None
                        for dch in range(16):
                            st_ = dch == 0
                            sp = dch == 15
                            lx = xq1t[:, TB * dch + 128 * b4:TB * dch + 128 * (b4 + 1)]
                            last = te.matmul(pb[pbase][:, :], lx,
                                             wqrt[:, 1408 * dch:1408 * dch + 512], start=st_, stop=sp)
                            last = te.matmul(pb[pbase + 1][:, :], lx,
                                             wqrt[:, 1408 * dch + 512:1408 * dch + 1024], start=st_, stop=sp)
                            last = te.matmul(pb[pbase + 2][:, 0:384], lx,
                                             wqrt[:, 1408 * dch + 1024:1408 * dch + 1408], start=st_, stop=sp)
                        ste.inc(last)
                    TE_BL = ste.v
                    if b4 == NBLK - 1:
                        P5_TE_PASS[pl] = TE_BL

                    if wpass == 0:
                        @block.scalar
                        def _(sc, q4=q4, b4=b4, pbase=pbase, TE_BL=TE_BL):
                            ste.wait(sc, TE_BL)
                            prevgu = V_GUS.get((q4 - 1, b4))
                            if prevgu:
                                sv.wait(sc, prevgu)
                            last = None
                            for g3, w3_ in ((0, 512), (1, 512), (2, 384)):
                                last = sc.activation(
                                    out=gq[:, 1408 * b4 + 512 * g3:1408 * b4 + 512 * g3 + w3_],
                                    in_=pb[pbase + g3][:, 0:w3_], func=AF.Silu,
                                    bias=0.0, scale=os1w[:, b4:b4 + 1])
                            ssc.inc(last)
                        P5_EVS.append(ssc.v)
                    else:
                        @block.scalar
                        def _(sc, q4=q4, b4=b4, pbase=pbase, TE_BL=TE_BL):
                            ste.wait(sc, TE_BL)
                            prevgu = V_GUS.get((q4, b4 - 2)) or V_GUS.get((q4 - 1, b4 + 6))
                            if prevgu:
                                sv.wait(sc, prevgu)
                            last = None
                            for g3, w3_ in ((0, 512), (1, 512), (2, 384)):
                                last = sc.mul(out=ubuf[b4 % 2][:, 512 * g3:512 * g3 + w3_],
                                              in_=pb[pbase + g3][:, 0:w3_],
                                              mul=os1w[:, b4:b4 + 1])
                            ssc.inc(last)
                        SC_UP = ssc.v
                        P5_EVS.append(SC_UP)

                        @block.vector
                        def _(v, q4=q4, b4=b4, SC_UP=SC_UP):
                            ssc.wait(v)
                            prevst = GUB_ST.get((q4, b4 - 2)) or GUB_ST.get((q4 - 1, b4 + 6))
                            if prevst:
                                v.wait_ge(stx[b4 % 2].h, prevst)
                            v.drain()
                            v.tensor_tensor(out=gub[b4 % 2][:, :],
                                            in0=gq[:, 1408 * b4:1408 * (b4 + 1)],
                                            in1=ubuf[b4 % 2][:, :], op=ALU.mult)
                            v.drain()
                            v.reduce_max(out=stmp[b4 % 2][:, :], in_=gub[b4 % 2][:, :], axis=AX.X,
                                         apply_absolute_value=True)
                            v.drain()
                            sv.inc(v.tensor_tensor(out=macc[:, b4:b4 + 1], in0=macc[:, b4:b4 + 1],
                                                   in1=stmp[b4 % 2][:, :], op=ALU.max))
                        V_GU = sv.v
                        V_GUS[(q4, b4)] = V_GU
                        GU_SQ.append((q4, b4, V_GU))

                        @block.gpsimd
                        def _(g, q4=q4, b4=b4, V_GU=V_GU):
                            sv.wait(g, V_GU)
                            GUB_ST[(q4, b4)] = stx[b4 % 2].dma(
                                g.dma_start(out=gu_d[128 * b4:128 * (b4 + 1),
                                                     1408 * q4:1408 * (q4 + 1)],
                                            in_=gub[b4 % 2][:, :]))

                        if len(GU_SQ) >= 2:
                            emit_gusq(GU_SQ[-2])
        if GU_SQ:
            emit_gusq(GU_SQ[-1])
        V_P5 = sv.v

        @block.sync
        def _(s2):
            sv.wait(s2)
            ssc.wait(s2)
            ste.wait(s2)
            stx[0].wait(s2)
            stx[1].wait(s2)
            sg.inc(s2.nop())
        SG_P5 = sg.v
        stp4a.close()

        stp4b = contextlib.ExitStack()
        xq3t = stp4b.enter_context(nc.sbuf_tensor("xq3t", [128, 44 * TB], BF16))
        gul = [stp4b.enter_context(nc.sbuf_tensor(f"gul{i}", [128, FF], BF16)) for i in range(2)]
        xq3b = stp4b.enter_context(nc.sbuf_tensor("xq3b", [128, FF], BF16))
        rnd3 = stp4b.enter_context(nc.sbuf_tensor("rnd3", [128, FF], F32))

        @block.vector
        def _(v):
            sg.wait(v, SG_P5)
            v.tensor_scalar(out=q3v[:, :], in0=sacc2[:, :], scalar1=INV_FF,
                            scalar2=EPS_RMS, op0=ALU.mult, op1=ALU.add)
            v.drain()
            sv.inc(v.reciprocal(out=q3v[:, :], in_=q3v[:, :]))
        V_Q3A = sv.v

        @block.scalar
        def _(sc):
            sv.wait(sc, V_Q3A)
            ssc.inc(sc.sqrt(out=q3v[:, :], in_=q3v[:, :]))
        SC_R3 = ssc.v

        @block.vector
        def _(v):
            ssc.wait(v, SC_R3)
            v.tensor_tensor(out=os3w[:, :], in0=q3v[:, :], in1=macc[:, :], op=ALU.mult)
            v.drain()
            v.tensor_scalar(out=os3w[:, :], in0=os3w[:, :], scalar1=EPS_Q,
                            scalar2=None, op0=ALU.max)
            v.drain()
            v.reciprocal(out=macc[:, :], in_=os3w[:, :])
            v.drain()
            v.tensor_scalar(out=macc[:, :], in0=macc[:, :], scalar1=127.0,
                            scalar2=None, op0=ALU.mult)
            v.drain()
            v.tensor_tensor(out=q3v[:, :], in0=q3v[:, :], in1=macc[:, :], op=ALU.mult)
            v.tensor_scalar(out=os3w[:, :], in0=os3w[:, :], scalar1=INV_127,
                            scalar2=None, op0=ALU.mult)
            v.drain()
            sv.inc(v.tensor_scalar(out=os3w[:, :], in0=os3w[:, :], scalar1=mwb[:, 2:3],
                                   scalar2=None, op0=ALU.mult))
        V_Q3 = sv.v

        GUL_LD = {}
        EV6 = []
        for b4 in range(NBLK):
            @block.sync
            def _(s2, b4=b4):
                sg.wait(s2, SG_P5)
                if b4 >= 2:
                    sv.wait(s2, XQ3_Q[b4 - 2])
                GUL_LD[b4] = lx[b4 % 2].dma(s2.dma_start(out=gul[b4 % 2][:, :],
                                                         in_=gu_d[128 * b4:128 * (b4 + 1), :]))

            @block.vector
            def _(v, b4=b4):
                v.wait_ge(lx[b4 % 2].h, GUL_LD[b4])
                sv.wait(v, V_Q3)
                if b4 >= 2:
                    ste.wait(v, XQ3_T[b4 - 2])
                v.drain()
                v.tensor_scalar(out=rnd3[:, :], in0=gul[b4 % 2][:, :], scalar1=q3v[:, b4:b4 + 1],
                                scalar2=MAGIC, op0=ALU.mult, op1=ALU.add)
                v.drain()
                sv.inc(v.tensor_scalar(out=xq3b[:, :], in0=rnd3[:, :], scalar1=MAGIC,
                                       scalar2=None, op0=ALU.subtract))
            XQ3_Q[b4] = sv.v

            for c44g in range(11):
                q6i = len(EV6)
                pbs = 0 if q6i % 2 == 0 else 4

                @block.tensor
                def _(te, b4=b4, c44g=c44g, q6i=q6i, pbs=pbs):
                    sv.wait(te, XQ3_Q[b4])
                    if q6i >= 2:
                        sv.wait(te, EV6[q6i - 2])
                    last = None
                    for c4i in range(4):
                        c44 = 4 * c44g + c4i
                        last = te.matmul(pb[pbs + c4i][:, 0:128],
                                         xq3b[:, 128 * c44:128 * (c44 + 1)],
                                         identb[:, :], start=True, stop=True)
                    ste.inc(last)

                @block.vector
                def _(v, b4=b4, c44g=c44g, pbs=pbs):
                    ste.wait(v)
                    last = None
                    for c4i in range(4):
                        c44 = 4 * c44g + c4i
                        last = v.tensor_copy(
                            out=xq3t[:, TB * c44 + 128 * b4:TB * c44 + 128 * (b4 + 1)],
                            in_=pb[pbs + c4i][:, 0:128])
                    sv.inc(last)
                EV6.append(sv.v)
            XQ3_T[b4] = ste.v
        V_XQ3 = sv.v

        if debug:
            @block.gpsimd
            def _(g):
                sv.wait(g, V_XQ3)
                g.wait_ge(cc.h, CC_RS)
                stm.dma(g.dma_start(out=dbg["gu"][:, :], in_=gu_d[:, :]))
                stm.dma(g.dma_start(out=dbg["aor"][:, :], in_=aor_d[:, :]))


        stp4c = contextlib.ExitStack()
        w3t = [stp4c.enter_context(nc.sbuf_tensor(f"w3t{i}", [128, 1024], BF16)) for i in range(4)]
        fbuf2 = [stp4c.enter_context(nc.sbuf_tensor(f"fbuf{i}", [128, 1024], F32)) for i in range(2)]
        xrb2 = [stp4c.enter_context(nc.sbuf_tensor(f"xrb{i}", [128, 1024], F32)) for i in range(2)]
        aorb2 = [stp4c.enter_context(nc.sbuf_tensor(f"aorb{i}", [128, 1024], F32)) for i in range(2)]
        FIN_VF = {}
        FIN_ST = {}

        SCF_LAST = {}
        for th in range(2):
            for dh in range(2):
                grp = 2 * th + dh
                W3_LD = {}
                for c44 in range(44):
                    @block.sync
                    def _(s2, th=th, dh=dh, c44=c44):
                        if c44 == 0 and th == 0 and dh == 0:
                            sv.wait(s2, V_XQ3)
                        if c44 >= 4:
                            ste.wait(s2, MM3[(th, dh, c44 - 4)])
                        elif (th, dh) != (0, 0):
                            pth, pdh = (th, dh - 1) if dh > 0 else (th - 1, 1)
                            ste.wait(s2, MM3[(pth, pdh, 40 + c44)])
                        W3_LD[c44] = lnt6[c44 % 4].dma(s2.dma_start(
                            out=w3t[c44 % 4][:, :],
                            in_=w3q[128 * c44:128 * (c44 + 1), 1024 * dh:1024 * (dh + 1)]))

                    @block.tensor
                    def _(te, th=th, dh=dh, c44=c44, grp=grp):
                        te.wait_ge(lnt6[c44 % 4].h, W3_LD[c44])
                        if c44 == 0:
                            if grp == 0:
                                sv.wait(te)
                                ssc.wait(te)
                            else:
                                ssc.wait(te, SCF_LAST[grp - 1])
                        last = None
                        for b4 in range(4):
                            blk = 4 * th + b4
                            lx = xq3t[:, TB * c44 + 128 * blk:TB * c44 + 128 * (blk + 1)]
                            st_ = c44 == 0
                            sp = c44 == 43
                            last = te.matmul(pb[2 * b4][:, :], lx, w3t[c44 % 4][:, 0:512],
                                             start=st_, stop=sp)
                            last = te.matmul(pb[2 * b4 + 1][:, :], lx, w3t[c44 % 4][:, 512:1024],
                                             start=st_, stop=sp)
                        MM3[(th, dh, c44)] = ste.inc(last)

                def fin_prefetch(th, dh, b4):
                    blk = 4 * th + b4
                    gidx = 4 * (2 * th + dh) + b4

                    @block.gpsimd
                    def _(g, th=th, dh=dh, b4=b4, blk=blk, gidx=gidx):
                        if gidx == 0:
                            g.wait_ge(cc.h, CC_RS)
                        if gidx >= 2:
                            sv.wait(g, FIN_VF[gidx - 2])
                        ldf[gidx % 2].dma(g.dma_start(out=xrb2[gidx % 2][:, :],
                                                      in_=x_sl[128 * blk:128 * (blk + 1),
                                                               1024 * dh:1024 * (dh + 1)]))
                        FIN_LD[(th, dh, b4)] = ldf[gidx % 2].dma(
                            g.dma_start(out=aorb2[gidx % 2][:, :],
                                        in_=aor_d[128 * blk:128 * (blk + 1),
                                                  1024 * dh:1024 * (dh + 1)]))

                fin_prefetch(th, dh, 0)
                fin_prefetch(th, dh, 1)

                for b4 in range(4):
                    blk = 4 * th + b4
                    gidx = 4 * grp + b4
                    fbuf = fbuf2[gidx % 2]

                    @block.scalar
                    def _(sc, th=th, dh=dh, b4=b4, blk=blk, gidx=gidx, fbuf=fbuf):
                        ste.wait(sc, MM3[(th, dh, 43)])
                        if gidx >= 2:
                            sv.wait(sc, FIN_VF[gidx - 2])
                            sc.wait_ge(stx[gidx % 2].h, FIN_ST[gidx - 2])
                        last = None
                        for ph in range(2):
                            last = sc.mul(out=fbuf[:, 512 * ph:512 * (ph + 1)],
                                          in_=pb[2 * b4 + ph][:, :], mul=os3w[:, blk:blk + 1])
                        ssc.inc(last)
                    SC_F = ssc.v
                    if b4 == 3:
                        SCF_LAST[grp] = SC_F

                    @block.vector
                    def _(v, th=th, dh=dh, b4=b4, gidx=gidx, SC_F=SC_F, fbuf=fbuf):
                        ssc.wait(v, SC_F)
                        v.wait_ge(ldf[gidx % 2].h, FIN_LD[(th, dh, b4)])
                        v.drain()
                        v.tensor_tensor(out=fbuf[:, :], in0=fbuf[:, :],
                                        in1=aorb2[gidx % 2][:, :], op=ALU.add)
                        v.drain()
                        v.tensor_tensor(out=fbuf[:, :], in0=fbuf[:, :],
                                        in1=xrb2[gidx % 2][:, :], op=ALU.add)
                        v.drain()
                        sv.inc(v.tensor_tensor(out=fbuf[:, :], in0=fbuf[:, :],
                                               in1=borep[:, 1024 * dh:1024 * (dh + 1)], op=ALU.add))
                    FIN_VF[gidx] = sv.v

                    @block.gpsimd
                    def _(g, th=th, dh=dh, b4=b4, blk=blk, gidx=gidx, fbuf=fbuf):
                        sv.wait(g, FIN_VF[gidx])
                        FIN_ST[gidx] = stx[gidx % 2].dma(
                            g.dma_start(out=out_d[128 * blk:128 * (blk + 1),
                                                  1024 * dh:1024 * (dh + 1)],
                                        in_=fbuf[:, :]))
                    if b4 < 2:
                        fin_prefetch(th, dh, b4 + 2)

        stp4c.close()
        stp4b.close()
        stp4.close()
        @block.sync
        def _(s2):
            stm.wait(s2)
            cc.wait(s2)
            stx[0].wait(s2)
            stx[1].wait(s2)
    return nc, dbg


def build_p2(env):
    pass


G2N = (np.arange(S) % A) * 4 + (np.arange(S) // A)


def prep_inputs(inp):
    pieces, atiles, strips, stiles = build_attn_plan()
    NT = len(atiles)
    SW = max(t.shape[1] for t in stiles)
    atl_loc = np.zeros((NT * 128, 512), np.float32)
    atl_str = np.zeros((NT * 128, 512), np.float32)
    for i, (lt, stt) in enumerate(atiles):
        atl_loc[128 * i:128 * (i + 1)] = lt
        atl_str[128 * i:128 * (i + 1)] = stt
    strip_arr = np.zeros((len(stiles) * 128, SW), np.float32)
    for i, t in enumerate(stiles):
        strip_arr[128 * i:128 * (i + 1), :t.shape[1]] = t
    bf = ml_dtypes.bfloat16
    x2 = np.asarray(inp["L0_attn_hidden_states"])[0]
    ident = np.eye(128, dtype=np.float32)
    in_maps = []
    for c in range(NCORES):
        sl = G2N[1024 * c:1024 * (c + 1)]
        m = {
            "x_sl": np.ascontiguousarray(x2[sl]),
            "wq_sl": np.ascontiguousarray(inp["L0_attn_Wq"][256 * c:256 * (c + 1)]),
            "wk_sl": np.ascontiguousarray(inp["L0_attn_Wk"][256 * c:256 * (c + 1)]),
            "wv_sl": np.ascontiguousarray(inp["L0_attn_Wv"][256 * c:256 * (c + 1)]),
            "wo_sl": np.ascontiguousarray(np.asarray(inp["L0_attn_Wo"])[:, 256 * c:256 * (c + 1)].T),
            "bq_sl": np.ascontiguousarray(inp["L0_attn_bq"][256 * c:256 * (c + 1)]).reshape(256, 1),
            "bk_sl": np.ascontiguousarray(inp["L0_attn_bk"][256 * c:256 * (c + 1)]).reshape(256, 1),
            "bv_sl": np.ascontiguousarray(inp["L0_attn_bv"][256 * c:256 * (c + 1)]).reshape(1, 256),
            "bo_in": np.ascontiguousarray(inp["L0_attn_bo"]).reshape(1, D),
            "w1_sh": np.ascontiguousarray(np.asarray(inp["L0_ffn_w1"])[:, 256 * c:256 * (c + 1)].T),
            "w2_sh": np.ascontiguousarray(np.asarray(inp["L0_ffn_w2"])[:, 256 * c:256 * (c + 1)].T),
            "w3_sh": np.ascontiguousarray(np.asarray(inp["L0_ffn_w3"])[:, FSH * c:FSH * (c + 1)].T),
            "atl_loc": atl_loc.astype(bf),
            "atl_str": atl_str.astype(bf),
            "strip_in": strip_arr.astype(bf),
            "identb_in": ident.astype(bf),
            "identf_in": ident,
        }
        m = {k: np.ascontiguousarray(v) for k, v in m.items()}
        in_maps.append(m)
    return in_maps



_CACHE = {}


def kernel(**inputs):
    inp = {k: np.asarray(v) for k, v in inputs.items()}
    if "nc" not in _CACHE:
        _CACHE["nc"] = build_nc(run_to="p7", debug=False)[0]
    nc = _CACHE["nc"]
    in_maps = prep_inputs(inp)
    from concourse.bass_utils import run_bass_kernel_spmd
    res = run_bass_kernel_spmd(nc, in_maps, core_ids=list(range(NCORES)))
    outs = np.concatenate([res.results[c]["out"] for c in range(NCORES)], 0)
    out = np.empty((S, D), np.float32)
    out[G2N] = outs
    return out[None]
